# version 13
# baseline (speedup 1.0000x reference)
import numpy as np
import ml_dtypes
import concourse.bass as bass
import concourse.mybir as mybir
from concourse.bass_utils import run_bass_kernel_spmd

F32 = mybir.dt.float32
F16 = mybir.dt.float16
BF16 = mybir.dt.bfloat16
F8 = mybir.dt.float8e4
AX = mybir.AxisListType.X
AF = mybir.ActivationFunctionType
ALU = mybir.AluOpType
BF16NP = ml_dtypes.bfloat16

S, D, NH, HD, FF = 4096, 1024, 16, 64, 4096
TOWN = 2048
NQ = 512

_STREAMS = {
    "tensor": ("tensor", "t", 1),
    "vector": ("vector", "v", 1),
    "scalar": ("scalar", "s", 1),
    "dsync": ("sync", None, 16),
    "dgp": ("gpsimd", None, 16),
    "dscalar": ("scalar", None, 16),
}
_ENGINES = ("sync", "scalar", "vector", "tensor", "gpsimd")


class Buf:
    __slots__ = ("name", "writer", "readers")

    def __init__(self, name):
        self.name = name
        self.writer = None
        self.readers = []


class Sched:
    def __init__(self):
        self.ops = {e: [] for e in _ENGINES}
        self.cnt = {"t": 0, "v": 0, "s": 0}
        self.pend = {e: {} for e in _ENGINES}
        self.already = {e: {} for e in _ENGINES}

    SELF_DIST = 6

    def op(self, stream, fn, reads=(), writes=(), tag=""):
        eng, sem, inc = _STREAMS[stream]
        if inc == 16:
            sem = "D" + writes[0].name if writes else "Dmisc"
            self.cnt.setdefault(sem, 0)
        already = self.already[eng]
        waits = {k: v for k, v in self.pend[eng].items()
                 if already.get(k, -1) < v}
        self.pend[eng] = {}

        def add(dep):
            if dep is None:
                return
            dsem, dval = dep
            if dsem == sem:
                if inc != 1:
                    return
                if sem == "t":
                    return
                if self.cnt[sem] + 1 - dval > self.SELF_DIST:
                    return
            if already.get(dsem, -1) >= dval:
                return
            if waits.get(dsem, -1) < dval:
                waits[dsem] = dval

        for b in reads:
            add(b.writer)
        for b in writes:
            add(b.writer)
            for r in b.readers:
                add(r)
        for ws, wv in waits.items():
            if already.get(ws, -1) < wv:
                already[ws] = wv
        after = self.cnt[sem] + inc
        self.cnt[sem] = after
        self.ops[eng].append((fn, waits, sem, after, tag))
        for b in writes:
            b.writer = (sem, after)
            b.readers = []
        for b in reads:
            b.readers.append((sem, after))

    def barrier(self):
        snap = dict(self.cnt)
        for e in _ENGINES:
            p = self.pend[e]
            a = self.already[e]
            for sn, v in snap.items():
                if a.get(sn, -1) >= v:
                    continue
                if p.get(sn, -1) < v:
                    p[sn] = v

    def finalize_incs(self):
        needed = {}
        for e in _ENGINES:
            for fn, waits, sem, after, tag in self.ops[e]:
                for ws, wv in waits.items():
                    needed.setdefault(ws, set()).add(wv)
        for sn, v in self.cnt.items():
            needed.setdefault(sn, set()).add(v)
        out_ops = {}
        pending = {}
        for e in _ENGINES:
            lst = []
            for fn, waits, sem, after, tag in self.ops[e]:
                if sem not in ("t", "v", "s"):
                    lst.append((fn, waits, sem, 16, tag))
                    continue
                pend = pending.get(sem, 0) + 1
                if after in needed.get(sem, ()):
                    lst.append((fn, waits, sem, pend, tag))
                    pending[sem] = 0
                else:
                    lst.append((fn, waits, sem, 0, tag))
                    pending[sem] = pend
            out_ops[e] = lst
        self.ops = out_ops


def build_program(debug=(), stages=(1, 2, 25, 3)):
    nc = bass.Bass("TRN2", target_bir_lowering=False,
                   detect_race_conditions=False)
    dbg = set(debug)

    def mb(name, n, shape, dt):
        return [nc.alloc_sbuf_tensor(f"{name}{i}", shape, dt) for i in range(n)]

    X = nc.dram_tensor("x", [S, D], F32, kind="ExternalInput").ap()
    WQT = nc.dram_tensor("wqT", [D, D], F32, kind="ExternalInput").ap()
    WKT = nc.dram_tensor("wkT", [D, D], F32, kind="ExternalInput").ap()
    WVT = nc.dram_tensor("wvT", [D, D], F32, kind="ExternalInput").ap()
    WOT = nc.dram_tensor("woT", [D, D], F32, kind="ExternalInput").ap()
    W1T = nc.dram_tensor("w1T", [D, FF], F32, kind="ExternalInput").ap()
    W2T = nc.dram_tensor("w2T", [FF, D], F32, kind="ExternalInput").ap()
    BQ = nc.dram_tensor("bq", [128, 8], F32, kind="ExternalInput").ap()
    BK = nc.dram_tensor("bk", [128, 8], F32, kind="ExternalInput").ap()
    BO = nc.dram_tensor("bo", [128, 8], F32, kind="ExternalInput").ap()
    B1 = nc.dram_tensor("b1", [128, 32], F32, kind="ExternalInput").ap()
    B2 = nc.dram_tensor("b2", [128, 8], F32, kind="ExternalInput").ap()
    BV = nc.dram_tensor("bv", [1, D], F32, kind="ExternalInput").ap()
    ALHI = nc.dram_tensor("alhi", [NH, S], BF16, kind="ExternalInput").ap()
    ALLO = nc.dram_tensor("allo", [NH, S], BF16, kind="ExternalInput").ap()
    MASK = nc.dram_tensor("maskv", [128, 16384], F16, kind="ExternalInput").ap()
    EYE = nc.dram_tensor("eye", [128, 128], F32, kind="ExternalInput").ap()
    OUT = nc.dram_tensor("out", [TOWN, D], F32, kind="ExternalOutput").ap()
    KTS = nc.dram_tensor("kts", [D, S], BF16, kind="Internal").ap()
    QS = nc.dram_tensor("qs", [D, TOWN], BF16, kind="Internal").ap()
    VS = nc.dram_tensor("vs", [S, D], BF16, kind="Internal").ap()

    dbg_out = {}
    for name, shape, dt in (
        ("xn", [S, D], F32), ("ktd", [D, S], F32), ("vd", [S, D], F32),
        ("qtd", [D, TOWN], F32), ("xnqd", [D, TOWN], F32),
        ("oacc", [D, TOWN], F32), ("std", [D, TOWN], F32),
        ("hqd", [FF, 512], F32), ("expd", [S, NQ], F32),
        ("yd", [128, 512], F32), ("gd", [128, 512], F32),
        ("od", [128, 128], F32),
    ):
        if name in dbg:
            dbg_out[name] = nc.dram_tensor("dbg_" + name, shape, dt,
                                           kind="ExternalOutput").ap()

    sch = Sched()
    B = Buf

    eye = nc.alloc_sbuf_tensor("eye_sb", [128, 128], F32)
    ones_r = nc.alloc_sbuf_tensor("ones_r", [128, 128], F32)
    bq_sb = nc.alloc_sbuf_tensor("bq_sb", [128, 8], F32)
    bk_sb = nc.alloc_sbuf_tensor("bk_sb", [128, 8], F32)
    bo_sb = nc.alloc_sbuf_tensor("bo_sb", [128, 8], F32)
    b1_sb = nc.alloc_sbuf_tensor("b1_sb", [128, 32], F32)
    b2_sb = nc.alloc_sbuf_tensor("b2_sb", [128, 8], F32)
    b1_f8 = nc.alloc_sbuf_tensor("b1_f8", [128, 32], F8)
    b2_f8 = nc.alloc_sbuf_tensor("b2_f8", [128, 8], F8)
    b1q = nc.alloc_sbuf_tensor("b1q", [128, 32], F32)
    b2q = nc.alloc_sbuf_tensor("b2q", [128, 8], F32)
    xnq = nc.alloc_sbuf_tensor("xnq", [128, 8, TOWN], F8)
    sT = nc.alloc_sbuf_tensor("sT", [128, 8, TOWN], BF16)

    b_eye, b_ones = B("eye"), B("ones")
    b_bq, b_bk, b_bo = B("bq"), B("bk"), B("bo")
    b_b1s, b_b2s, b_b18, b_b28, b_b1q, b_b2q = (B(x) for x in
        ("b1s", "b2s", "b18", "b28", "b1q", "b2q"))
    b_bv, b_bvrep = B("bv"), B("bvrep")
    b_xnq, b_sT = B("xnq"), B("sT")
    b_wst = [B("wst0"), B("wst1")]
    b_kts, b_vs = B("kts"), B("vs")

    if "nosetup" in stages:
        pass
    else:
        sch.op("dsync", lambda e: e.dma_start(out=eye[:, :], in_=EYE), writes=[b_eye])
        sch.op("dsync", lambda e: e.dma_start(out=bq_sb[:, :], in_=BQ), writes=[b_bq])
        sch.op("dsync", lambda e: e.dma_start(out=bk_sb[:, :], in_=BK), writes=[b_bk])
        sch.op("dsync", lambda e: e.dma_start(out=bo_sb[:, :], in_=BO), writes=[b_bo])
        sch.op("dsync", lambda e: e.dma_start(out=b1_sb[:, :], in_=B1), writes=[b_b1s])
        sch.op("dsync", lambda e: e.dma_start(out=b2_sb[:, :], in_=B2), writes=[b_b2s])
        sch.op("dsync", lambda e: e.dma_start(out=bv_sb[:, :], in_=BV), writes=[b_bv])
        sch.op("vector", lambda e: e.memset(ones_r[:, :], 1.0), writes=[b_ones])
        sch.op("vector", lambda e: e.tensor_copy(b1_f8[:, :], b1_sb[:, :]),
               reads=[b_b1s], writes=[b_b18])
        sch.op("vector", lambda e: e.tensor_copy(b1q[:, :], b1_f8[:, :]),
               reads=[b_b18], writes=[b_b1q])
        sch.op("vector", lambda e: e.tensor_copy(b2_f8[:, :], b2_sb[:, :]),
               reads=[b_b2s], writes=[b_b28])
        sch.op("vector", lambda e: e.tensor_copy(b2q[:, :], b2_f8[:, :]),
               reads=[b_b28], writes=[b_b2q])

    sb_base1 = nc.sbuf_base
    ps_base1 = nc.psum_base
    bv_sb = nc.alloc_sbuf_tensor("bv_sb", [1, D], F32)
    bvrep = nc.alloc_sbuf_tensor("bvrep", [128, D], F32)
    wstage = mb("wstage", 2, [128, D], F32)
    xin = mb("xin", 2, [128, D], F32)
    sq_t = mb("sq_t", 2, [128, D], F32)
    st = mb("st", 2, [128, 16], F32)
    xn_tm = mb("xn_tm", 2, [128, D], F32)
    xnT = mb("xnT", 2, [128, 8, 512], BF16)
    wq_bf = nc.alloc_sbuf_tensor("wq_bf", [128, 8, D], BF16)
    wk_bf = nc.alloc_sbuf_tensor("wk_bf", [128, 8, D], BF16)
    wv_bf = nc.alloc_sbuf_tensor("wv_bf", [128, 8, D], BF16)
    kt_sp = mb("kt_sp", 2, [128, 8, 512], BF16)
    q_sp = mb("q_sp", 2, [128, 8, 512], BF16)
    v_sp = mb("v_sp", 2, [128, D], BF16)

    trA = nc.alloc_psum_tensor("trA", [128, 512], F32)
    trB = nc.alloc_psum_tensor("trB", [128, 512], F32)
    pjK = [nc.alloc_psum_tensor(f"pjK{i}", [128, 512], F32) for i in range(2)]
    pjQ = [nc.alloc_psum_tensor(f"pjQ{i}", [128, 512], F32) for i in range(2)]
    pjV = [nc.alloc_psum_tensor(f"pjV{i}", [128, 512], F32) for i in range(2)]

    b_xin = [B("xin0"), B("xin1")]
    b_sq = [B("sq0"), B("sq1")]
    b_st = [B("st0"), B("st1")]
    b_xntm = [B("xntm0"), B("xntm1")]
    b_xnT = [B("xnT0"), B("xnT1")]
    b_wq, b_wk, b_wv = B("wq"), B("wk"), B("wv")
    b_ktsp = [B("ktsp0"), B("ktsp1")]
    b_qsp = [B("qsp0"), B("qsp1")]
    b_vsp = [B("vsp0"), B("vsp1")]
    b_qs = B("qs")
    b_trA, b_trB = B("trA"), B("trB")
    b_pjK = [B("pjK0"), B("pjK1")]
    b_pjQ = [B("pjQ0"), B("pjQ1")]
    b_pjV = [B("pjV0"), B("pjV1")]

    for half in (() if "nobvrep" in stages else range(2)):
        sch.op("tensor", lambda e, h=half: e.matmul(
            pjV[h][:, :], ones_r[0:1, 0:128], bv_sb[0:1, h * 512:(h + 1) * 512],
            start=True, stop=True),
            reads=[b_ones, b_bv], writes=[b_pjV[half]])
        sch.op("vector", lambda e, h=half: e.tensor_copy(
            bvrep[:, h * 512:(h + 1) * 512], pjV[h][:, :]),
            reads=[b_pjV[half]], writes=[b_bvrep])

    for wdst, bw, src in ((wk_bf, b_wk, WKT), (wv_bf, b_wv, WVT), (wq_bf, b_wq, WQT)):
        for ci in range(8):
            p = ci % 2
            sch.op("dscalar", lambda e, ci=ci, p=p, src=src: e.dma_start(
                out=wstage[p][:, :], in_=src[ci * 128:(ci + 1) * 128, :]),
                writes=[b_wst[p]])
            sch.op("vector", lambda e, ci=ci, p=p, wdst=wdst: e.tensor_copy(
                wdst[:, ci, :], wstage[p][:, :]),
                reads=[b_wst[p]], writes=[bw])

    INV_D = 1.0 / D
    for sc in (range(8) if 1 in stages else ()):
        own = sc < 4
        sp = sc % 2
        def ln_steps(tb):
            t0 = sc * 512 + tb * 128
            p = tb % 2
            bx, bs, bst, bxn = b_xin[p], b_sq[p], b_st[p], b_xntm[p]
            steps = []

            def st_(stream, fn, reads=(), writes=()):
                steps.append((stream, fn, reads, writes))

            st_("dsync", lambda e, t0=t0, p=p: e.dma_start(
                out=xin[p][:, :], in_=X[t0:t0 + 128, :]), (), [bx])
            st_("vector", lambda e, p=p: e.reduce_sum(
                st[p][:, 0:1], xin[p][:, :], axis=AX), [bx], [bst])
            st_("vector", lambda e, p=p: e.tensor_mul(
                sq_t[p][:, :], xin[p][:, :], xin[p][:, :]), [bx], [bs])
            st_("vector", lambda e, p=p: e.reduce_sum(
                st[p][:, 1:2], sq_t[p][:, :], axis=AX), [bs], [bst])
            st_("vector", lambda e, p=p: e.tensor_scalar(
                st[p][:, 2:3], st[p][:, 0:1], INV_D, None, ALU.mult),
                [bst], [bst])
            st_("vector", lambda e, p=p: e.tensor_mul(
                st[p][:, 3:4], st[p][:, 2:3], st[p][:, 2:3]), [bst], [bst])
            st_("vector", lambda e, p=p: e.tensor_scalar(
                st[p][:, 4:5], st[p][:, 1:2], INV_D, 1e-5, ALU.mult, ALU.add),
                [bst], [bst])
            st_("vector", lambda e, p=p: e.tensor_sub(
                st[p][:, 5:6], st[p][:, 4:5], st[p][:, 3:4]), [bst], [bst])
            st_("scalar", lambda e, p=p: e.activation(
                st[p][:, 6:7], st[p][:, 5:6], AF.Sqrt), [bst], [bst])
            st_("vector", lambda e, p=p: e.reciprocal(
                st[p][:, 7:8], st[p][:, 6:7]), [bst], [bst])
            st_("vector", lambda e, p=p: e.tensor_mul(
                st[p][:, 8:9], st[p][:, 7:8], st[p][:, 7:8]), [bst], [bst])
            st_("vector", lambda e, p=p: e.tensor_mul(
                st[p][:, 9:10], st[p][:, 8:9], st[p][:, 5:6]), [bst], [bst])
            st_("vector", lambda e, p=p: e.tensor_scalar(
                st[p][:, 10:11], st[p][:, 9:10], -0.5, 1.5, ALU.mult, ALU.add),
                [bst], [bst])
            st_("vector", lambda e, p=p: e.tensor_mul(
                st[p][:, 11:12], st[p][:, 7:8], st[p][:, 10:11]), [bst], [bst])
            st_("vector", lambda e, p=p: e.tensor_mul(
                st[p][:, 12:13], st[p][:, 2:3], st[p][:, 11:12]), [bst], [bst])
            st_("vector", lambda e, p=p: e.tensor_scalar(
                st[p][:, 13:14], st[p][:, 12:13], -1.0, None, ALU.mult),
                [bst], [bst])
            st_("scalar", lambda e, p=p: e.activation(
                xn_tm[p][:, :], xin[p][:, :], AF.Identity,
                bias=st[p][:, 13:14], scale=st[p][:, 11:12]), [bx, bst], [bxn])
            return steps

        def emit_tr_evac(tb):
            t0 = sc * 512 + tb * 128
            p = tb % 2
            bxn = b_xntm[p]
            if "xn" in dbg:
                sch.op("dgp", lambda e, t0=t0, p=p: e.dma_start(
                    out=dbg_out["xn"][t0:t0 + 128, :], in_=xn_tm[p][:, :]),
                    reads=[bxn])
            for ci in range(8):
                tr, btr = (trA, b_trA) if ci < 4 else (trB, b_trB)
                sch.op("tensor", lambda e, p=p, ci=ci, tr=tr: e.transpose(
                    tr[:, (ci % 4) * 128:(ci % 4) * 128 + 128],
                    xn_tm[p][:, ci * 128:(ci + 1) * 128], eye[:, :]),
                    reads=[bxn, b_eye], writes=[btr])
            for gi, (tr, btr) in enumerate(((trA, b_trA), (trB, b_trB))):
                src3 = tr[:, :].rearrange("p (c j) -> p c j", j=128)
                sch.op("scalar", lambda e, sp=sp, gi=gi, tb=tb, src3=src3:
                       e.copy(xnT[sp][:, gi * 4:gi * 4 + 4,
                                      tb * 128:tb * 128 + 128], src3),
                       reads=[btr], writes=[b_xnT[sp]])
                if own:
                    sch.op("scalar", lambda e, sc=sc, gi=gi, tb=tb, src3=src3:
                           e.copy(
                               xnq[:, gi * 4:gi * 4 + 4,
                                   sc * 512 + tb * 128:sc * 512 + tb * 128 + 128],
                               src3),
                           reads=[btr], writes=[b_xnq])

        for pair in range(2):
            sa = ln_steps(pair * 2)
            sb = ln_steps(pair * 2 + 1)
            for i in range(max(len(sa), len(sb))):
                if i < len(sa):
                    stream, fn, rr, ww = sa[i]
                    sch.op(stream, fn, reads=rr, writes=ww)
                if i < len(sb):
                    stream, fn, rr, ww = sb[i]
                    sch.op(stream, fn, reads=rr, writes=ww)
            emit_tr_evac(pair * 2)
            emit_tr_evac(pair * 2 + 1)
        for co in range(8):
            pj, bpj = pjK[co % 2], b_pjK[co % 2]
            for ci in range(8):
                sch.op("tensor", lambda e, sp=sp, ci=ci, co=co, pj=pj: e.matmul(
                    pj[:, :], wk_bf[:, ci, co * 128:co * 128 + 128],
                    xnT[sp][:, ci, :], start=(ci == 0), stop=(ci == 7)),
                    reads=[b_wk, b_xnT[sp]], writes=[bpj])
            sch.op("scalar", lambda e, sp=sp, co=co, pj=pj: e.activation(
                kt_sp[sp][:, co, :], pj[:, :], AF.Identity,
                bias=bk_sb[:, co:co + 1], scale=1.0),
                reads=[bpj, b_bk], writes=[b_ktsp[sp]])
        for co in range(8):
            sch.op("dgp", lambda e, sp=sp, sc=sc, co=co: e.dma_start(
                out=KTS[co * 128:(co + 1) * 128, sc * 512:(sc + 1) * 512],
                in_=kt_sp[sp][:, co, :]),
                reads=[b_ktsp[sp]], writes=[b_kts])
        if own:
            for co in range(8):
                pj, bpj = pjQ[co % 2], b_pjQ[co % 2]
                for ci in range(8):
                    sch.op("tensor", lambda e, sp=sp, ci=ci, co=co, pj=pj: e.matmul(
                        pj[:, :], wq_bf[:, ci, co * 128:co * 128 + 128],
                        xnT[sp][:, ci, :], start=(ci == 0), stop=(ci == 7)),
                        reads=[b_wq, b_xnT[sp]], writes=[bpj])
                sch.op("scalar", lambda e, sp=sp, co=co, pj=pj, bpj=bpj: e.activation(
                    q_sp[sp][:, co, :], pj[:, :], AF.Identity,
                    bias=bq_sb[:, co:co + 1], scale=1.0),
                    reads=[bpj, b_bq], writes=[b_qsp[sp]])
            for co in range(8):
                sch.op("dgp", lambda e, sp=sp, sc=sc, co=co: e.dma_start(
                    out=QS[co * 128:(co + 1) * 128, sc * 512:(sc + 1) * 512],
                    in_=q_sp[sp][:, co, :]),
                    reads=[b_qsp[sp]], writes=[b_qs])
        for tb in range(4):
            vp = tb % 2
            for eh in range(2):
                pj, bpj = pjV[eh], b_pjV[eh]
                for ci in range(8):
                    sch.op("tensor", lambda e, sp=sp, tb=tb, eh=eh, ci=ci, pj=pj:
                           e.matmul(pj[:, :],
                                    xnT[sp][:, ci, tb * 128:tb * 128 + 128],
                                    wv_bf[:, ci, eh * 512:eh * 512 + 512],
                                    start=(ci == 0), stop=(ci == 7)),
                           reads=[b_wv, b_xnT[sp]], writes=[bpj])
                sch.op("vector", lambda e, vp=vp, eh=eh, pj=pj, bpj=bpj:
                       e.tensor_add(v_sp[vp][:, eh * 512:eh * 512 + 512],
                                    pj[:, :], bvrep[:, eh * 512:eh * 512 + 512]),
                       reads=[bpj, b_bvrep], writes=[b_vsp[vp]])
            sch.op("dgp", lambda e, sc=sc, tb=tb, vp=vp: e.dma_start(
                out=VS[sc * 512 + tb * 128:sc * 512 + tb * 128 + 128, :],
                in_=v_sp[vp][:, :]),
                reads=[b_vsp[vp]], writes=[b_vs])

    if "xnqd" in dbg:
        for co in range(8):
            for tq in range(4):
                sch.op("vector", lambda e, co=co, tq=tq: e.tensor_copy(
                    sq_t[0][:, 0:512],
                    xnq[:, co, tq * 512:(tq + 1) * 512]),
                    reads=[b_xnq], writes=[b_sq[0]])
                sch.op("dgp", lambda e, co=co, tq=tq: e.dma_start(
                    out=dbg_out["xnqd"][co * 128:(co + 1) * 128, tq * 512:(tq + 1) * 512],
                    in_=sq_t[0][:, 0:512]),
                    reads=[b_sq[0]])

    sch.barrier()
    nc.sbuf_base = sb_base1
    nc.psum_base = ps_base1

    o_acc = nc.alloc_sbuf_tensor("o_acc", [128, 8, TOWN], BF16)
    b_oacc = B("oacc")
    sb_base2i = nc.sbuf_base
    ps_base2i = nc.psum_base
    kt_buf = mb("kt_buf", 2, [128, S], BF16)
    v_buf = mb("v_buf", 2, [128, 4224], BF16)
    q_st = mb("q_st", 2, [128, TOWN], BF16)
    expS = mb("expS", 2, [128, 32, NQ], BF16)
    rcp_t = nc.alloc_sbuf_tensor("rcp_t", [128, NQ], F32)
    bc_sb = nc.alloc_sbuf_tensor("bc_sb", [128, NQ], F32)

    sc_ps = [nc.alloc_psum_tensor(f"sc{i}", [128, 1024], F32) for i in range(2)]
    pv_ps = [nc.alloc_psum_tensor(f"pv{i}", [128, NQ], F32) for i in range(2)]
    bc_ps = nc.alloc_psum_tensor("bc", [128, NQ], F32)

    b_kt = [B(f"kt{i}") for i in range(2)]
    b_vb = [B(f"vb{i}") for i in range(2)]
    b_qst = [B("qst0"), B("qst1")]
    b_expg = [[B(f"expg{e}_{g}") for g in range(16)] for e in range(2)]
    b_rcp = B("rcp")
    b_bcs = B("bcs")
    b_sc = [B(f"sc{i}") for i in range(2)]
    b_pv = [B("pv0"), B("pv1")]
    b_bc = B("bc")

    for qb_ in range(2):
        sch.op("vector", lambda e, qb_=qb_: e.memset(q_st[qb_][64:128, :], 0.0),
               writes=[b_qst[qb_]])
        sch.op("vector", lambda e, qb_=qb_: e.memset(q_st[qb_][64:66, :], 1.0),
               writes=[b_qst[qb_]])
    for vb in range(2):
        sch.op("vector", lambda e, vb=vb: e.memset(kt_buf[vb][64:128, :], 0.0),
               writes=[b_kt[vb]])
        sch.op("vector", lambda e, vb=vb: e.memset(v_buf[vb][:, 0:4160:130], 1.0),
               writes=[b_vb[vb]])
        sch.op("vector", lambda e, vb=vb: e.memset(
            v_buf[vb][:, 128:4160:130], 1.0),
            writes=[b_vb[vb]])

    def emit_head_dmas(hh):
        vb = hh % 2
        sch.op("dscalar", lambda e, hh=hh, vb=vb: e.dma_start(
            out=q_st[vb][0:64, :], in_=QS[hh * 64:(hh + 1) * 64, :]),
            reads=[b_qs], writes=[b_qst[vb]])
        sch.op("dsync", lambda e, hh=hh, vb=vb: e.dma_start(
            out=kt_buf[vb][0:64, :], in_=KTS[hh * 64:(hh + 1) * 64, :]),
            reads=[b_kts], writes=[b_kt[vb]])
        sch.op("dsync", lambda e, hh=hh, vb=vb: e.dma_start(
            out=kt_buf[vb][64:65, :], in_=ALHI[hh:hh + 1, :]), writes=[b_kt[vb]])
        sch.op("dsync", lambda e, hh=hh, vb=vb: e.dma_start(
            out=kt_buf[vb][65:66, :], in_=ALLO[hh:hh + 1, :]), writes=[b_kt[vb]])
        for kq in range(4):
            sch.op("dgp" if kq % 2 else "dsync",
                   lambda e, hh=hh, vb=vb, kq=kq: e.dma_start(
                       out=v_buf[vb][:, 64 + 1040 * kq:64 + 1040 * (kq + 1)]
                       .rearrange("p (k j) -> p k j", j=130)[:, :, 0:64],
                       in_=VS[kq * 1024:(kq + 1) * 1024, hh * 64:(hh + 1) * 64]
                       .rearrange("(kc p) j -> p kc j", p=128)),
                   reads=[b_vs], writes=[b_vb[vb]])

    def emit_pv_group(pend, kc0, n):
        hh, qs, pv, bpv, se = pend
        vb = hh % 2
        odd = hh % 2 == 1
        for kc in range(kc0, kc0 + n):
            if odd:
                lhs = v_buf[vb][:, 130 * kc:130 * kc + 128]
                o = pv[0:128, :]
            else:
                lhs = v_buf[vb][:, 130 * kc + 64:130 * kc + 129]
                o = pv[0:65, :]
            sch.op("tensor", lambda e, lhs=lhs, o=o, kc=kc, se=se: e.matmul(
                o, lhs, expS[se][:, kc, :], start=(kc == 0), stop=(kc == 31)),
                reads=[b_vb[vb], b_expg[se][kc // 2]], writes=[bpv])

    def emit_norm(pend):
        hh, qs, pv, bpv, se = pend
        odd = hh % 2 == 1
        dr = 0 if odd else 64
        orow = 64 if odd else 0
        sch.op("vector", lambda e, dr=dr, pv=pv: e.tensor_copy(
            rcp_t[dr:dr + 1, :], pv[dr:dr + 1, :]),
            reads=[bpv], writes=[b_rcp])
        sch.op("tensor", lambda e, dr=dr: e.matmul(
            bc_ps[:, :], ones_r[dr:dr + 1, 0:128], rcp_t[dr:dr + 1, :],
            start=True, stop=True),
            reads=[b_ones, b_rcp], writes=[b_bc])
        sch.op("vector", lambda e, orow=orow: e.reciprocal(
            bc_sb[orow:orow + 64, :], bc_ps[orow:orow + 64, :]),
            reads=[b_bc], writes=[b_bcs])
        sch.op("vector", lambda e, hh=hh, qs=qs, orow=orow, pv=pv: e.tensor_mul(
            o_acc[orow:orow + 64, hh // 2, qs:qs + NQ],
            pv[orow:orow + 64, :], bc_sb[orow:orow + 64, :]),
            reads=[bpv, b_bcs], writes=[b_oacc])

    pending = None
    step2 = 0
    for hh in (range(NH) if 2 in stages else ()):
        vb = hh % 2
        if hh == 0:
            emit_head_dmas(0)
        for qtile in range(TOWN // NQ):
            qs = qtile * NQ
            pv, bpv = pv_ps[step2 % 2], b_pv[step2 % 2]
            se = step2 % 2
            step2 += 1
            for g in range(16):
                sp2, bsp = sc_ps[g % 2], b_sc[g % 2]
                for j in range(2):
                    kc = 2 * g + j
                    sch.op("tensor", lambda e, vb=vb, kc=kc, j=j, qs=qs, sp2=sp2:
                           e.matmul(sp2[:, j * 512:j * 512 + 512],
                                    kt_buf[vb][0:128, kc * 128:kc * 128 + 128],
                                    q_st[vb][0:128, qs:qs + NQ],
                                    start=True, stop=True),
                           reads=[b_kt[vb], b_qst[vb]], writes=[bsp])
                if pending is not None:
                    emit_pv_group(pending, kc0=2 * g, n=2)
                sch.op("scalar", lambda e, g=g, sp2=sp2, se=se: e.activation(
                    expS[se][:, 2 * g:2 * g + 2, :], sp2[:, :], AF.Exp,
                    bias=0.0, scale=0.125),
                    reads=[bsp], writes=[b_expg[se][g]])
            if pending is not None:
                emit_norm(pending)
            pending = (hh, qs, pv, bpv, se)
            if qtile == 0 and hh + 1 < NH:
                emit_head_dmas(hh + 1)
    if pending is not None:
        emit_pv_group(pending, kc0=0, n=32)
        emit_norm(pending)

    if "oacc" in dbg:
        for co in range(8):
            for tq in range(4):
                sch.op("vector", lambda e, co=co, tq=tq: e.tensor_copy(
                    rcp_t[:, :], o_acc[:, co, tq * 512:(tq + 1) * 512]),
                    reads=[b_oacc], writes=[b_rcp])
                sch.op("dgp", lambda e, co=co, tq=tq: e.dma_start(
                    out=dbg_out["oacc"][co * 128:(co + 1) * 128,
                                        tq * 512:(tq + 1) * 512],
                    in_=rcp_t[:, :]), reads=[b_rcp])

    sch.barrier()
    nc.sbuf_base = sb_base2i
    nc.psum_base = ps_base2i
    wo_bf = nc.alloc_sbuf_tensor("wo_bf", [128, 8, D], BF16)
    wstage25 = mb("wstage25", 2, [128, D], F32)
    b_wst25 = [B("wst25a"), B("wst25b")]
    b_wo = B("wo")
    pr_ps = [nc.alloc_psum_tensor(f"pr{i}", [128, 512], F32) for i in range(2)]
    b_pr = [B("pr0"), B("pr1")]
    for ci in range(8):
        p = ci % 2
        sch.op("dscalar", lambda e, ci=ci: e.dma_start(
            out=wstage25[ci % 2][:, :], in_=WOT[ci * 128:(ci + 1) * 128, :]),
            writes=[b_wst25[p]])
        sch.op("vector", lambda e, ci=ci, p=p: e.tensor_copy(
            wo_bf[:, ci, :], wstage25[p][:, :]), reads=[b_wst25[p]], writes=[b_wo])
    for co in (range(8) if 25 in stages else ()):
        for ts in range(4):
            pj, bpj = pr_ps[ts % 2], b_pr[ts % 2]
            for ci in range(8):
                sch.op("tensor", lambda e, ci=ci, co=co, ts=ts, pj=pj: e.matmul(
                    pj[:, :], wo_bf[:, ci, co * 128:co * 128 + 128],
                    o_acc[:, ci, ts * 512:(ts + 1) * 512],
                    start=(ci == 0), stop=(ci == 7)),
                    reads=[b_wo, b_oacc], writes=[bpj])
            sch.op("vector", lambda e, co=co, ts=ts, pj=pj, bpj=bpj: e.tensor_scalar(
                sT[:, co, ts * 512:(ts + 1) * 512], pj[:, :],
                bo_sb[:, co:co + 1], None, ALU.add),
                reads=[bpj, b_bo], writes=[b_sT])

    if "std" in dbg:
        dtmp = nc.alloc_sbuf_tensor("dtmp", [128, 512], F32)
        b_dt = B("dtmp")
        for co in range(8):
            for tq in range(4):
                sch.op("vector", lambda e, co=co, tq=tq: e.tensor_copy(
                    dtmp[:, :], sT[:, co, tq * 512:(tq + 1) * 512]),
                    reads=[b_sT], writes=[b_dt])
                sch.op("dgp", lambda e, co=co, tq=tq: e.dma_start(
                    out=dbg_out["std"][co * 128:(co + 1) * 128,
                                       tq * 512:(tq + 1) * 512],
                    in_=dtmp[:, :]), reads=[b_dt])

    sch.barrier()
    nc.sbuf_base = sb_base1
    nc.psum_base = ps_base1

    hq = nc.alloc_sbuf_tensor("hq", [128, 32, 1024], F8)
    W1PRE = 2
    out_st = nc.alloc_sbuf_tensor("out_st", [128, 8, D], F32)
    xin3 = nc.alloc_sbuf_tensor("xin3", [128, 8, D], F32)
    w1st = mb("w1st", 2, [128, 8, 128], F32)
    w1q = mb("w1q", 2, [128, 8, 128], F8)
    w2st = mb("w2st", 4, [128, 8, 128], F32)
    w2q = mb("w2q", 8, [128, 8, 128], F8)
    h16 = mb("h16", 4, [128, 512], F16)
    y16 = mb("y16", 4, [128, 512], F16)
    dp_t = mb("dp_t", 4, [128, 512], F32)
    sg_t = mb("sg_t", 4, [128, 512], F32)
    mk_t = mb("mk_t", 2, [128, 1024], F16)

    f1_ps = [nc.alloc_psum_tensor(f"f1{i}", [128, 512], F32) for i in range(3)]
    f2_ps = [nc.alloc_psum_tensor(f"f2{i}", [128, 512], F32) for i in range(2)]
    tr3_ps = [nc.alloc_psum_tensor(f"tr3{i}", [128, 512], F32) for i in range(2)]

    b_hq = B("hq")
    b_outst = B("outst")
    b_x3 = B("x3")
    b_w1s = [B("w1s0"), B("w1s1")]
    b_w1q = [B("w1q0"), B("w1q1")]
    b_w2s = [B(f"w2s{i}") for i in range(4)]
    b_w2q = [B(f"w2q{i}") for i in range(8)]
    b_h16 = [B(f"h16{i}") for i in range(4)]
    b_y16 = [B(f"y16{i}") for i in range(4)]
    b_dp = [B(f"dp{i}") for i in range(4)]
    b_sg = [B(f"sg{i}") for i in range(4)]
    b_mk = [B("mk0"), B("mk1")]
    b_f1 = [B(f"f1{i}") for i in range(3)]
    b_f2 = [B("f20"), B("f21")]
    b_tr3 = [B("tr30"), B("tr31")]

    if 3 in stages:
        for fo in range(W1PRE):
            p = fo % 2
            sch.op("dsync", lambda e, fo=fo, p=p: e.dma_start(
                out=w1st[p][:, :, :],
                in_=W1T[:, fo * 128:(fo + 1) * 128]
                .rearrange("(c p) j -> p c j", p=128)),
                writes=[b_w1s[p]])
            sch.op("vector", lambda e, p=p: e.tensor_copy(
                w1q[p][:, :, :], w1st[p][:, :, :]),
                reads=[b_w1s[p]], writes=[b_w1q[p]])

    for H in (range(2) if 3 in stages else ()):
        hb = H * 1024
        for tb in (() if "nox3" in stages else range(8)):
            sch.op("dscalar", lambda e, hb=hb, tb=tb: e.dma_start(
                out=xin3[:, tb, :], in_=X[hb + tb * 128:hb + tb * 128 + 128, :]),
                writes=[b_x3])
        for fo in range(32):
            p = fo % 2
            if H == 0 and fo < W1PRE:
                pass
            else:
                if "now1dma" not in stages:
                    sch.op("dsync", lambda e, fo=fo, p=p: e.dma_start(
                        out=w1st[p][:, :, :],
                        in_=W1T[:, fo * 128:(fo + 1) * 128]
                        .rearrange("(c p) j -> p c j", p=128)),
                        writes=[b_w1s[p]])
                if "now1cast" not in stages:
                    sch.op("vector", lambda e, p=p: e.tensor_copy(
                        w1q[p][:, :, :], w1st[p][:, :, :]),
                        reads=[b_w1s[p]], writes=[b_w1q[p]])
            for ts2 in range(2):
                gidx = fo * 2 + ts2
                pj, bpj = f1_ps[gidx % 3], b_f1[gidx % 3]
                hb16, bh16 = h16[gidx % 4], b_h16[gidx % 4]
                for ci in range(8):
                    sch.op("tensor", lambda e, p=p, ci=ci, H=H, ts2=ts2, pj=pj:
                           e.matmul(pj[:, :], w1q[p][:, ci, :],
                                    xnq[:, ci, H * 1024 + ts2 * 512:
                                        H * 1024 + ts2 * 512 + 512],
                                    start=(ci == 0), stop=(ci == 7)),
                           reads=[b_w1q[p], b_xnq], writes=[bpj])
                if "noevac" not in stages:
                    sch.op("vector", lambda e, fo=fo, hb16=hb16, pj=pj, bpj=bpj:
                           e.tensor_scalar(hb16[:, :], pj[:, :],
                                           b1q[:, fo:fo + 1], 0.0,
                                           ALU.add, ALU.max),
                           reads=[bpj, b_b1q], writes=[bh16])
                    sch.op("vector", lambda e, fo=fo, ts2=ts2, hb16=hb16, bh16=bh16:
                           e.tensor_copy(
                               hq[:, fo, ts2 * 512:ts2 * 512 + 512], hb16[:, :]),
                           reads=[bh16], writes=[b_hq])
        if "hqd" in dbg and H == 0:
            for fo in range(32):
                sch.op("vector", lambda e, fo=fo: e.tensor_copy(
                    dp_t[0][:, :], hq[:, fo, 0:512]), reads=[b_hq],
                    writes=[b_dp[0]])
                sch.op("dgp", lambda e, fo=fo: e.dma_start(
                    out=dbg_out["hqd"][fo * 128:(fo + 1) * 128, :],
                    in_=dp_t[0][:, :]), reads=[b_dp[0]])
        def emit_tr_adds(p3):
            pH, pdo, pts2, slot = p3
            for qb in range(4):
                sch.op("tensor", lambda e, pts2=pts2, qb=qb, slot=slot: e.transpose(
                    tr3_ps[pts2][:, qb * 128:qb * 128 + 128],
                    sg_t[slot][:, qb * 128:qb * 128 + 128], eye[:, :]),
                    reads=[b_sg[slot], b_eye], writes=[b_tr3[pts2]])
            for qb in range(4):
                tb = pts2 * 4 + qb
                sch.op("vector", lambda e, pdo=pdo, pts2=pts2, qb=qb, tb=tb:
                       e.tensor_add(out_st[:, tb, pdo * 128:pdo * 128 + 128],
                                    tr3_ps[pts2][:, qb * 128:qb * 128 + 128],
                                    xin3[:, tb, pdo * 128:pdo * 128 + 128]),
                       reads=[b_tr3[pts2], b_x3], writes=[b_outst])

        pend3 = None
        for do in (() if "noffn2" in stages else range(8)):
            for fh in range(4):
                p = (do * 4 + fh) % 4
                q4 = (do * 4 + fh) % 8
                sch.op("dgp", lambda e, do=do, fh=fh, p=p: e.dma_start(
                    out=w2st[p][:, :, :],
                    in_=W2T[fh * 1024:(fh + 1) * 1024, do * 128:(do + 1) * 128]
                    .rearrange("(c p) j -> p c j", p=128)),
                    writes=[b_w2s[p]])
                sch.op("vector", lambda e, p=p, q4=q4: e.tensor_copy(
                    w2q[q4][:, :, :], w2st[p][:, :, :]),
                    reads=[b_w2s[p]], writes=[b_w2q[q4]])
            sch.op("dsync", lambda e, H=H, do=do: e.dma_start(
                out=mk_t[do % 2][:, :],
                in_=MASK[:, H * 8192 + do * 1024:H * 8192 + (do + 1) * 1024]),
                writes=[b_mk[do % 2]])
            for ts2 in range(2):
                if pend3 is not None:
                    emit_tr_adds(pend3)
                slot = (do * 2 + ts2) % 4
                pj, bpj = f2_ps[ts2], b_f2[ts2]
                for fo in range(32):
                    q4 = (do * 4 + fo // 8) % 8
                    sch.op("tensor", lambda e, fo=fo, ts2=ts2, q4=q4, pj=pj:
                           e.matmul(pj[:, :], w2q[q4][:, fo % 8, :],
                                    hq[:, fo, ts2 * 512:ts2 * 512 + 512],
                                    start=(fo == 0), stop=(fo == 31)),
                           reads=[b_w2q[q4], b_hq], writes=[bpj])
                sch.op("vector", lambda e, do=do, slot=slot, pj=pj, bpj=bpj:
                       e.tensor_scalar(y16[slot][:, :], pj[:, :],
                                       b2q[:, do:do + 1], None, ALU.add),
                       reads=[bpj, b_b2q], writes=[b_y16[slot]])
                if "yd" in dbg and H == 0 and do == 0 and ts2 == 0:
                    sch.op("vector", lambda e, slot=slot: e.tensor_copy(
                        sg_t[3][:, :], y16[slot][:, :]),
                        reads=[b_y16[slot]], writes=[b_sg[3]])
                    sch.op("dgp", lambda e: e.dma_start(
                        out=dbg_out["yd"], in_=sg_t[3][:, :]), reads=[b_sg[3]])
                sch.op("vector", lambda e, do=do, ts2=ts2, slot=slot: e.tensor_mul(
                    dp_t[slot][:, :], y16[slot][:, :],
                    mk_t[do % 2][:, ts2 * 512:ts2 * 512 + 512]),
                    reads=[b_y16[slot], b_mk[do % 2]], writes=[b_dp[slot]])
                sch.op("scalar", lambda e, slot=slot: e.activation(
                    dp_t[slot][:, :], dp_t[slot][:, :], AF.Silu),
                    reads=[b_dp[slot]], writes=[b_dp[slot]])
                sch.op("vector", lambda e, do=do, H=H, ts2=ts2, slot=slot: e.tensor_mul(
                    sg_t[slot][:, :], dp_t[slot][:, :],
                    sT[:, do, H * 1024 + ts2 * 512:H * 1024 + ts2 * 512 + 512]),
                    reads=[b_dp[slot], b_sT], writes=[b_sg[slot]])
                if "gd" in dbg and H == 0 and do == 0 and ts2 == 0:
                    sch.op("dgp", lambda e, slot=slot: e.dma_start(
                        out=dbg_out["gd"], in_=sg_t[slot][:, :]), reads=[b_sg[slot]])
                pend3 = (H, do, ts2, slot)
        if pend3 is not None:
            emit_tr_adds(pend3)
            pend3 = None
        if "od" in dbg and H == 0:
            sch.op("dgp", lambda e: e.dma_start(
                out=dbg_out["od"], in_=out_st[:, 0, 0:128]),
                reads=[b_outst])
        for tb in (() if "nostores" in stages else range(8)):
            sch.op("dgp", lambda e, hb=hb, tb=tb: e.dma_start(
                out=OUT[hb + tb * 128:hb + tb * 128 + 128, :],
                in_=out_st[:, tb, :]),
                reads=[b_outst])

    sch.finalize_incs()
    sems = {sn: nc.alloc_semaphore(name="sem_" + sn) for sn in sch.cnt}
    with nc.Block() as block:
        def emit(eng, name):
            for fn, waits, sem, inc, tag in sch.ops[name]:
                for ws, wv in waits.items():
                    eng.wait_ge(sems[ws], wv)
                inst = fn(eng)
                if inc:
                    inst.then_inc(sems[sem], inc)
            if name == "gpsimd":
                for sn, v in sch.cnt.items():
                    if v:
                        eng.wait_ge(sems[sn], v)

        block.sync(lambda e: emit(e, "sync"))
        block.scalar(lambda e: emit(e, "scalar"))
        block.vector(lambda e: emit(e, "vector"))
        block.tensor(lambda e: emit(e, "tensor"))
        block.gpsimd(lambda e: emit(e, "gpsimd"))
    return nc


def _threefry2x32(k1, k2, x0, x1):
    u32 = np.uint32
    rot = ((13, 15, 26, 6), (17, 29, 16, 24))

    def rl(x, d):
        return (x << u32(d)) | (x >> u32(32 - d))

    ks = [k1, k2, k1 ^ k2 ^ u32(0x1BD11BDA)]
    x = [x0 + ks[0], x1 + ks[1]]
    order = [(0, 1, 2, 1), (1, 2, 0, 2), (0, 0, 1, 3), (1, 1, 2, 4), (0, 2, 0, 5)]
    with np.errstate(over="ignore"):
        for ri, (rots, a, b, i) in enumerate(order):
            for r in rot[rots]:
                x[0] = x[0] + x[1]
                x[1] = rl(x[1], r)
                x[1] = x[0] ^ x[1]
            x[0] = x[0] + ks[a]
            x[1] = x[1] + ks[b] + u32(i)
    return x


def _jax_bernoulli_key42(p, shape):
    n = int(np.prod(shape))
    counts = np.arange(n, dtype=np.uint64)
    hi = (counts >> np.uint64(32)).astype(np.uint32)
    lo = counts.astype(np.uint32)
    with np.errstate(over="ignore"):
        b1, b2 = _threefry2x32(np.uint32(0), np.uint32(42), hi, lo)
        bits = b1 ^ b2
    fb = (bits >> np.uint32(9)) | np.uint32(0x3F800000)
    u = fb.view(np.float32) - np.float32(1.0)
    return (u < np.float32(p)).reshape(shape)


_SLOPE_BITS = [0x3d800000, 0x3d97de02, 0x3db42f4e, 0x3dd5c855, 0x3dfda51c,
               0x3e16785c, 0x3e3286f6, 0x3e53d0db, 0x3e7b4fc4, 0x3e9515fe,
               0x3eb0e286, 0x3ed1de05, 0x3ef8ffed, 0x3f13b6e5, 0x3f2f41f5,
               0x3f4fefc6]


def _host_constants():
    keep = _jax_bernoulli_key42(0.8, (4, S, D))
    slopes = np.array(_SLOPE_BITS, dtype=np.uint32).view(np.float32)
    pos = np.arange(S, dtype=np.float32)
    bias = -(slopes.astype(np.float32)[:, None] * pos[None, :])
    return keep, bias


def kernel(L0_attn_x, L0_attn_Wq, L0_attn_bq, L0_attn_Wk, L0_attn_bk,
           L0_attn_Wv, L0_attn_bv, L0_attn_Wo, L0_attn_bo,
           L0_ffn_w1, L0_ffn_b1, L0_ffn_w2, L0_ffn_b2):
    x = np.ascontiguousarray(np.asarray(L0_attn_x, dtype=np.float32))
    wq = np.asarray(L0_attn_Wq, dtype=np.float32)
    wk = np.asarray(L0_attn_Wk, dtype=np.float32)
    wv = np.asarray(L0_attn_Wv, dtype=np.float32)
    wo = np.asarray(L0_attn_Wo, dtype=np.float32)
    w1 = np.asarray(L0_ffn_w1, dtype=np.float32)
    w2 = np.asarray(L0_ffn_w2, dtype=np.float32)

    def pc(b, n):
        return np.ascontiguousarray(
            np.asarray(b, dtype=np.float32).reshape(n, 128).T)

    keep, bias = _host_constants()
    eye = np.eye(128, dtype=np.float32)
    shared = {
        "wqT": np.ascontiguousarray(wq.T), "wkT": np.ascontiguousarray(wk.T),
        "wvT": np.ascontiguousarray(wv.T), "woT": np.ascontiguousarray(wo.T),
        "w1T": np.ascontiguousarray(w1.T), "w2T": np.ascontiguousarray(w2.T),
        "bq": pc(L0_attn_bq, 8), "bk": pc(L0_attn_bk, 8), "bo": pc(L0_attn_bo, 8),
        "b1": pc(L0_ffn_b1, 32), "b2": pc(L0_ffn_b2, 8),
        "bv": np.ascontiguousarray(
            np.asarray(L0_attn_bv, dtype=np.float32).reshape(1, D)),
        "eye": eye,
    }

    nc = build_program()
    in_maps = []
    for c in range(8):
        b, h = c // 2, c % 2
        off = h * TOWN
        perm = np.concatenate([np.arange(off, off + TOWN),
                               np.arange(0, off), np.arange(off + TOWN, S)])
        x_rot = np.ascontiguousarray(x[b][perm])
        a8 = (8.0 * bias[:, perm]).astype(np.float32)
        hi = a8.astype(BF16NP)
        lo = (a8 - hi.astype(np.float32)).astype(BF16NP)
        m = keep[b, off:off + TOWN]
        mm = m.reshape(2, 2, 512, 8, 128)
        maskv = (1.25 * mm.astype(np.float32)).transpose(4, 0, 3, 1, 2)
        maskv = np.ascontiguousarray(maskv.reshape(128, 16384)).astype(np.float16)
        in_maps.append(dict(shared, x=x_rot, alhi=np.ascontiguousarray(hi),
                            allo=np.ascontiguousarray(lo), maskv=maskv))

    res = run_bass_kernel_spmd(nc, in_maps, core_ids=list(range(8)))
    out = np.empty((4, S, D), dtype=np.float32)
    for c in range(8):
        b, h = c // 2, c % 2
        out[b, h * TOWN:(h + 1) * TOWN] = res.results[c]["out"]
    return out


# version 14
# speedup vs baseline: 1.0279x; 1.0279x over previous
import numpy as np
import ml_dtypes
import concourse.bass as bass
import concourse.mybir as mybir
from concourse.bass_utils import run_bass_kernel_spmd

F32 = mybir.dt.float32
F16 = mybir.dt.float16
BF16 = mybir.dt.bfloat16
F8 = mybir.dt.float8e4
AX = mybir.AxisListType.X
AF = mybir.ActivationFunctionType
ALU = mybir.AluOpType
BF16NP = ml_dtypes.bfloat16

S, D, NH, HD, FF = 4096, 1024, 16, 64, 4096
TOWN = 2048
NQ = 512

_STREAMS = {
    "tensor": ("tensor", "t", 1),
    "vector": ("vector", "v", 1),
    "scalar": ("scalar", "s", 1),
    "dsync": ("sync", None, 16),
    "dgp": ("gpsimd", None, 16),
    "dscalar": ("scalar", None, 16),
}
_ENGINES = ("sync", "scalar", "vector", "tensor", "gpsimd")


class Buf:
    __slots__ = ("name", "writer", "readers")

    def __init__(self, name):
        self.name = name
        self.writer = None
        self.readers = []


class Sched:
    def __init__(self):
        self.ops = {e: [] for e in _ENGINES}
        self.cnt = {"t": 0, "v": 0, "s": 0}
        self.pend = {e: {} for e in _ENGINES}
        self.already = {e: {} for e in _ENGINES}

    SELF_DIST = 6

    def op(self, stream, fn, reads=(), writes=(), tag=""):
        eng, sem, inc = _STREAMS[stream]
        if inc == 16:
            sem = "D" + writes[0].name if writes else "Dmisc"
            self.cnt.setdefault(sem, 0)
        already = self.already[eng]
        waits = {k: v for k, v in self.pend[eng].items()
                 if already.get(k, -1) < v}
        self.pend[eng] = {}

        def add(dep):
            if dep is None:
                return
            dsem, dval = dep
            if dsem == sem:
                if inc != 1:
                    return
                if sem == "t":
                    return
                if self.cnt[sem] + 1 - dval > self.SELF_DIST:
                    return
            if already.get(dsem, -1) >= dval:
                return
            if waits.get(dsem, -1) < dval:
                waits[dsem] = dval

        for b in reads:
            add(b.writer)
        for b in writes:
            add(b.writer)
            for r in b.readers:
                add(r)
        for ws, wv in waits.items():
            if already.get(ws, -1) < wv:
                already[ws] = wv
        after = self.cnt[sem] + inc
        self.cnt[sem] = after
        self.ops[eng].append((fn, waits, sem, after, tag))
        for b in writes:
            b.writer = (sem, after)
            b.readers = []
        for b in reads:
            b.readers.append((sem, after))

    def barrier(self):
        snap = dict(self.cnt)
        for e in _ENGINES:
            p = self.pend[e]
            a = self.already[e]
            for sn, v in snap.items():
                if a.get(sn, -1) >= v:
                    continue
                if p.get(sn, -1) < v:
                    p[sn] = v

    def finalize_incs(self):
        needed = {}
        for e in _ENGINES:
            for fn, waits, sem, after, tag in self.ops[e]:
                for ws, wv in waits.items():
                    needed.setdefault(ws, set()).add(wv)
        for sn, v in self.cnt.items():
            needed.setdefault(sn, set()).add(v)
        out_ops = {}
        pending = {}
        for e in _ENGINES:
            lst = []
            for fn, waits, sem, after, tag in self.ops[e]:
                if sem not in ("t", "v", "s"):
                    lst.append((fn, waits, sem, 16, tag))
                    continue
                pend = pending.get(sem, 0) + 1
                if after in needed.get(sem, ()):
                    lst.append((fn, waits, sem, pend, tag))
                    pending[sem] = 0
                else:
                    lst.append((fn, waits, sem, 0, tag))
                    pending[sem] = pend
            out_ops[e] = lst
        self.ops = out_ops


def build_program(debug=(), stages=(1, 2, 25, 3)):
    nc = bass.Bass("TRN2", target_bir_lowering=False,
                   detect_race_conditions=False)
    dbg = set(debug)

    def mb(name, n, shape, dt):
        return [nc.alloc_sbuf_tensor(f"{name}{i}", shape, dt) for i in range(n)]

    X = nc.dram_tensor("x", [S, D], F32, kind="ExternalInput").ap()
    WQT = nc.dram_tensor("wqT", [D, D], F32, kind="ExternalInput").ap()
    WKT = nc.dram_tensor("wkT", [D, D], F32, kind="ExternalInput").ap()
    WVT = nc.dram_tensor("wvT", [D, D], F32, kind="ExternalInput").ap()
    WOT = nc.dram_tensor("woT", [D, D], F32, kind="ExternalInput").ap()
    W1T = nc.dram_tensor("w1T", [D, FF], F32, kind="ExternalInput").ap()
    W2T = nc.dram_tensor("w2T", [FF, D], F32, kind="ExternalInput").ap()
    BQ = nc.dram_tensor("bq", [128, 8], F32, kind="ExternalInput").ap()
    BK = nc.dram_tensor("bk", [128, 8], F32, kind="ExternalInput").ap()
    BO = nc.dram_tensor("bo", [128, 8], F32, kind="ExternalInput").ap()
    B1 = nc.dram_tensor("b1", [128, 32], F32, kind="ExternalInput").ap()
    B2 = nc.dram_tensor("b2", [128, 8], F32, kind="ExternalInput").ap()
    BV = nc.dram_tensor("bv", [1, D], F32, kind="ExternalInput").ap()
    ALHI = nc.dram_tensor("alhi", [NH, S], BF16, kind="ExternalInput").ap()
    ALLO = nc.dram_tensor("allo", [NH, S], BF16, kind="ExternalInput").ap()
    MASK = nc.dram_tensor("maskv", [128, 16384], F16, kind="ExternalInput").ap()
    EYE = nc.dram_tensor("eye", [128, 128], F32, kind="ExternalInput").ap()
    OUT = nc.dram_tensor("out", [TOWN, D], F32, kind="ExternalOutput").ap()
    KTS = nc.dram_tensor("kts", [D, S], BF16, kind="Internal").ap()
    QS = nc.dram_tensor("qs", [D, TOWN], BF16, kind="Internal").ap()
    VS = nc.dram_tensor("vs", [S, D], BF16, kind="Internal").ap()

    dbg_out = {}
    for name, shape, dt in (
        ("xn", [S, D], F32), ("ktd", [D, S], F32), ("vd", [S, D], F32),
        ("qtd", [D, TOWN], F32), ("xnqd", [D, TOWN], F32),
        ("oacc", [D, TOWN], F32), ("std", [D, TOWN], F32),
        ("hqd", [FF, 512], F32), ("expd", [S, NQ], F32),
        ("yd", [128, 512], F32), ("gd", [128, 512], F32),
        ("od", [128, 128], F32),
    ):
        if name in dbg:
            dbg_out[name] = nc.dram_tensor("dbg_" + name, shape, dt,
                                           kind="ExternalOutput").ap()

    sch = Sched()
    B = Buf

    eye = nc.alloc_sbuf_tensor("eye_sb", [128, 128], F32)
    ones_r = nc.alloc_sbuf_tensor("ones_r", [128, 128], F32)
    bq_sb = nc.alloc_sbuf_tensor("bq_sb", [128, 8], F32)
    bk_sb = nc.alloc_sbuf_tensor("bk_sb", [128, 8], F32)
    bo_sb = nc.alloc_sbuf_tensor("bo_sb", [128, 8], F32)
    b1_sb = nc.alloc_sbuf_tensor("b1_sb", [128, 32], F32)
    b2_sb = nc.alloc_sbuf_tensor("b2_sb", [128, 8], F32)
    b1_f8 = nc.alloc_sbuf_tensor("b1_f8", [128, 32], F8)
    b2_f8 = nc.alloc_sbuf_tensor("b2_f8", [128, 8], F8)
    b1q = nc.alloc_sbuf_tensor("b1q", [128, 32], F32)
    b2q = nc.alloc_sbuf_tensor("b2q", [128, 8], F32)
    xnq = nc.alloc_sbuf_tensor("xnq", [128, 8, TOWN], F8)
    sT = nc.alloc_sbuf_tensor("sT", [128, 8, TOWN], BF16)

    b_eye, b_ones = B("eye"), B("ones")
    b_bq, b_bk, b_bo = B("bq"), B("bk"), B("bo")
    b_b1s, b_b2s, b_b18, b_b28, b_b1q, b_b2q = (B(x) for x in
        ("b1s", "b2s", "b18", "b28", "b1q", "b2q"))
    b_bv, b_bvrep = B("bv"), B("bvrep")
    b_xnq, b_sT = B("xnq"), B("sT")
    b_wst = [B("wst0"), B("wst1")]
    b_kts, b_vs = B("kts"), B("vs")

    if "nosetup" in stages:
        pass
    else:
        sch.op("dsync", lambda e: e.dma_start(out=eye[:, :], in_=EYE), writes=[b_eye])
        sch.op("dsync", lambda e: e.dma_start(out=bq_sb[:, :], in_=BQ), writes=[b_bq])
        sch.op("dsync", lambda e: e.dma_start(out=bk_sb[:, :], in_=BK), writes=[b_bk])
        sch.op("dsync", lambda e: e.dma_start(out=bo_sb[:, :], in_=BO), writes=[b_bo])
        sch.op("dsync", lambda e: e.dma_start(out=b1_sb[:, :], in_=B1), writes=[b_b1s])
        sch.op("dsync", lambda e: e.dma_start(out=b2_sb[:, :], in_=B2), writes=[b_b2s])
        sch.op("dsync", lambda e: e.dma_start(out=bv_sb[:, :], in_=BV), writes=[b_bv])
        sch.op("vector", lambda e: e.memset(ones_r[:, :], 1.0), writes=[b_ones])
        sch.op("vector", lambda e: e.tensor_copy(b1_f8[:, :], b1_sb[:, :]),
               reads=[b_b1s], writes=[b_b18])
        sch.op("vector", lambda e: e.tensor_copy(b1q[:, :], b1_f8[:, :]),
               reads=[b_b18], writes=[b_b1q])
        sch.op("vector", lambda e: e.tensor_copy(b2_f8[:, :], b2_sb[:, :]),
               reads=[b_b2s], writes=[b_b28])
        sch.op("vector", lambda e: e.tensor_copy(b2q[:, :], b2_f8[:, :]),
               reads=[b_b28], writes=[b_b2q])

    sb_base1 = nc.sbuf_base
    ps_base1 = nc.psum_base
    bv_sb = nc.alloc_sbuf_tensor("bv_sb", [1, D], F32)
    bvrep = nc.alloc_sbuf_tensor("bvrep", [128, D], F32)
    wstage = mb("wstage", 2, [128, D], F32)
    xin = mb("xin", 2, [128, D], F32)
    sq_t = mb("sq_t", 2, [128, D], F32)
    st = mb("st", 2, [128, 16], F32)
    xn_tm = mb("xn_tm", 2, [128, D], F32)
    xnT = mb("xnT", 2, [128, 8, 512], BF16)
    wq_bf = nc.alloc_sbuf_tensor("wq_bf", [128, 8, D], BF16)
    wk_bf = nc.alloc_sbuf_tensor("wk_bf", [128, 8, D], BF16)
    wv_bf = nc.alloc_sbuf_tensor("wv_bf", [128, 8, D], BF16)
    kt_sp = mb("kt_sp", 2, [128, 8, 512], BF16)
    q_sp = mb("q_sp", 2, [128, 8, 512], BF16)
    v_sp = mb("v_sp", 2, [128, D], BF16)

    trA = nc.alloc_psum_tensor("trA", [128, 512], F32)
    trB = nc.alloc_psum_tensor("trB", [128, 512], F32)
    pjK = [nc.alloc_psum_tensor(f"pjK{i}", [128, 512], F32) for i in range(2)]
    pjQ = [nc.alloc_psum_tensor(f"pjQ{i}", [128, 512], F32) for i in range(2)]
    pjV = [nc.alloc_psum_tensor(f"pjV{i}", [128, 512], F32) for i in range(2)]

    b_xin = [B("xin0"), B("xin1")]
    b_sq = [B("sq0"), B("sq1")]
    b_st = [B("st0"), B("st1")]
    b_xntm = [B("xntm0"), B("xntm1")]
    b_xnT = [B("xnT0"), B("xnT1")]
    b_wq, b_wk, b_wv = B("wq"), B("wk"), B("wv")
    b_ktsp = [B("ktsp0"), B("ktsp1")]
    b_qsp = [B("qsp0"), B("qsp1")]
    b_vsp = [B("vsp0"), B("vsp1")]
    b_qs = B("qs")
    b_trA, b_trB = B("trA"), B("trB")
    b_pjK = [B("pjK0"), B("pjK1")]
    b_pjQ = [B("pjQ0"), B("pjQ1")]
    b_pjV = [B("pjV0"), B("pjV1")]

    for half in (() if "nobvrep" in stages else range(2)):
        sch.op("tensor", lambda e, h=half: e.matmul(
            pjV[h][:, :], ones_r[0:1, 0:128], bv_sb[0:1, h * 512:(h + 1) * 512],
            start=True, stop=True),
            reads=[b_ones, b_bv], writes=[b_pjV[half]])
        sch.op("vector", lambda e, h=half: e.tensor_copy(
            bvrep[:, h * 512:(h + 1) * 512], pjV[h][:, :]),
            reads=[b_pjV[half]], writes=[b_bvrep])

    for wdst, bw, src in ((wk_bf, b_wk, WKT), (wv_bf, b_wv, WVT), (wq_bf, b_wq, WQT)):
        for ci in range(8):
            p = ci % 2
            sch.op("dscalar", lambda e, ci=ci, p=p, src=src: e.dma_start(
                out=wstage[p][:, :], in_=src[ci * 128:(ci + 1) * 128, :]),
                writes=[b_wst[p]])
            sch.op("vector", lambda e, ci=ci, p=p, wdst=wdst: e.tensor_copy(
                wdst[:, ci, :], wstage[p][:, :]),
                reads=[b_wst[p]], writes=[bw])

    INV_D = 1.0 / D
    for sc in (range(8) if 1 in stages else ()):
        own = sc < 4
        sp = sc % 2
        def ln_steps(tb):
            t0 = sc * 512 + tb * 128
            p = tb % 2
            bx, bs, bst, bxn = b_xin[p], b_sq[p], b_st[p], b_xntm[p]
            steps = []

            def st_(stream, fn, reads=(), writes=()):
                steps.append((stream, fn, reads, writes))

            st_("dsync", lambda e, t0=t0, p=p: e.dma_start(
                out=xin[p][:, :], in_=X[t0:t0 + 128, :]), (), [bx])
            st_("vector", lambda e, p=p: e.reduce_sum(
                st[p][:, 0:1], xin[p][:, :], axis=AX), [bx], [bst])
            st_("vector", lambda e, p=p: e.tensor_mul(
                sq_t[p][:, :], xin[p][:, :], xin[p][:, :]), [bx], [bs])
            st_("vector", lambda e, p=p: e.reduce_sum(
                st[p][:, 1:2], sq_t[p][:, :], axis=AX), [bs], [bst])
            st_("vector", lambda e, p=p: e.tensor_scalar(
                st[p][:, 2:3], st[p][:, 0:1], INV_D, None, ALU.mult),
                [bst], [bst])
            st_("vector", lambda e, p=p: e.tensor_mul(
                st[p][:, 3:4], st[p][:, 2:3], st[p][:, 2:3]), [bst], [bst])
            st_("vector", lambda e, p=p: e.tensor_scalar(
                st[p][:, 4:5], st[p][:, 1:2], INV_D, 1e-5, ALU.mult, ALU.add),
                [bst], [bst])
            st_("vector", lambda e, p=p: e.tensor_sub(
                st[p][:, 5:6], st[p][:, 4:5], st[p][:, 3:4]), [bst], [bst])
            st_("scalar", lambda e, p=p: e.activation(
                st[p][:, 6:7], st[p][:, 5:6], AF.Sqrt), [bst], [bst])
            st_("vector", lambda e, p=p: e.reciprocal(
                st[p][:, 7:8], st[p][:, 6:7]), [bst], [bst])
            st_("vector", lambda e, p=p: e.tensor_mul(
                st[p][:, 8:9], st[p][:, 7:8], st[p][:, 7:8]), [bst], [bst])
            st_("vector", lambda e, p=p: e.tensor_mul(
                st[p][:, 9:10], st[p][:, 8:9], st[p][:, 5:6]), [bst], [bst])
            st_("vector", lambda e, p=p: e.tensor_scalar(
                st[p][:, 10:11], st[p][:, 9:10], -0.5, 1.5, ALU.mult, ALU.add),
                [bst], [bst])
            st_("vector", lambda e, p=p: e.tensor_mul(
                st[p][:, 11:12], st[p][:, 7:8], st[p][:, 10:11]), [bst], [bst])
            st_("vector", lambda e, p=p: e.tensor_mul(
                st[p][:, 12:13], st[p][:, 2:3], st[p][:, 11:12]), [bst], [bst])
            st_("vector", lambda e, p=p: e.tensor_scalar(
                st[p][:, 13:14], st[p][:, 12:13], -1.0, None, ALU.mult),
                [bst], [bst])
            st_("scalar", lambda e, p=p: e.activation(
                xn_tm[p][:, :], xin[p][:, :], AF.Identity,
                bias=st[p][:, 13:14], scale=st[p][:, 11:12]), [bx, bst], [bxn])
            return steps

        def emit_tr_evac(tb):
            t0 = sc * 512 + tb * 128
            p = tb % 2
            bxn = b_xntm[p]
            if "xn" in dbg:
                sch.op("dgp", lambda e, t0=t0, p=p: e.dma_start(
                    out=dbg_out["xn"][t0:t0 + 128, :], in_=xn_tm[p][:, :]),
                    reads=[bxn])
            for ci in range(8):
                tr, btr = (trA, b_trA) if ci < 4 else (trB, b_trB)
                sch.op("tensor", lambda e, p=p, ci=ci, tr=tr: e.transpose(
                    tr[:, (ci % 4) * 128:(ci % 4) * 128 + 128],
                    xn_tm[p][:, ci * 128:(ci + 1) * 128], eye[:, :]),
                    reads=[bxn, b_eye], writes=[btr])
            for gi, (tr, btr) in enumerate(((trA, b_trA), (trB, b_trB))):
                src3 = tr[:, :].rearrange("p (c j) -> p c j", j=128)
                sch.op("scalar", lambda e, sp=sp, gi=gi, tb=tb, src3=src3:
                       e.copy(xnT[sp][:, gi * 4:gi * 4 + 4,
                                      tb * 128:tb * 128 + 128], src3),
                       reads=[btr], writes=[b_xnT[sp]])
                if own:
                    sch.op("scalar", lambda e, sc=sc, gi=gi, tb=tb, src3=src3:
                           e.copy(
                               xnq[:, gi * 4:gi * 4 + 4,
                                   sc * 512 + tb * 128:sc * 512 + tb * 128 + 128],
                               src3),
                           reads=[btr], writes=[b_xnq])

        for pair in range(2):
            sa = ln_steps(pair * 2)
            sb = ln_steps(pair * 2 + 1)
            for i in range(max(len(sa), len(sb))):
                if i < len(sa):
                    stream, fn, rr, ww = sa[i]
                    sch.op(stream, fn, reads=rr, writes=ww)
                if i < len(sb):
                    stream, fn, rr, ww = sb[i]
                    sch.op(stream, fn, reads=rr, writes=ww)
            emit_tr_evac(pair * 2)
            emit_tr_evac(pair * 2 + 1)
        for co in range(8):
            pj, bpj = pjK[co % 2], b_pjK[co % 2]
            for ci in range(8):
                sch.op("tensor", lambda e, sp=sp, ci=ci, co=co, pj=pj: e.matmul(
                    pj[:, :], wk_bf[:, ci, co * 128:co * 128 + 128],
                    xnT[sp][:, ci, :], start=(ci == 0), stop=(ci == 7)),
                    reads=[b_wk, b_xnT[sp]], writes=[bpj])
            sch.op("scalar", lambda e, sp=sp, co=co, pj=pj: e.activation(
                kt_sp[sp][:, co, :], pj[:, :], AF.Identity,
                bias=bk_sb[:, co:co + 1], scale=1.0),
                reads=[bpj, b_bk], writes=[b_ktsp[sp]])
        for co in range(8):
            sch.op("dgp", lambda e, sp=sp, sc=sc, co=co: e.dma_start(
                out=KTS[co * 128:(co + 1) * 128, sc * 512:(sc + 1) * 512],
                in_=kt_sp[sp][:, co, :]),
                reads=[b_ktsp[sp]], writes=[b_kts])
        if own:
            for co in range(8):
                pj, bpj = pjQ[co % 2], b_pjQ[co % 2]
                for ci in range(8):
                    sch.op("tensor", lambda e, sp=sp, ci=ci, co=co, pj=pj: e.matmul(
                        pj[:, :], wq_bf[:, ci, co * 128:co * 128 + 128],
                        xnT[sp][:, ci, :], start=(ci == 0), stop=(ci == 7)),
                        reads=[b_wq, b_xnT[sp]], writes=[bpj])
                sch.op("scalar", lambda e, sp=sp, co=co, pj=pj, bpj=bpj: e.activation(
                    q_sp[sp][:, co, :], pj[:, :], AF.Identity,
                    bias=bq_sb[:, co:co + 1], scale=1.0),
                    reads=[bpj, b_bq], writes=[b_qsp[sp]])
            for co in range(8):
                sch.op("dgp", lambda e, sp=sp, sc=sc, co=co: e.dma_start(
                    out=QS[co * 128:(co + 1) * 128, sc * 512:(sc + 1) * 512],
                    in_=q_sp[sp][:, co, :]),
                    reads=[b_qsp[sp]], writes=[b_qs])
        for tb in range(4):
            vp = tb % 2
            for eh in range(2):
                pj, bpj = pjV[eh], b_pjV[eh]
                for ci in range(8):
                    sch.op("tensor", lambda e, sp=sp, tb=tb, eh=eh, ci=ci, pj=pj:
                           e.matmul(pj[:, :],
                                    xnT[sp][:, ci, tb * 128:tb * 128 + 128],
                                    wv_bf[:, ci, eh * 512:eh * 512 + 512],
                                    start=(ci == 0), stop=(ci == 7)),
                           reads=[b_wv, b_xnT[sp]], writes=[bpj])
                sch.op("vector", lambda e, vp=vp, eh=eh, pj=pj, bpj=bpj:
                       e.tensor_add(v_sp[vp][:, eh * 512:eh * 512 + 512],
                                    pj[:, :], bvrep[:, eh * 512:eh * 512 + 512]),
                       reads=[bpj, b_bvrep], writes=[b_vsp[vp]])
            sch.op("dgp", lambda e, sc=sc, tb=tb, vp=vp: e.dma_start(
                out=VS[sc * 512 + tb * 128:sc * 512 + tb * 128 + 128, :],
                in_=v_sp[vp][:, :]),
                reads=[b_vsp[vp]], writes=[b_vs])

    if "xnqd" in dbg:
        for co in range(8):
            for tq in range(4):
                sch.op("vector", lambda e, co=co, tq=tq: e.tensor_copy(
                    sq_t[0][:, 0:512],
                    xnq[:, co, tq * 512:(tq + 1) * 512]),
                    reads=[b_xnq], writes=[b_sq[0]])
                sch.op("dgp", lambda e, co=co, tq=tq: e.dma_start(
                    out=dbg_out["xnqd"][co * 128:(co + 1) * 128, tq * 512:(tq + 1) * 512],
                    in_=sq_t[0][:, 0:512]),
                    reads=[b_sq[0]])

    sch.barrier()
    nc.sbuf_base = sb_base1
    nc.psum_base = ps_base1

    o_acc = nc.alloc_sbuf_tensor("o_acc", [128, 8, TOWN], BF16)
    b_oacc = B("oacc")
    sb_base2i = nc.sbuf_base
    ps_base2i = nc.psum_base
    kt_buf = mb("kt_buf", 2, [128, S], BF16)
    v_buf = mb("v_buf", 2, [128, 4224], BF16)
    q_st = mb("q_st", 2, [128, TOWN], BF16)
    expS = mb("expS", 2, [128, 32, NQ], BF16)
    rcp_t = nc.alloc_sbuf_tensor("rcp_t", [128, NQ], F32)
    bc_sb = nc.alloc_sbuf_tensor("bc_sb", [128, NQ], F32)

    sc_ps = [nc.alloc_psum_tensor(f"sc{i}", [128, 1024], F32) for i in range(2)]
    pv_ps = [nc.alloc_psum_tensor(f"pv{i}", [128, NQ], F32) for i in range(2)]
    bc_ps = nc.alloc_psum_tensor("bc", [128, NQ], F32)

    b_kt = [B(f"kt{i}") for i in range(2)]
    b_vb = [B(f"vb{i}") for i in range(2)]
    b_qst = [B("qst0"), B("qst1")]
    b_expg = [[B(f"expg{e}_{g}") for g in range(16)] for e in range(2)]
    b_rcp = B("rcp")
    b_bcs = B("bcs")
    b_sc = [B(f"sc{i}") for i in range(2)]
    b_pv = [B("pv0"), B("pv1")]
    b_bc = B("bc")

    for qb_ in range(2):
        sch.op("vector", lambda e, qb_=qb_: e.memset(q_st[qb_][64:128, :], 0.0),
               writes=[b_qst[qb_]])
        sch.op("vector", lambda e, qb_=qb_: e.memset(q_st[qb_][64:66, :], 1.0),
               writes=[b_qst[qb_]])
    for vb in range(2):
        sch.op("vector", lambda e, vb=vb: e.memset(kt_buf[vb][64:128, :], 0.0),
               writes=[b_kt[vb]])
        sch.op("vector", lambda e, vb=vb: e.memset(v_buf[vb][:, 0:4160:130], 1.0),
               writes=[b_vb[vb]])
        sch.op("vector", lambda e, vb=vb: e.memset(
            v_buf[vb][:, 128:4160:130], 1.0),
            writes=[b_vb[vb]])

    def emit_head_dmas(hh):
        vb = hh % 2
        sch.op("dscalar", lambda e, hh=hh, vb=vb: e.dma_start(
            out=q_st[vb][0:64, :], in_=QS[hh * 64:(hh + 1) * 64, :]),
            reads=[b_qs], writes=[b_qst[vb]])
        sch.op("dsync", lambda e, hh=hh, vb=vb: e.dma_start(
            out=kt_buf[vb][0:64, :], in_=KTS[hh * 64:(hh + 1) * 64, :]),
            reads=[b_kts], writes=[b_kt[vb]])
        sch.op("dsync", lambda e, hh=hh, vb=vb: e.dma_start(
            out=kt_buf[vb][64:65, :], in_=ALHI[hh:hh + 1, :]), writes=[b_kt[vb]])
        sch.op("dsync", lambda e, hh=hh, vb=vb: e.dma_start(
            out=kt_buf[vb][65:66, :], in_=ALLO[hh:hh + 1, :]), writes=[b_kt[vb]])
        for kq in range(4):
            sch.op("dgp" if kq % 2 else "dsync",
                   lambda e, hh=hh, vb=vb, kq=kq: e.dma_start(
                       out=v_buf[vb][:, 64 + 1040 * kq:64 + 1040 * (kq + 1)]
                       .rearrange("p (k j) -> p k j", j=130)[:, :, 0:64],
                       in_=VS[kq * 1024:(kq + 1) * 1024, hh * 64:(hh + 1) * 64]
                       .rearrange("(kc p) j -> p kc j", p=128)),
                   reads=[b_vs], writes=[b_vb[vb]])

    def emit_pv_group(pend, kc0, n):
        hh, qs, pv, bpv, se = pend
        vb = hh % 2
        odd = hh % 2 == 1
        for kc in range(kc0, kc0 + n):
            if odd:
                lhs = v_buf[vb][:, 130 * kc:130 * kc + 128]
                o = pv[0:128, :]
            else:
                lhs = v_buf[vb][:, 130 * kc + 64:130 * kc + 129]
                o = pv[0:65, :]
            sch.op("tensor", lambda e, lhs=lhs, o=o, kc=kc, se=se: e.matmul(
                o, lhs, expS[se][:, kc, :], start=(kc == 0), stop=(kc == 31)),
                reads=[b_vb[vb], b_expg[se][kc // 2]], writes=[bpv])

    def emit_norm(pend):
        hh, qs, pv, bpv, se = pend
        odd = hh % 2 == 1
        dr = 0 if odd else 64
        orow = 64 if odd else 0
        sch.op("vector", lambda e, dr=dr, pv=pv: e.tensor_copy(
            rcp_t[dr:dr + 1, :], pv[dr:dr + 1, :]),
            reads=[bpv], writes=[b_rcp])
        sch.op("tensor", lambda e, dr=dr: e.matmul(
            bc_ps[:, :], ones_r[dr:dr + 1, 0:128], rcp_t[dr:dr + 1, :],
            start=True, stop=True),
            reads=[b_ones, b_rcp], writes=[b_bc])
        sch.op("vector", lambda e, orow=orow: e.reciprocal(
            bc_sb[orow:orow + 64, :], bc_ps[orow:orow + 64, :]),
            reads=[b_bc], writes=[b_bcs])
        sch.op("vector", lambda e, hh=hh, qs=qs, orow=orow, pv=pv: e.tensor_mul(
            o_acc[orow:orow + 64, hh // 2, qs:qs + NQ],
            pv[orow:orow + 64, :], bc_sb[orow:orow + 64, :]),
            reads=[bpv, b_bcs], writes=[b_oacc])

    pending = None
    norm_pend = None
    step2 = 0
    for hh in (range(NH) if 2 in stages else ()):
        vb = hh % 2
        if hh == 0:
            emit_head_dmas(0)
        for qtile in range(TOWN // NQ):
            qs = qtile * NQ
            pv, bpv = pv_ps[step2 % 2], b_pv[step2 % 2]
            se = step2 % 2
            step2 += 1
            for g in range(16):
                sp2, bsp = sc_ps[g % 2], b_sc[g % 2]
                for j in range(2):
                    kc = 2 * g + j
                    sch.op("tensor", lambda e, vb=vb, kc=kc, j=j, qs=qs, sp2=sp2:
                           e.matmul(sp2[:, j * 512:j * 512 + 512],
                                    kt_buf[vb][0:128, kc * 128:kc * 128 + 128],
                                    q_st[vb][0:128, qs:qs + NQ],
                                    start=True, stop=True),
                           reads=[b_kt[vb], b_qst[vb]], writes=[bsp])
                if g == 0 and norm_pend is not None:
                    emit_norm(norm_pend)
                    norm_pend = None
                if pending is not None:
                    emit_pv_group(pending, kc0=2 * g, n=2)
                sch.op("scalar", lambda e, g=g, sp2=sp2, se=se: e.activation(
                    expS[se][:, 2 * g:2 * g + 2, :], sp2[:, :], AF.Exp,
                    bias=0.0, scale=0.125),
                    reads=[bsp], writes=[b_expg[se][g]])
            norm_pend = pending
            pending = (hh, qs, pv, bpv, se)
            if qtile == 0 and hh + 1 < NH:
                emit_head_dmas(hh + 1)
    if norm_pend is not None:
        emit_norm(norm_pend)
    if pending is not None:
        emit_pv_group(pending, kc0=0, n=32)
        emit_norm(pending)

    if "oacc" in dbg:
        for co in range(8):
            for tq in range(4):
                sch.op("vector", lambda e, co=co, tq=tq: e.tensor_copy(
                    rcp_t[:, :], o_acc[:, co, tq * 512:(tq + 1) * 512]),
                    reads=[b_oacc], writes=[b_rcp])
                sch.op("dgp", lambda e, co=co, tq=tq: e.dma_start(
                    out=dbg_out["oacc"][co * 128:(co + 1) * 128,
                                        tq * 512:(tq + 1) * 512],
                    in_=rcp_t[:, :]), reads=[b_rcp])

    sch.barrier()
    nc.sbuf_base = sb_base2i
    nc.psum_base = ps_base2i
    wo_bf = nc.alloc_sbuf_tensor("wo_bf", [128, 8, D], BF16)
    wstage25 = mb("wstage25", 2, [128, D], F32)
    b_wst25 = [B("wst25a"), B("wst25b")]
    b_wo = B("wo")
    pr_ps = [nc.alloc_psum_tensor(f"pr{i}", [128, 512], F32) for i in range(2)]
    b_pr = [B("pr0"), B("pr1")]
    for ci in range(8):
        p = ci % 2
        sch.op("dscalar", lambda e, ci=ci: e.dma_start(
            out=wstage25[ci % 2][:, :], in_=WOT[ci * 128:(ci + 1) * 128, :]),
            writes=[b_wst25[p]])
        sch.op("vector", lambda e, ci=ci, p=p: e.tensor_copy(
            wo_bf[:, ci, :], wstage25[p][:, :]), reads=[b_wst25[p]], writes=[b_wo])
    for co in (range(8) if 25 in stages else ()):
        for ts in range(4):
            pj, bpj = pr_ps[ts % 2], b_pr[ts % 2]
            for ci in range(8):
                sch.op("tensor", lambda e, ci=ci, co=co, ts=ts, pj=pj: e.matmul(
                    pj[:, :], wo_bf[:, ci, co * 128:co * 128 + 128],
                    o_acc[:, ci, ts * 512:(ts + 1) * 512],
                    start=(ci == 0), stop=(ci == 7)),
                    reads=[b_wo, b_oacc], writes=[bpj])
            sch.op("vector", lambda e, co=co, ts=ts, pj=pj, bpj=bpj: e.tensor_scalar(
                sT[:, co, ts * 512:(ts + 1) * 512], pj[:, :],
                bo_sb[:, co:co + 1], None, ALU.add),
                reads=[bpj, b_bo], writes=[b_sT])

    if "std" in dbg:
        dtmp = nc.alloc_sbuf_tensor("dtmp", [128, 512], F32)
        b_dt = B("dtmp")
        for co in range(8):
            for tq in range(4):
                sch.op("vector", lambda e, co=co, tq=tq: e.tensor_copy(
                    dtmp[:, :], sT[:, co, tq * 512:(tq + 1) * 512]),
                    reads=[b_sT], writes=[b_dt])
                sch.op("dgp", lambda e, co=co, tq=tq: e.dma_start(
                    out=dbg_out["std"][co * 128:(co + 1) * 128,
                                       tq * 512:(tq + 1) * 512],
                    in_=dtmp[:, :]), reads=[b_dt])

    sch.barrier()
    nc.sbuf_base = sb_base1
    nc.psum_base = ps_base1

    hq = nc.alloc_sbuf_tensor("hq", [128, 32, 1024], F8)
    W1PRE = 2
    out_st = nc.alloc_sbuf_tensor("out_st", [128, 8, D], F32)
    xin3 = nc.alloc_sbuf_tensor("xin3", [128, 8, D], F32)
    w1st = mb("w1st", 2, [128, 8, 128], F32)
    w1q = mb("w1q", 2, [128, 8, 128], F8)
    w2st = mb("w2st", 4, [128, 8, 128], F32)
    w2q = mb("w2q", 8, [128, 8, 128], F8)
    h16 = mb("h16", 4, [128, 512], F16)
    y16 = mb("y16", 4, [128, 512], F16)
    dp_t = mb("dp_t", 4, [128, 512], F32)
    sg_t = mb("sg_t", 4, [128, 512], F32)
    mk_t = mb("mk_t", 2, [128, 1024], F16)

    f1_ps = [nc.alloc_psum_tensor(f"f1{i}", [128, 512], F32) for i in range(3)]
    f2_ps = [nc.alloc_psum_tensor(f"f2{i}", [128, 512], F32) for i in range(2)]
    tr3_ps = [nc.alloc_psum_tensor(f"tr3{i}", [128, 512], F32) for i in range(2)]

    b_hq = B("hq")
    b_outst = B("outst")
    b_x3 = B("x3")
    b_w1s = [B("w1s0"), B("w1s1")]
    b_w1q = [B("w1q0"), B("w1q1")]
    b_w2s = [B(f"w2s{i}") for i in range(4)]
    b_w2q = [B(f"w2q{i}") for i in range(8)]
    b_h16 = [B(f"h16{i}") for i in range(4)]
    b_y16 = [B(f"y16{i}") for i in range(4)]
    b_dp = [B(f"dp{i}") for i in range(4)]
    b_sg = [B(f"sg{i}") for i in range(4)]
    b_mk = [B("mk0"), B("mk1")]
    b_f1 = [B(f"f1{i}") for i in range(3)]
    b_f2 = [B("f20"), B("f21")]
    b_tr3 = [B("tr30"), B("tr31")]

    if 3 in stages:
        for fo in range(W1PRE):
            p = fo % 2
            sch.op("dsync", lambda e, fo=fo, p=p: e.dma_start(
                out=w1st[p][:, :, :],
                in_=W1T[:, fo * 128:(fo + 1) * 128]
                .rearrange("(c p) j -> p c j", p=128)),
                writes=[b_w1s[p]])
            sch.op("vector", lambda e, p=p: e.tensor_copy(
                w1q[p][:, :, :], w1st[p][:, :, :]),
                reads=[b_w1s[p]], writes=[b_w1q[p]])

    for H in (range(2) if 3 in stages else ()):
        hb = H * 1024
        for tb in (() if "nox3" in stages else range(8)):
            sch.op("dscalar", lambda e, hb=hb, tb=tb: e.dma_start(
                out=xin3[:, tb, :], in_=X[hb + tb * 128:hb + tb * 128 + 128, :]),
                writes=[b_x3])
        for fo in range(32):
            p = fo % 2
            if H == 0 and fo < W1PRE:
                pass
            else:
                if "now1dma" not in stages:
                    sch.op("dsync", lambda e, fo=fo, p=p: e.dma_start(
                        out=w1st[p][:, :, :],
                        in_=W1T[:, fo * 128:(fo + 1) * 128]
                        .rearrange("(c p) j -> p c j", p=128)),
                        writes=[b_w1s[p]])
                if "now1cast" not in stages:
                    sch.op("vector", lambda e, p=p: e.tensor_copy(
                        w1q[p][:, :, :], w1st[p][:, :, :]),
                        reads=[b_w1s[p]], writes=[b_w1q[p]])
            for ts2 in range(2):
                gidx = fo * 2 + ts2
                pj, bpj = f1_ps[gidx % 3], b_f1[gidx % 3]
                hb16, bh16 = h16[gidx % 4], b_h16[gidx % 4]
                for ci in range(8):
                    sch.op("tensor", lambda e, p=p, ci=ci, H=H, ts2=ts2, pj=pj:
                           e.matmul(pj[:, :], w1q[p][:, ci, :],
                                    xnq[:, ci, H * 1024 + ts2 * 512:
                                        H * 1024 + ts2 * 512 + 512],
                                    start=(ci == 0), stop=(ci == 7)),
                           reads=[b_w1q[p], b_xnq], writes=[bpj])
                if "noevac" not in stages:
                    sch.op("vector", lambda e, fo=fo, hb16=hb16, pj=pj, bpj=bpj:
                           e.tensor_scalar(hb16[:, :], pj[:, :],
                                           b1q[:, fo:fo + 1], 0.0,
                                           ALU.add, ALU.max),
                           reads=[bpj, b_b1q], writes=[bh16])
                    sch.op("vector", lambda e, fo=fo, ts2=ts2, hb16=hb16, bh16=bh16:
                           e.tensor_copy(
                               hq[:, fo, ts2 * 512:ts2 * 512 + 512], hb16[:, :]),
                           reads=[bh16], writes=[b_hq])
        if "hqd" in dbg and H == 0:
            for fo in range(32):
                sch.op("vector", lambda e, fo=fo: e.tensor_copy(
                    dp_t[0][:, :], hq[:, fo, 0:512]), reads=[b_hq],
                    writes=[b_dp[0]])
                sch.op("dgp", lambda e, fo=fo: e.dma_start(
                    out=dbg_out["hqd"][fo * 128:(fo + 1) * 128, :],
                    in_=dp_t[0][:, :]), reads=[b_dp[0]])
        def emit_tr_adds(p3):
            pH, pdo, pts2, slot = p3
            for qb in range(4):
                sch.op("tensor", lambda e, pts2=pts2, qb=qb, slot=slot: e.transpose(
                    tr3_ps[pts2][:, qb * 128:qb * 128 + 128],
                    sg_t[slot][:, qb * 128:qb * 128 + 128], eye[:, :]),
                    reads=[b_sg[slot], b_eye], writes=[b_tr3[pts2]])
            for qb in range(4):
                tb = pts2 * 4 + qb
                sch.op("vector", lambda e, pdo=pdo, pts2=pts2, qb=qb, tb=tb:
                       e.tensor_add(out_st[:, tb, pdo * 128:pdo * 128 + 128],
                                    tr3_ps[pts2][:, qb * 128:qb * 128 + 128],
                                    xin3[:, tb, pdo * 128:pdo * 128 + 128]),
                       reads=[b_tr3[pts2], b_x3], writes=[b_outst])

        pend3 = None
        for do in (() if "noffn2" in stages else range(8)):
            for fh in range(4):
                p = (do * 4 + fh) % 4
                q4 = (do * 4 + fh) % 8
                sch.op("dgp", lambda e, do=do, fh=fh, p=p: e.dma_start(
                    out=w2st[p][:, :, :],
                    in_=W2T[fh * 1024:(fh + 1) * 1024, do * 128:(do + 1) * 128]
                    .rearrange("(c p) j -> p c j", p=128)),
                    writes=[b_w2s[p]])
                sch.op("vector", lambda e, p=p, q4=q4: e.tensor_copy(
                    w2q[q4][:, :, :], w2st[p][:, :, :]),
                    reads=[b_w2s[p]], writes=[b_w2q[q4]])
            sch.op("dsync", lambda e, H=H, do=do: e.dma_start(
                out=mk_t[do % 2][:, :],
                in_=MASK[:, H * 8192 + do * 1024:H * 8192 + (do + 1) * 1024]),
                writes=[b_mk[do % 2]])
            for ts2 in range(2):
                if pend3 is not None:
                    emit_tr_adds(pend3)
                slot = (do * 2 + ts2) % 4
                pj, bpj = f2_ps[ts2], b_f2[ts2]
                for fo in range(32):
                    q4 = (do * 4 + fo // 8) % 8
                    sch.op("tensor", lambda e, fo=fo, ts2=ts2, q4=q4, pj=pj:
                           e.matmul(pj[:, :], w2q[q4][:, fo % 8, :],
                                    hq[:, fo, ts2 * 512:ts2 * 512 + 512],
                                    start=(fo == 0), stop=(fo == 31)),
                           reads=[b_w2q[q4], b_hq], writes=[bpj])
                sch.op("vector", lambda e, do=do, slot=slot, pj=pj, bpj=bpj:
                       e.tensor_scalar(y16[slot][:, :], pj[:, :],
                                       b2q[:, do:do + 1], None, ALU.add),
                       reads=[bpj, b_b2q], writes=[b_y16[slot]])
                if "yd" in dbg and H == 0 and do == 0 and ts2 == 0:
                    sch.op("vector", lambda e, slot=slot: e.tensor_copy(
                        sg_t[3][:, :], y16[slot][:, :]),
                        reads=[b_y16[slot]], writes=[b_sg[3]])
                    sch.op("dgp", lambda e: e.dma_start(
                        out=dbg_out["yd"], in_=sg_t[3][:, :]), reads=[b_sg[3]])
                sch.op("vector", lambda e, do=do, ts2=ts2, slot=slot: e.tensor_mul(
                    dp_t[slot][:, :], y16[slot][:, :],
                    mk_t[do % 2][:, ts2 * 512:ts2 * 512 + 512]),
                    reads=[b_y16[slot], b_mk[do % 2]], writes=[b_dp[slot]])
                sch.op("scalar", lambda e, slot=slot: e.activation(
                    dp_t[slot][:, :], dp_t[slot][:, :], AF.Silu),
                    reads=[b_dp[slot]], writes=[b_dp[slot]])
                sch.op("vector", lambda e, do=do, H=H, ts2=ts2, slot=slot: e.tensor_mul(
                    sg_t[slot][:, :], dp_t[slot][:, :],
                    sT[:, do, H * 1024 + ts2 * 512:H * 1024 + ts2 * 512 + 512]),
                    reads=[b_dp[slot], b_sT], writes=[b_sg[slot]])
                if "gd" in dbg and H == 0 and do == 0 and ts2 == 0:
                    sch.op("dgp", lambda e, slot=slot: e.dma_start(
                        out=dbg_out["gd"], in_=sg_t[slot][:, :]), reads=[b_sg[slot]])
                pend3 = (H, do, ts2, slot)
        if pend3 is not None:
            emit_tr_adds(pend3)
            pend3 = None
        if "od" in dbg and H == 0:
            sch.op("dgp", lambda e: e.dma_start(
                out=dbg_out["od"], in_=out_st[:, 0, 0:128]),
                reads=[b_outst])
        for tb in (() if "nostores" in stages else range(8)):
            sch.op("dgp", lambda e, hb=hb, tb=tb: e.dma_start(
                out=OUT[hb + tb * 128:hb + tb * 128 + 128, :],
                in_=out_st[:, tb, :]),
                reads=[b_outst])

    sch.finalize_incs()
    sems = {sn: nc.alloc_semaphore(name="sem_" + sn) for sn in sch.cnt}
    with nc.Block() as block:
        def emit(eng, name):
            for fn, waits, sem, inc, tag in sch.ops[name]:
                for ws, wv in waits.items():
                    eng.wait_ge(sems[ws], wv)
                inst = fn(eng)
                if inc:
                    inst.then_inc(sems[sem], inc)
            if name == "gpsimd":
                for sn, v in sch.cnt.items():
                    if v:
                        eng.wait_ge(sems[sn], v)

        block.sync(lambda e: emit(e, "sync"))
        block.scalar(lambda e: emit(e, "scalar"))
        block.vector(lambda e: emit(e, "vector"))
        block.tensor(lambda e: emit(e, "tensor"))
        block.gpsimd(lambda e: emit(e, "gpsimd"))
    return nc


def _threefry2x32(k1, k2, x0, x1):
    u32 = np.uint32
    rot = ((13, 15, 26, 6), (17, 29, 16, 24))

    def rl(x, d):
        return (x << u32(d)) | (x >> u32(32 - d))

    ks = [k1, k2, k1 ^ k2 ^ u32(0x1BD11BDA)]
    x = [x0 + ks[0], x1 + ks[1]]
    order = [(0, 1, 2, 1), (1, 2, 0, 2), (0, 0, 1, 3), (1, 1, 2, 4), (0, 2, 0, 5)]
    with np.errstate(over="ignore"):
        for ri, (rots, a, b, i) in enumerate(order):
            for r in rot[rots]:
                x[0] = x[0] + x[1]
                x[1] = rl(x[1], r)
                x[1] = x[0] ^ x[1]
            x[0] = x[0] + ks[a]
            x[1] = x[1] + ks[b] + u32(i)
    return x


def _jax_bernoulli_key42(p, shape):
    n = int(np.prod(shape))
    counts = np.arange(n, dtype=np.uint64)
    hi = (counts >> np.uint64(32)).astype(np.uint32)
    lo = counts.astype(np.uint32)
    with np.errstate(over="ignore"):
        b1, b2 = _threefry2x32(np.uint32(0), np.uint32(42), hi, lo)
        bits = b1 ^ b2
    fb = (bits >> np.uint32(9)) | np.uint32(0x3F800000)
    u = fb.view(np.float32) - np.float32(1.0)
    return (u < np.float32(p)).reshape(shape)


_SLOPE_BITS = [0x3d800000, 0x3d97de02, 0x3db42f4e, 0x3dd5c855, 0x3dfda51c,
               0x3e16785c, 0x3e3286f6, 0x3e53d0db, 0x3e7b4fc4, 0x3e9515fe,
               0x3eb0e286, 0x3ed1de05, 0x3ef8ffed, 0x3f13b6e5, 0x3f2f41f5,
               0x3f4fefc6]


def _host_constants():
    keep = _jax_bernoulli_key42(0.8, (4, S, D))
    slopes = np.array(_SLOPE_BITS, dtype=np.uint32).view(np.float32)
    pos = np.arange(S, dtype=np.float32)
    bias = -(slopes.astype(np.float32)[:, None] * pos[None, :])
    return keep, bias


def kernel(L0_attn_x, L0_attn_Wq, L0_attn_bq, L0_attn_Wk, L0_attn_bk,
           L0_attn_Wv, L0_attn_bv, L0_attn_Wo, L0_attn_bo,
           L0_ffn_w1, L0_ffn_b1, L0_ffn_w2, L0_ffn_b2):
    x = np.ascontiguousarray(np.asarray(L0_attn_x, dtype=np.float32))
    wq = np.asarray(L0_attn_Wq, dtype=np.float32)
    wk = np.asarray(L0_attn_Wk, dtype=np.float32)
    wv = np.asarray(L0_attn_Wv, dtype=np.float32)
    wo = np.asarray(L0_attn_Wo, dtype=np.float32)
    w1 = np.asarray(L0_ffn_w1, dtype=np.float32)
    w2 = np.asarray(L0_ffn_w2, dtype=np.float32)

    def pc(b, n):
        return np.ascontiguousarray(
            np.asarray(b, dtype=np.float32).reshape(n, 128).T)

    keep, bias = _host_constants()
    eye = np.eye(128, dtype=np.float32)
    shared = {
        "wqT": np.ascontiguousarray(wq.T), "wkT": np.ascontiguousarray(wk.T),
        "wvT": np.ascontiguousarray(wv.T), "woT": np.ascontiguousarray(wo.T),
        "w1T": np.ascontiguousarray(w1.T), "w2T": np.ascontiguousarray(w2.T),
        "bq": pc(L0_attn_bq, 8), "bk": pc(L0_attn_bk, 8), "bo": pc(L0_attn_bo, 8),
        "b1": pc(L0_ffn_b1, 32), "b2": pc(L0_ffn_b2, 8),
        "bv": np.ascontiguousarray(
            np.asarray(L0_attn_bv, dtype=np.float32).reshape(1, D)),
        "eye": eye,
    }

    nc = build_program()
    in_maps = []
    for c in range(8):
        b, h = c // 2, c % 2
        off = h * TOWN
        perm = np.concatenate([np.arange(off, off + TOWN),
                               np.arange(0, off), np.arange(off + TOWN, S)])
        x_rot = np.ascontiguousarray(x[b][perm])
        a8 = (8.0 * bias[:, perm]).astype(np.float32)
        hi = a8.astype(BF16NP)
        lo = (a8 - hi.astype(np.float32)).astype(BF16NP)
        m = keep[b, off:off + TOWN]
        mm = m.reshape(2, 2, 512, 8, 128)
        maskv = (1.25 * mm.astype(np.float32)).transpose(4, 0, 3, 1, 2)
        maskv = np.ascontiguousarray(maskv.reshape(128, 16384)).astype(np.float16)
        in_maps.append(dict(shared, x=x_rot, alhi=np.ascontiguousarray(hi),
                            allo=np.ascontiguousarray(lo), maskv=maskv))

    res = run_bass_kernel_spmd(nc, in_maps, core_ids=list(range(8)))
    out = np.empty((4, S, D), dtype=np.float32)
    for c in range(8):
        b, h = c // 2, c % 2
        out[b, h * TOWN:(h + 1) * TOWN] = res.results[c]["out"]
    return out


# version 15 (final)
# speedup vs baseline: 1.0300x; 1.0020x over previous
import numpy as np
import ml_dtypes
import concourse.bass as bass
import concourse.mybir as mybir
from concourse.bass_utils import run_bass_kernel_spmd

F32 = mybir.dt.float32
F16 = mybir.dt.float16
BF16 = mybir.dt.bfloat16
F8 = mybir.dt.float8e4
AX = mybir.AxisListType.X
AF = mybir.ActivationFunctionType
ALU = mybir.AluOpType
BF16NP = ml_dtypes.bfloat16

S, D, NH, HD, FF = 4096, 1024, 16, 64, 4096
TOWN = 2048
NQ = 512

_STREAMS = {
    "tensor": ("tensor", "t", 1),
    "vector": ("vector", "v", 1),
    "scalar": ("scalar", "s", 1),
    "dsync": ("sync", None, 16),
    "dgp": ("gpsimd", None, 16),
    "dscalar": ("scalar", None, 16),
}
_ENGINES = ("sync", "scalar", "vector", "tensor", "gpsimd")


class Buf:
    __slots__ = ("name", "writer", "readers")

    def __init__(self, name):
        self.name = name
        self.writer = None
        self.readers = []


class Sched:
    def __init__(self):
        self.ops = {e: [] for e in _ENGINES}
        self.cnt = {"t": 0, "v": 0, "s": 0}
        self.pend = {e: {} for e in _ENGINES}
        self.already = {e: {} for e in _ENGINES}

    SELF_DIST = 6

    def op(self, stream, fn, reads=(), writes=(), tag=""):
        eng, sem, inc = _STREAMS[stream]
        if inc == 16:
            sem = "D" + writes[0].name if writes else "Dmisc"
            self.cnt.setdefault(sem, 0)
        already = self.already[eng]
        waits = {k: v for k, v in self.pend[eng].items()
                 if already.get(k, -1) < v}
        self.pend[eng] = {}

        def add(dep):
            if dep is None:
                return
            dsem, dval = dep
            if dsem == sem:
                if inc != 1:
                    return
                if sem == "t":
                    return
                if self.cnt[sem] + 1 - dval > self.SELF_DIST:
                    return
            if already.get(dsem, -1) >= dval:
                return
            if waits.get(dsem, -1) < dval:
                waits[dsem] = dval

        for b in reads:
            add(b.writer)
        for b in writes:
            add(b.writer)
            for r in b.readers:
                add(r)
        for ws, wv in waits.items():
            if already.get(ws, -1) < wv:
                already[ws] = wv
        after = self.cnt[sem] + inc
        self.cnt[sem] = after
        self.ops[eng].append((fn, waits, sem, after, tag))
        for b in writes:
            b.writer = (sem, after)
            b.readers = []
        for b in reads:
            b.readers.append((sem, after))

    def barrier(self):
        snap = dict(self.cnt)
        for e in _ENGINES:
            p = self.pend[e]
            a = self.already[e]
            for sn, v in snap.items():
                if a.get(sn, -1) >= v:
                    continue
                if p.get(sn, -1) < v:
                    p[sn] = v

    def finalize_incs(self):
        needed = {}
        for e in _ENGINES:
            for fn, waits, sem, after, tag in self.ops[e]:
                for ws, wv in waits.items():
                    needed.setdefault(ws, set()).add(wv)
        for sn, v in self.cnt.items():
            needed.setdefault(sn, set()).add(v)
        out_ops = {}
        pending = {}
        for e in _ENGINES:
            lst = []
            for fn, waits, sem, after, tag in self.ops[e]:
                if sem not in ("t", "v", "s"):
                    lst.append((fn, waits, sem, 16, tag))
                    continue
                pend = pending.get(sem, 0) + 1
                if after in needed.get(sem, ()):
                    lst.append((fn, waits, sem, pend, tag))
                    pending[sem] = 0
                else:
                    lst.append((fn, waits, sem, 0, tag))
                    pending[sem] = pend
            out_ops[e] = lst
        self.ops = out_ops


def build_program(debug=(), stages=(1, 2, 25, 3)):
    nc = bass.Bass("TRN2", target_bir_lowering=False,
                   detect_race_conditions=False)
    dbg = set(debug)

    def mb(name, n, shape, dt):
        return [nc.alloc_sbuf_tensor(f"{name}{i}", shape, dt) for i in range(n)]

    X = nc.dram_tensor("x", [S, D], F32, kind="ExternalInput").ap()
    WQT = nc.dram_tensor("wqT", [D, D], F32, kind="ExternalInput").ap()
    WKT = nc.dram_tensor("wkT", [D, D], F32, kind="ExternalInput").ap()
    WVT = nc.dram_tensor("wvT", [D, D], F32, kind="ExternalInput").ap()
    WOT = nc.dram_tensor("woT", [D, D], F32, kind="ExternalInput").ap()
    W1T = nc.dram_tensor("w1T", [D, FF], F32, kind="ExternalInput").ap()
    W2T = nc.dram_tensor("w2T", [FF, D], F32, kind="ExternalInput").ap()
    BQ = nc.dram_tensor("bq", [128, 8], F32, kind="ExternalInput").ap()
    BK = nc.dram_tensor("bk", [128, 8], F32, kind="ExternalInput").ap()
    BO = nc.dram_tensor("bo", [128, 8], F32, kind="ExternalInput").ap()
    B1 = nc.dram_tensor("b1", [128, 32], F32, kind="ExternalInput").ap()
    B2 = nc.dram_tensor("b2", [128, 8], F32, kind="ExternalInput").ap()
    BV = nc.dram_tensor("bv", [1, D], F32, kind="ExternalInput").ap()
    ALHI = nc.dram_tensor("alhi", [NH, S], BF16, kind="ExternalInput").ap()
    ALLO = nc.dram_tensor("allo", [NH, S], BF16, kind="ExternalInput").ap()
    MASK = nc.dram_tensor("maskv", [128, 16384], F16, kind="ExternalInput").ap()
    EYE = nc.dram_tensor("eye", [128, 128], F32, kind="ExternalInput").ap()
    OUT = nc.dram_tensor("out", [TOWN, D], F32, kind="ExternalOutput").ap()
    KTS = nc.dram_tensor("kts", [D, S], BF16, kind="Internal").ap()
    QS = nc.dram_tensor("qs", [D, TOWN], BF16, kind="Internal").ap()
    VS = nc.dram_tensor("vs", [S, D], BF16, kind="Internal").ap()

    dbg_out = {}
    for name, shape, dt in (
        ("xn", [S, D], F32), ("ktd", [D, S], F32), ("vd", [S, D], F32),
        ("qtd", [D, TOWN], F32), ("xnqd", [D, TOWN], F32),
        ("oacc", [D, TOWN], F32), ("std", [D, TOWN], F32),
        ("hqd", [FF, 512], F32), ("expd", [S, NQ], F32),
        ("yd", [128, 512], F32), ("gd", [128, 512], F32),
        ("od", [128, 128], F32),
    ):
        if name in dbg:
            dbg_out[name] = nc.dram_tensor("dbg_" + name, shape, dt,
                                           kind="ExternalOutput").ap()

    sch = Sched()
    B = Buf

    eye = nc.alloc_sbuf_tensor("eye_sb", [128, 128], F32)
    ones_r = nc.alloc_sbuf_tensor("ones_r", [128, 128], F32)
    bq_sb = nc.alloc_sbuf_tensor("bq_sb", [128, 8], F32)
    bk_sb = nc.alloc_sbuf_tensor("bk_sb", [128, 8], F32)
    bo_sb = nc.alloc_sbuf_tensor("bo_sb", [128, 8], F32)
    b1_sb = nc.alloc_sbuf_tensor("b1_sb", [128, 32], F32)
    b2_sb = nc.alloc_sbuf_tensor("b2_sb", [128, 8], F32)
    b1_f8 = nc.alloc_sbuf_tensor("b1_f8", [128, 32], F8)
    b2_f8 = nc.alloc_sbuf_tensor("b2_f8", [128, 8], F8)
    b1q = nc.alloc_sbuf_tensor("b1q", [128, 32], F32)
    b2q = nc.alloc_sbuf_tensor("b2q", [128, 8], F32)
    xnq = nc.alloc_sbuf_tensor("xnq", [128, 8, TOWN], F8)
    sT = nc.alloc_sbuf_tensor("sT", [128, 8, TOWN], BF16)

    b_eye, b_ones = B("eye"), B("ones")
    b_bq, b_bk, b_bo = B("bq"), B("bk"), B("bo")
    b_b1s, b_b2s, b_b18, b_b28, b_b1q, b_b2q = (B(x) for x in
        ("b1s", "b2s", "b18", "b28", "b1q", "b2q"))
    b_bv, b_bvrep = B("bv"), B("bvrep")
    b_xnq, b_sT = B("xnq"), B("sT")
    b_wst = [B("wst0"), B("wst1")]
    b_kts, b_vs = B("kts"), B("vs")

    if "nosetup" in stages:
        pass
    else:
        sch.op("dsync", lambda e: e.dma_start(out=eye[:, :], in_=EYE), writes=[b_eye])
        sch.op("dsync", lambda e: e.dma_start(out=bq_sb[:, :], in_=BQ), writes=[b_bq])
        sch.op("dsync", lambda e: e.dma_start(out=bk_sb[:, :], in_=BK), writes=[b_bk])
        sch.op("dsync", lambda e: e.dma_start(out=bo_sb[:, :], in_=BO), writes=[b_bo])
        sch.op("dsync", lambda e: e.dma_start(out=b1_sb[:, :], in_=B1), writes=[b_b1s])
        sch.op("dsync", lambda e: e.dma_start(out=b2_sb[:, :], in_=B2), writes=[b_b2s])
        sch.op("dsync", lambda e: e.dma_start(out=bv_sb[:, :], in_=BV), writes=[b_bv])
        sch.op("vector", lambda e: e.memset(ones_r[:, :], 1.0), writes=[b_ones])
        sch.op("vector", lambda e: e.tensor_copy(b1_f8[:, :], b1_sb[:, :]),
               reads=[b_b1s], writes=[b_b18])
        sch.op("vector", lambda e: e.tensor_copy(b1q[:, :], b1_f8[:, :]),
               reads=[b_b18], writes=[b_b1q])
        sch.op("vector", lambda e: e.tensor_copy(b2_f8[:, :], b2_sb[:, :]),
               reads=[b_b2s], writes=[b_b28])
        sch.op("vector", lambda e: e.tensor_copy(b2q[:, :], b2_f8[:, :]),
               reads=[b_b28], writes=[b_b2q])

    sb_base1 = nc.sbuf_base
    ps_base1 = nc.psum_base
    bv_sb = nc.alloc_sbuf_tensor("bv_sb", [1, D], F32)
    bvrep = nc.alloc_sbuf_tensor("bvrep", [128, D], F32)
    wstage = mb("wstage", 2, [128, D], F32)
    xin = mb("xin", 2, [128, D], F32)
    sq_t = mb("sq_t", 2, [128, D], F32)
    st = mb("st", 2, [128, 16], F32)
    xn_tm = mb("xn_tm", 2, [128, D], F32)
    xnT = mb("xnT", 2, [128, 8, 512], BF16)
    wq_bf = nc.alloc_sbuf_tensor("wq_bf", [128, 8, D], BF16)
    wk_bf = nc.alloc_sbuf_tensor("wk_bf", [128, 8, D], BF16)
    wv_bf = nc.alloc_sbuf_tensor("wv_bf", [128, 8, D], BF16)
    kt_sp = mb("kt_sp", 2, [128, 8, 512], BF16)
    q_sp = mb("q_sp", 2, [128, 8, 512], BF16)
    v_sp = mb("v_sp", 2, [128, D], BF16)

    trA = nc.alloc_psum_tensor("trA", [128, 512], F32)
    trB = nc.alloc_psum_tensor("trB", [128, 512], F32)
    pjK = [nc.alloc_psum_tensor(f"pjK{i}", [128, 512], F32) for i in range(2)]
    pjQ = [nc.alloc_psum_tensor(f"pjQ{i}", [128, 512], F32) for i in range(2)]
    pjV = [nc.alloc_psum_tensor(f"pjV{i}", [128, 512], F32) for i in range(2)]

    b_xin = [B("xin0"), B("xin1")]
    b_sq = [B("sq0"), B("sq1")]
    b_st = [B("st0"), B("st1")]
    b_xntm = [B("xntm0"), B("xntm1")]
    b_xnT = [B("xnT0"), B("xnT1")]
    b_wq, b_wk, b_wv = B("wq"), B("wk"), B("wv")
    b_ktsp = [B("ktsp0"), B("ktsp1")]
    b_qsp = [B("qsp0"), B("qsp1")]
    b_vsp = [B("vsp0"), B("vsp1")]
    b_qs = B("qs")
    b_trA, b_trB = B("trA"), B("trB")
    b_pjK = [B("pjK0"), B("pjK1")]
    b_pjQ = [B("pjQ0"), B("pjQ1")]
    b_pjV = [B("pjV0"), B("pjV1")]

    for half in (() if "nobvrep" in stages else range(2)):
        sch.op("tensor", lambda e, h=half: e.matmul(
            pjV[h][:, :], ones_r[0:1, 0:128], bv_sb[0:1, h * 512:(h + 1) * 512],
            start=True, stop=True),
            reads=[b_ones, b_bv], writes=[b_pjV[half]])
        sch.op("vector", lambda e, h=half: e.tensor_copy(
            bvrep[:, h * 512:(h + 1) * 512], pjV[h][:, :]),
            reads=[b_pjV[half]], writes=[b_bvrep])

    for wdst, bw, src in ((wk_bf, b_wk, WKT), (wv_bf, b_wv, WVT), (wq_bf, b_wq, WQT)):
        for ci in range(8):
            p = ci % 2
            sch.op("dscalar", lambda e, ci=ci, p=p, src=src: e.dma_start(
                out=wstage[p][:, :], in_=src[ci * 128:(ci + 1) * 128, :]),
                writes=[b_wst[p]])
            sch.op("vector", lambda e, ci=ci, p=p, wdst=wdst: e.tensor_copy(
                wdst[:, ci, :], wstage[p][:, :]),
                reads=[b_wst[p]], writes=[bw])

    INV_D = 1.0 / D
    for sc in (range(8) if 1 in stages else ()):
        own = sc < 4
        sp = sc % 2
        def ln_steps(tb):
            t0 = sc * 512 + tb * 128
            p = tb % 2
            bx, bs, bst, bxn = b_xin[p], b_sq[p], b_st[p], b_xntm[p]
            steps = []

            def st_(stream, fn, reads=(), writes=()):
                steps.append((stream, fn, reads, writes))

            st_("dsync", lambda e, t0=t0, p=p: e.dma_start(
                out=xin[p][:, :], in_=X[t0:t0 + 128, :]), (), [bx])
            st_("vector", lambda e, p=p: e.reduce_sum(
                st[p][:, 0:1], xin[p][:, :], axis=AX), [bx], [bst])
            st_("vector", lambda e, p=p: e.tensor_mul(
                sq_t[p][:, :], xin[p][:, :], xin[p][:, :]), [bx], [bs])
            st_("vector", lambda e, p=p: e.reduce_sum(
                st[p][:, 1:2], sq_t[p][:, :], axis=AX), [bs], [bst])
            st_("vector", lambda e, p=p: e.tensor_scalar(
                st[p][:, 2:3], st[p][:, 0:1], INV_D, None, ALU.mult),
                [bst], [bst])
            st_("vector", lambda e, p=p: e.tensor_mul(
                st[p][:, 3:4], st[p][:, 2:3], st[p][:, 2:3]), [bst], [bst])
            st_("vector", lambda e, p=p: e.tensor_scalar(
                st[p][:, 4:5], st[p][:, 1:2], INV_D, 1e-5, ALU.mult, ALU.add),
                [bst], [bst])
            st_("vector", lambda e, p=p: e.tensor_sub(
                st[p][:, 5:6], st[p][:, 4:5], st[p][:, 3:4]), [bst], [bst])
            st_("scalar", lambda e, p=p: e.activation(
                st[p][:, 6:7], st[p][:, 5:6], AF.Sqrt), [bst], [bst])
            st_("vector", lambda e, p=p: e.reciprocal(
                st[p][:, 7:8], st[p][:, 6:7]), [bst], [bst])
            st_("vector", lambda e, p=p: e.tensor_mul(
                st[p][:, 8:9], st[p][:, 7:8], st[p][:, 7:8]), [bst], [bst])
            st_("vector", lambda e, p=p: e.tensor_mul(
                st[p][:, 9:10], st[p][:, 8:9], st[p][:, 5:6]), [bst], [bst])
            st_("vector", lambda e, p=p: e.tensor_scalar(
                st[p][:, 10:11], st[p][:, 9:10], -0.5, 1.5, ALU.mult, ALU.add),
                [bst], [bst])
            st_("vector", lambda e, p=p: e.tensor_mul(
                st[p][:, 11:12], st[p][:, 7:8], st[p][:, 10:11]), [bst], [bst])
            st_("vector", lambda e, p=p: e.tensor_mul(
                st[p][:, 12:13], st[p][:, 2:3], st[p][:, 11:12]), [bst], [bst])
            st_("vector", lambda e, p=p: e.tensor_scalar(
                st[p][:, 13:14], st[p][:, 12:13], -1.0, None, ALU.mult),
                [bst], [bst])
            st_("scalar", lambda e, p=p: e.activation(
                xn_tm[p][:, :], xin[p][:, :], AF.Identity,
                bias=st[p][:, 13:14], scale=st[p][:, 11:12]), [bx, bst], [bxn])
            return steps

        def emit_tr_evac(tb):
            t0 = sc * 512 + tb * 128
            p = tb % 2
            bxn = b_xntm[p]
            if "xn" in dbg:
                sch.op("dgp", lambda e, t0=t0, p=p: e.dma_start(
                    out=dbg_out["xn"][t0:t0 + 128, :], in_=xn_tm[p][:, :]),
                    reads=[bxn])
            for ci in range(8):
                tr, btr = (trA, b_trA) if ci < 4 else (trB, b_trB)
                sch.op("tensor", lambda e, p=p, ci=ci, tr=tr: e.transpose(
                    tr[:, (ci % 4) * 128:(ci % 4) * 128 + 128],
                    xn_tm[p][:, ci * 128:(ci + 1) * 128], eye[:, :]),
                    reads=[bxn, b_eye], writes=[btr])
            for gi, (tr, btr) in enumerate(((trA, b_trA), (trB, b_trB))):
                src3 = tr[:, :].rearrange("p (c j) -> p c j", j=128)
                sch.op("scalar", lambda e, sp=sp, gi=gi, tb=tb, src3=src3:
                       e.copy(xnT[sp][:, gi * 4:gi * 4 + 4,
                                      tb * 128:tb * 128 + 128], src3),
                       reads=[btr], writes=[b_xnT[sp]])
                if own:
                    sch.op("scalar", lambda e, sc=sc, gi=gi, tb=tb, src3=src3:
                           e.copy(
                               xnq[:, gi * 4:gi * 4 + 4,
                                   sc * 512 + tb * 128:sc * 512 + tb * 128 + 128],
                               src3),
                           reads=[btr], writes=[b_xnq])

        for pair in range(2):
            sa = ln_steps(pair * 2)
            sb = ln_steps(pair * 2 + 1)
            for i in range(max(len(sa), len(sb))):
                if i < len(sa):
                    stream, fn, rr, ww = sa[i]
                    sch.op(stream, fn, reads=rr, writes=ww)
                if i < len(sb):
                    stream, fn, rr, ww = sb[i]
                    sch.op(stream, fn, reads=rr, writes=ww)
            emit_tr_evac(pair * 2)
            emit_tr_evac(pair * 2 + 1)
        for co in range(8):
            pj, bpj = pjK[co % 2], b_pjK[co % 2]
            for ci in range(8):
                sch.op("tensor", lambda e, sp=sp, ci=ci, co=co, pj=pj: e.matmul(
                    pj[:, :], wk_bf[:, ci, co * 128:co * 128 + 128],
                    xnT[sp][:, ci, :], start=(ci == 0), stop=(ci == 7)),
                    reads=[b_wk, b_xnT[sp]], writes=[bpj])
            sch.op("scalar", lambda e, sp=sp, co=co, pj=pj: e.activation(
                kt_sp[sp][:, co, :], pj[:, :], AF.Identity,
                bias=bk_sb[:, co:co + 1], scale=1.0),
                reads=[bpj, b_bk], writes=[b_ktsp[sp]])
        for co in range(8):
            sch.op("dgp", lambda e, sp=sp, sc=sc, co=co: e.dma_start(
                out=KTS[co * 128:(co + 1) * 128, sc * 512:(sc + 1) * 512],
                in_=kt_sp[sp][:, co, :]),
                reads=[b_ktsp[sp]], writes=[b_kts])
        if own:
            for co in range(8):
                pj, bpj = pjQ[co % 2], b_pjQ[co % 2]
                for ci in range(8):
                    sch.op("tensor", lambda e, sp=sp, ci=ci, co=co, pj=pj: e.matmul(
                        pj[:, :], wq_bf[:, ci, co * 128:co * 128 + 128],
                        xnT[sp][:, ci, :], start=(ci == 0), stop=(ci == 7)),
                        reads=[b_wq, b_xnT[sp]], writes=[bpj])
                sch.op("scalar", lambda e, sp=sp, co=co, pj=pj, bpj=bpj: e.activation(
                    q_sp[sp][:, co, :], pj[:, :], AF.Identity,
                    bias=bq_sb[:, co:co + 1], scale=1.0),
                    reads=[bpj, b_bq], writes=[b_qsp[sp]])
            for co in range(8):
                sch.op("dgp", lambda e, sp=sp, sc=sc, co=co: e.dma_start(
                    out=QS[co * 128:(co + 1) * 128, sc * 512:(sc + 1) * 512],
                    in_=q_sp[sp][:, co, :]),
                    reads=[b_qsp[sp]], writes=[b_qs])
        for tb in range(4):
            vp = tb % 2
            for eh in range(2):
                pj, bpj = pjV[eh], b_pjV[eh]
                for ci in range(8):
                    sch.op("tensor", lambda e, sp=sp, tb=tb, eh=eh, ci=ci, pj=pj:
                           e.matmul(pj[:, :],
                                    xnT[sp][:, ci, tb * 128:tb * 128 + 128],
                                    wv_bf[:, ci, eh * 512:eh * 512 + 512],
                                    start=(ci == 0), stop=(ci == 7)),
                           reads=[b_wv, b_xnT[sp]], writes=[bpj])
                sch.op("vector", lambda e, vp=vp, eh=eh, pj=pj, bpj=bpj:
                       e.tensor_add(v_sp[vp][:, eh * 512:eh * 512 + 512],
                                    pj[:, :], bvrep[:, eh * 512:eh * 512 + 512]),
                       reads=[bpj, b_bvrep], writes=[b_vsp[vp]])
            sch.op("dgp", lambda e, sc=sc, tb=tb, vp=vp: e.dma_start(
                out=VS[sc * 512 + tb * 128:sc * 512 + tb * 128 + 128, :],
                in_=v_sp[vp][:, :]),
                reads=[b_vsp[vp]], writes=[b_vs])

    if "xnqd" in dbg:
        for co in range(8):
            for tq in range(4):
                sch.op("vector", lambda e, co=co, tq=tq: e.tensor_copy(
                    sq_t[0][:, 0:512],
                    xnq[:, co, tq * 512:(tq + 1) * 512]),
                    reads=[b_xnq], writes=[b_sq[0]])
                sch.op("dgp", lambda e, co=co, tq=tq: e.dma_start(
                    out=dbg_out["xnqd"][co * 128:(co + 1) * 128, tq * 512:(tq + 1) * 512],
                    in_=sq_t[0][:, 0:512]),
                    reads=[b_sq[0]])

    sch.barrier()
    nc.sbuf_base = sb_base1
    nc.psum_base = ps_base1

    o_acc = nc.alloc_sbuf_tensor("o_acc", [128, 8, TOWN], BF16)
    b_oacc = B("oacc")
    sb_base2i = nc.sbuf_base
    ps_base2i = nc.psum_base
    kt_buf = mb("kt_buf", 2, [128, S], BF16)
    v_buf = mb("v_buf", 2, [128, 4224], BF16)
    q_st = mb("q_st", 2, [128, TOWN], BF16)
    expS = mb("expS", 2, [128, 32, NQ], BF16)
    rcp_t = nc.alloc_sbuf_tensor("rcp_t", [128, NQ], F32)
    bc_sb = nc.alloc_sbuf_tensor("bc_sb", [128, NQ], F32)

    sc_ps = [nc.alloc_psum_tensor(f"sc{i}", [128, 1024], F32) for i in range(2)]
    pv_ps = [nc.alloc_psum_tensor(f"pv{i}", [128, NQ], F32) for i in range(2)]
    bc_ps = nc.alloc_psum_tensor("bc", [128, NQ], F32)

    b_kt = [B(f"kt{i}") for i in range(2)]
    b_vb = [B(f"vb{i}") for i in range(2)]
    b_qst = [B("qst0"), B("qst1")]
    b_expg = [[B(f"expg{e}_{g}") for g in range(16)] for e in range(2)]
    b_rcp = B("rcp")
    b_bcs = B("bcs")
    b_sc = [B(f"sc{i}") for i in range(2)]
    b_pv = [B("pv0"), B("pv1")]
    b_bc = B("bc")

    for qb_ in range(2):
        sch.op("vector", lambda e, qb_=qb_: e.memset(q_st[qb_][64:128, :], 0.0),
               writes=[b_qst[qb_]])
        sch.op("vector", lambda e, qb_=qb_: e.memset(q_st[qb_][64:66, :], 1.0),
               writes=[b_qst[qb_]])
    for vb in range(2):
        sch.op("vector", lambda e, vb=vb: e.memset(kt_buf[vb][64:128, :], 0.0),
               writes=[b_kt[vb]])
        sch.op("vector", lambda e, vb=vb: e.memset(v_buf[vb][:, 0:4160:130], 1.0),
               writes=[b_vb[vb]])
        sch.op("vector", lambda e, vb=vb: e.memset(
            v_buf[vb][:, 128:4160:130], 1.0),
            writes=[b_vb[vb]])

    def emit_head_dmas(hh):
        vb = hh % 2
        sch.op("dscalar", lambda e, hh=hh, vb=vb: e.dma_start(
            out=q_st[vb][0:64, :], in_=QS[hh * 64:(hh + 1) * 64, :]),
            reads=[b_qs], writes=[b_qst[vb]])
        sch.op("dsync", lambda e, hh=hh, vb=vb: e.dma_start(
            out=kt_buf[vb][0:64, :], in_=KTS[hh * 64:(hh + 1) * 64, :]),
            reads=[b_kts], writes=[b_kt[vb]])
        sch.op("dsync", lambda e, hh=hh, vb=vb: e.dma_start(
            out=kt_buf[vb][64:65, :], in_=ALHI[hh:hh + 1, :]), writes=[b_kt[vb]])
        sch.op("dsync", lambda e, hh=hh, vb=vb: e.dma_start(
            out=kt_buf[vb][65:66, :], in_=ALLO[hh:hh + 1, :]), writes=[b_kt[vb]])
        for kq in range(4):
            sch.op("dgp" if kq % 2 else "dsync",
                   lambda e, hh=hh, vb=vb, kq=kq: e.dma_start(
                       out=v_buf[vb][:, 64 + 1040 * kq:64 + 1040 * (kq + 1)]
                       .rearrange("p (k j) -> p k j", j=130)[:, :, 0:64],
                       in_=VS[kq * 1024:(kq + 1) * 1024, hh * 64:(hh + 1) * 64]
                       .rearrange("(kc p) j -> p kc j", p=128)),
                   reads=[b_vs], writes=[b_vb[vb]])

    def emit_pv_group(pend, kc0, n):
        hh, qs, pv, bpv, se = pend
        vb = hh % 2
        odd = hh % 2 == 1
        for kc in range(kc0, kc0 + n):
            if odd:
                lhs = v_buf[vb][:, 130 * kc:130 * kc + 128]
                o = pv[0:128, :]
            else:
                lhs = v_buf[vb][:, 130 * kc + 64:130 * kc + 129]
                o = pv[0:65, :]
            sch.op("tensor", lambda e, lhs=lhs, o=o, kc=kc, se=se: e.matmul(
                o, lhs, expS[se][:, kc, :], start=(kc == 0), stop=(kc == 31)),
                reads=[b_vb[vb], b_expg[se][kc // 2]], writes=[bpv])

    def emit_norm(pend):
        hh, qs, pv, bpv, se = pend
        odd = hh % 2 == 1
        dr = 0 if odd else 64
        orow = 64 if odd else 0
        sch.op("vector", lambda e, dr=dr, pv=pv: e.tensor_copy(
            rcp_t[dr:dr + 1, :], pv[dr:dr + 1, :]),
            reads=[bpv], writes=[b_rcp])
        sch.op("tensor", lambda e, dr=dr: e.matmul(
            bc_ps[:, :], ones_r[dr:dr + 1, 0:128], rcp_t[dr:dr + 1, :],
            start=True, stop=True),
            reads=[b_ones, b_rcp], writes=[b_bc])
        sch.op("vector", lambda e, orow=orow: e.reciprocal(
            bc_sb[orow:orow + 64, :], bc_ps[orow:orow + 64, :]),
            reads=[b_bc], writes=[b_bcs])
        sch.op("vector", lambda e, hh=hh, qs=qs, orow=orow, pv=pv: e.tensor_mul(
            o_acc[orow:orow + 64, hh // 2, qs:qs + NQ],
            pv[orow:orow + 64, :], bc_sb[orow:orow + 64, :]),
            reads=[bpv, b_bcs], writes=[b_oacc])

    pending = None
    norm_pend = None
    step2 = 0
    for hh in (range(NH) if 2 in stages else ()):
        vb = hh % 2
        if hh == 0:
            emit_head_dmas(0)
        for qtile in range(TOWN // NQ):
            qs = qtile * NQ
            pv, bpv = pv_ps[step2 % 2], b_pv[step2 % 2]
            se = step2 % 2
            step2 += 1
            for g in range(16):
                sp2, bsp = sc_ps[g % 2], b_sc[g % 2]
                for j in range(2):
                    kc = 2 * g + j
                    sch.op("tensor", lambda e, vb=vb, kc=kc, j=j, qs=qs, sp2=sp2:
                           e.matmul(sp2[:, j * 512:j * 512 + 512],
                                    kt_buf[vb][0:128, kc * 128:kc * 128 + 128],
                                    q_st[vb][0:128, qs:qs + NQ],
                                    start=True, stop=True),
                           reads=[b_kt[vb], b_qst[vb]], writes=[bsp])
                if g == 0 and norm_pend is not None:
                    emit_norm(norm_pend)
                    norm_pend = None
                if pending is not None:
                    emit_pv_group(pending, kc0=2 * g, n=2)
                sch.op("scalar", lambda e, g=g, sp2=sp2, se=se: e.activation(
                    expS[se][:, 2 * g:2 * g + 2, :], sp2[:, :], AF.Exp,
                    bias=0.0, scale=0.125),
                    reads=[bsp], writes=[b_expg[se][g]])
            norm_pend = pending
            pending = (hh, qs, pv, bpv, se)
            if qtile == 0 and hh + 1 < NH:
                emit_head_dmas(hh + 1)
    if norm_pend is not None:
        emit_norm(norm_pend)
    if pending is not None:
        emit_pv_group(pending, kc0=0, n=32)
        emit_norm(pending)

    if "oacc" in dbg:
        for co in range(8):
            for tq in range(4):
                sch.op("vector", lambda e, co=co, tq=tq: e.tensor_copy(
                    rcp_t[:, :], o_acc[:, co, tq * 512:(tq + 1) * 512]),
                    reads=[b_oacc], writes=[b_rcp])
                sch.op("dgp", lambda e, co=co, tq=tq: e.dma_start(
                    out=dbg_out["oacc"][co * 128:(co + 1) * 128,
                                        tq * 512:(tq + 1) * 512],
                    in_=rcp_t[:, :]), reads=[b_rcp])

    sch.barrier()
    nc.sbuf_base = sb_base2i
    nc.psum_base = ps_base2i
    wo_bf = nc.alloc_sbuf_tensor("wo_bf", [128, 8, D], BF16)
    wstage25 = mb("wstage25", 2, [128, D], F32)
    b_wst25 = [B("wst25a"), B("wst25b")]
    b_wo = B("wo")
    pr_ps = [nc.alloc_psum_tensor(f"pr{i}", [128, 512], F32) for i in range(2)]
    b_pr = [B("pr0"), B("pr1")]
    for ci in range(8):
        p = ci % 2
        sch.op("dscalar", lambda e, ci=ci: e.dma_start(
            out=wstage25[ci % 2][:, :], in_=WOT[ci * 128:(ci + 1) * 128, :]),
            writes=[b_wst25[p]])
        sch.op("vector", lambda e, ci=ci, p=p: e.tensor_copy(
            wo_bf[:, ci, :], wstage25[p][:, :]), reads=[b_wst25[p]], writes=[b_wo])
    for co in (range(8) if 25 in stages else ()):
        for ts in range(4):
            pj, bpj = pr_ps[ts % 2], b_pr[ts % 2]
            for ci in range(8):
                sch.op("tensor", lambda e, ci=ci, co=co, ts=ts, pj=pj: e.matmul(
                    pj[:, :], wo_bf[:, ci, co * 128:co * 128 + 128],
                    o_acc[:, ci, ts * 512:(ts + 1) * 512],
                    start=(ci == 0), stop=(ci == 7)),
                    reads=[b_wo, b_oacc], writes=[bpj])
            sch.op("vector", lambda e, co=co, ts=ts, pj=pj, bpj=bpj: e.tensor_scalar(
                sT[:, co, ts * 512:(ts + 1) * 512], pj[:, :],
                bo_sb[:, co:co + 1], None, ALU.add),
                reads=[bpj, b_bo], writes=[b_sT])

    if "std" in dbg:
        dtmp = nc.alloc_sbuf_tensor("dtmp", [128, 512], F32)
        b_dt = B("dtmp")
        for co in range(8):
            for tq in range(4):
                sch.op("vector", lambda e, co=co, tq=tq: e.tensor_copy(
                    dtmp[:, :], sT[:, co, tq * 512:(tq + 1) * 512]),
                    reads=[b_sT], writes=[b_dt])
                sch.op("dgp", lambda e, co=co, tq=tq: e.dma_start(
                    out=dbg_out["std"][co * 128:(co + 1) * 128,
                                       tq * 512:(tq + 1) * 512],
                    in_=dtmp[:, :]), reads=[b_dt])

    sch.barrier()
    nc.sbuf_base = sb_base1
    nc.psum_base = ps_base1

    hq = nc.alloc_sbuf_tensor("hq", [128, 32, 1024], F8)
    W1PRE = 2
    out_st = nc.alloc_sbuf_tensor("out_st", [128, 8, D], F32)
    xin3 = nc.alloc_sbuf_tensor("xin3", [128, 8, D], F32)
    w1st = mb("w1st", 2, [128, 8, 128], F32)
    w1q = mb("w1q", 2, [128, 8, 128], F8)
    w2st = mb("w2st", 4, [128, 8, 128], F32)
    w2q = mb("w2q", 8, [128, 8, 128], F8)
    h16 = mb("h16", 4, [128, 512], F16)
    y16 = mb("y16", 4, [128, 512], F16)
    dp_t = mb("dp_t", 4, [128, 512], F32)
    sg_t = mb("sg_t", 4, [128, 512], F32)
    mk_t = mb("mk_t", 2, [128, 1024], F16)

    f1_ps = [nc.alloc_psum_tensor(f"f1{i}", [128, 512], F32) for i in range(3)]
    f2_ps = [nc.alloc_psum_tensor(f"f2{i}", [128, 512], F32) for i in range(2)]
    tr3_ps = [nc.alloc_psum_tensor(f"tr3{i}", [128, 512], F32) for i in range(2)]

    b_hq = B("hq")
    b_outst = B("outst")
    b_x3 = B("x3")
    b_w1s = [B("w1s0"), B("w1s1")]
    b_w1q = [B("w1q0"), B("w1q1")]
    b_w2s = [B(f"w2s{i}") for i in range(4)]
    b_w2q = [B(f"w2q{i}") for i in range(8)]
    b_h16 = [B(f"h16{i}") for i in range(4)]
    b_y16 = [B(f"y16{i}") for i in range(4)]
    b_dp = [B(f"dp{i}") for i in range(4)]
    b_sg = [B(f"sg{i}") for i in range(4)]
    b_mk = [B("mk0"), B("mk1")]
    b_f1 = [B(f"f1{i}") for i in range(3)]
    b_f2 = [B("f20"), B("f21")]
    b_tr3 = [B("tr30"), B("tr31")]

    if 3 in stages:
        for fo in range(W1PRE):
            p = fo % 2
            sch.op("dsync", lambda e, fo=fo, p=p: e.dma_start(
                out=w1st[p][:, :, :],
                in_=W1T[:, fo * 128:(fo + 1) * 128]
                .rearrange("(c p) j -> p c j", p=128)),
                writes=[b_w1s[p]])
            sch.op("vector", lambda e, p=p: e.tensor_copy(
                w1q[p][:, :, :], w1st[p][:, :, :]),
                reads=[b_w1s[p]], writes=[b_w1q[p]])

    for H in (range(2) if 3 in stages else ()):
        hb = H * 1024
        for tb in (() if "nox3" in stages else range(8)):
            sch.op("dscalar", lambda e, hb=hb, tb=tb: e.dma_start(
                out=xin3[:, tb, :], in_=X[hb + tb * 128:hb + tb * 128 + 128, :]),
                writes=[b_x3])
        for fo in range(32):
            p = fo % 2
            if H == 0 and fo < W1PRE:
                pass
            else:
                if "now1dma" not in stages:
                    sch.op("dsync", lambda e, fo=fo, p=p: e.dma_start(
                        out=w1st[p][:, :, :],
                        in_=W1T[:, fo * 128:(fo + 1) * 128]
                        .rearrange("(c p) j -> p c j", p=128)),
                        writes=[b_w1s[p]])
                if "now1cast" not in stages:
                    sch.op("vector", lambda e, p=p: e.tensor_copy(
                        w1q[p][:, :, :], w1st[p][:, :, :]),
                        reads=[b_w1s[p]], writes=[b_w1q[p]])
            for ts2 in range(2):
                gidx = fo * 2 + ts2
                pj, bpj = f1_ps[gidx % 3], b_f1[gidx % 3]
                hb16, bh16 = h16[gidx % 4], b_h16[gidx % 4]
                for ci in range(8):
                    sch.op("tensor", lambda e, p=p, ci=ci, H=H, ts2=ts2, pj=pj:
                           e.matmul(pj[:, :], w1q[p][:, ci, :],
                                    xnq[:, ci, H * 1024 + ts2 * 512:
                                        H * 1024 + ts2 * 512 + 512],
                                    start=(ci == 0), stop=(ci == 7)),
                           reads=[b_w1q[p], b_xnq], writes=[bpj])
                if "noevac" not in stages:
                    sch.op("vector", lambda e, fo=fo, hb16=hb16, pj=pj, bpj=bpj:
                           e.tensor_scalar(hb16[:, :], pj[:, :],
                                           b1q[:, fo:fo + 1], 0.0,
                                           ALU.add, ALU.max),
                           reads=[bpj, b_b1q], writes=[bh16])
                    sch.op("vector", lambda e, fo=fo, ts2=ts2, hb16=hb16, bh16=bh16:
                           e.tensor_copy(
                               hq[:, fo, ts2 * 512:ts2 * 512 + 512], hb16[:, :]),
                           reads=[bh16], writes=[b_hq])
        if "hqd" in dbg and H == 0:
            for fo in range(32):
                sch.op("vector", lambda e, fo=fo: e.tensor_copy(
                    dp_t[0][:, :], hq[:, fo, 0:512]), reads=[b_hq],
                    writes=[b_dp[0]])
                sch.op("dgp", lambda e, fo=fo: e.dma_start(
                    out=dbg_out["hqd"][fo * 128:(fo + 1) * 128, :],
                    in_=dp_t[0][:, :]), reads=[b_dp[0]])
        def emit_tr_adds(p3):
            pH, pdo, pts2, slot = p3
            for qb in range(4):
                sch.op("tensor", lambda e, pts2=pts2, qb=qb, slot=slot: e.transpose(
                    tr3_ps[pts2][:, qb * 128:qb * 128 + 128],
                    sg_t[slot][:, qb * 128:qb * 128 + 128], eye[:, :]),
                    reads=[b_sg[slot], b_eye], writes=[b_tr3[pts2]])
            for qb in range(4):
                tb = pts2 * 4 + qb
                sch.op("vector", lambda e, pdo=pdo, pts2=pts2, qb=qb, tb=tb:
                       e.tensor_add(out_st[:, tb, pdo * 128:pdo * 128 + 128],
                                    tr3_ps[pts2][:, qb * 128:qb * 128 + 128],
                                    xin3[:, tb, pdo * 128:pdo * 128 + 128]),
                       reads=[b_tr3[pts2], b_x3], writes=[b_outst])

        pend3 = None
        for do in (() if "noffn2" in stages else range(8)):
            for fh in range(4):
                p = (do * 4 + fh) % 4
                q4 = (do * 4 + fh) % 8
                sch.op("dgp" if fh % 2 else "dsync",
                       lambda e, do=do, fh=fh, p=p: e.dma_start(
                           out=w2st[p][:, :, :],
                           in_=W2T[fh * 1024:(fh + 1) * 1024,
                                   do * 128:(do + 1) * 128]
                           .rearrange("(c p) j -> p c j", p=128)),
                       writes=[b_w2s[p]])
                sch.op("vector", lambda e, p=p, q4=q4: e.tensor_copy(
                    w2q[q4][:, :, :], w2st[p][:, :, :]),
                    reads=[b_w2s[p]], writes=[b_w2q[q4]])
            sch.op("dsync", lambda e, H=H, do=do: e.dma_start(
                out=mk_t[do % 2][:, :],
                in_=MASK[:, H * 8192 + do * 1024:H * 8192 + (do + 1) * 1024]),
                writes=[b_mk[do % 2]])
            for ts2 in range(2):
                if pend3 is not None:
                    emit_tr_adds(pend3)
                slot = (do * 2 + ts2) % 4
                pj, bpj = f2_ps[ts2], b_f2[ts2]
                for fo in range(32):
                    q4 = (do * 4 + fo // 8) % 8
                    sch.op("tensor", lambda e, fo=fo, ts2=ts2, q4=q4, pj=pj:
                           e.matmul(pj[:, :], w2q[q4][:, fo % 8, :],
                                    hq[:, fo, ts2 * 512:ts2 * 512 + 512],
                                    start=(fo == 0), stop=(fo == 31)),
                           reads=[b_w2q[q4], b_hq], writes=[bpj])
                sch.op("vector", lambda e, do=do, slot=slot, pj=pj, bpj=bpj:
                       e.tensor_scalar(y16[slot][:, :], pj[:, :],
                                       b2q[:, do:do + 1], None, ALU.add),
                       reads=[bpj, b_b2q], writes=[b_y16[slot]])
                if "yd" in dbg and H == 0 and do == 0 and ts2 == 0:
                    sch.op("vector", lambda e, slot=slot: e.tensor_copy(
                        sg_t[3][:, :], y16[slot][:, :]),
                        reads=[b_y16[slot]], writes=[b_sg[3]])
                    sch.op("dgp", lambda e: e.dma_start(
                        out=dbg_out["yd"], in_=sg_t[3][:, :]), reads=[b_sg[3]])
                sch.op("vector", lambda e, do=do, ts2=ts2, slot=slot: e.tensor_mul(
                    dp_t[slot][:, :], y16[slot][:, :],
                    mk_t[do % 2][:, ts2 * 512:ts2 * 512 + 512]),
                    reads=[b_y16[slot], b_mk[do % 2]], writes=[b_dp[slot]])
                sch.op("scalar", lambda e, slot=slot: e.activation(
                    dp_t[slot][:, :], dp_t[slot][:, :], AF.Silu),
                    reads=[b_dp[slot]], writes=[b_dp[slot]])
                sch.op("vector", lambda e, do=do, H=H, ts2=ts2, slot=slot: e.tensor_mul(
                    sg_t[slot][:, :], dp_t[slot][:, :],
                    sT[:, do, H * 1024 + ts2 * 512:H * 1024 + ts2 * 512 + 512]),
                    reads=[b_dp[slot], b_sT], writes=[b_sg[slot]])
                if "gd" in dbg and H == 0 and do == 0 and ts2 == 0:
                    sch.op("dgp", lambda e, slot=slot: e.dma_start(
                        out=dbg_out["gd"], in_=sg_t[slot][:, :]), reads=[b_sg[slot]])
                pend3 = (H, do, ts2, slot)
        if pend3 is not None:
            emit_tr_adds(pend3)
            pend3 = None
        if "od" in dbg and H == 0:
            sch.op("dgp", lambda e: e.dma_start(
                out=dbg_out["od"], in_=out_st[:, 0, 0:128]),
                reads=[b_outst])
        for tb in (() if "nostores" in stages else range(8)):
            sch.op("dgp", lambda e, hb=hb, tb=tb: e.dma_start(
                out=OUT[hb + tb * 128:hb + tb * 128 + 128, :],
                in_=out_st[:, tb, :]),
                reads=[b_outst])

    sch.finalize_incs()
    sems = {sn: nc.alloc_semaphore(name="sem_" + sn) for sn in sch.cnt}
    with nc.Block() as block:
        def emit(eng, name):
            for fn, waits, sem, inc, tag in sch.ops[name]:
                for ws, wv in waits.items():
                    eng.wait_ge(sems[ws], wv)
                inst = fn(eng)
                if inc:
                    inst.then_inc(sems[sem], inc)
            if name == "gpsimd":
                for sn, v in sch.cnt.items():
                    if v:
                        eng.wait_ge(sems[sn], v)

        block.sync(lambda e: emit(e, "sync"))
        block.scalar(lambda e: emit(e, "scalar"))
        block.vector(lambda e: emit(e, "vector"))
        block.tensor(lambda e: emit(e, "tensor"))
        block.gpsimd(lambda e: emit(e, "gpsimd"))
    return nc


def _threefry2x32(k1, k2, x0, x1):
    u32 = np.uint32
    rot = ((13, 15, 26, 6), (17, 29, 16, 24))

    def rl(x, d):
        return (x << u32(d)) | (x >> u32(32 - d))

    ks = [k1, k2, k1 ^ k2 ^ u32(0x1BD11BDA)]
    x = [x0 + ks[0], x1 + ks[1]]
    order = [(0, 1, 2, 1), (1, 2, 0, 2), (0, 0, 1, 3), (1, 1, 2, 4), (0, 2, 0, 5)]
    with np.errstate(over="ignore"):
        for ri, (rots, a, b, i) in enumerate(order):
            for r in rot[rots]:
                x[0] = x[0] + x[1]
                x[1] = rl(x[1], r)
                x[1] = x[0] ^ x[1]
            x[0] = x[0] + ks[a]
            x[1] = x[1] + ks[b] + u32(i)
    return x


def _jax_bernoulli_key42(p, shape):
    n = int(np.prod(shape))
    counts = np.arange(n, dtype=np.uint64)
    hi = (counts >> np.uint64(32)).astype(np.uint32)
    lo = counts.astype(np.uint32)
    with np.errstate(over="ignore"):
        b1, b2 = _threefry2x32(np.uint32(0), np.uint32(42), hi, lo)
        bits = b1 ^ b2
    fb = (bits >> np.uint32(9)) | np.uint32(0x3F800000)
    u = fb.view(np.float32) - np.float32(1.0)
    return (u < np.float32(p)).reshape(shape)


_SLOPE_BITS = [0x3d800000, 0x3d97de02, 0x3db42f4e, 0x3dd5c855, 0x3dfda51c,
               0x3e16785c, 0x3e3286f6, 0x3e53d0db, 0x3e7b4fc4, 0x3e9515fe,
               0x3eb0e286, 0x3ed1de05, 0x3ef8ffed, 0x3f13b6e5, 0x3f2f41f5,
               0x3f4fefc6]


def _host_constants():
    keep = _jax_bernoulli_key42(0.8, (4, S, D))
    slopes = np.array(_SLOPE_BITS, dtype=np.uint32).view(np.float32)
    pos = np.arange(S, dtype=np.float32)
    bias = -(slopes.astype(np.float32)[:, None] * pos[None, :])
    return keep, bias


def kernel(L0_attn_x, L0_attn_Wq, L0_attn_bq, L0_attn_Wk, L0_attn_bk,
           L0_attn_Wv, L0_attn_bv, L0_attn_Wo, L0_attn_bo,
           L0_ffn_w1, L0_ffn_b1, L0_ffn_w2, L0_ffn_b2):
    x = np.ascontiguousarray(np.asarray(L0_attn_x, dtype=np.float32))
    wq = np.asarray(L0_attn_Wq, dtype=np.float32)
    wk = np.asarray(L0_attn_Wk, dtype=np.float32)
    wv = np.asarray(L0_attn_Wv, dtype=np.float32)
    wo = np.asarray(L0_attn_Wo, dtype=np.float32)
    w1 = np.asarray(L0_ffn_w1, dtype=np.float32)
    w2 = np.asarray(L0_ffn_w2, dtype=np.float32)

    def pc(b, n):
        return np.ascontiguousarray(
            np.asarray(b, dtype=np.float32).reshape(n, 128).T)

    keep, bias = _host_constants()
    eye = np.eye(128, dtype=np.float32)
    shared = {
        "wqT": np.ascontiguousarray(wq.T), "wkT": np.ascontiguousarray(wk.T),
        "wvT": np.ascontiguousarray(wv.T), "woT": np.ascontiguousarray(wo.T),
        "w1T": np.ascontiguousarray(w1.T), "w2T": np.ascontiguousarray(w2.T),
        "bq": pc(L0_attn_bq, 8), "bk": pc(L0_attn_bk, 8), "bo": pc(L0_attn_bo, 8),
        "b1": pc(L0_ffn_b1, 32), "b2": pc(L0_ffn_b2, 8),
        "bv": np.ascontiguousarray(
            np.asarray(L0_attn_bv, dtype=np.float32).reshape(1, D)),
        "eye": eye,
    }

    nc = build_program()
    in_maps = []
    for c in range(8):
        b, h = c // 2, c % 2
        off = h * TOWN
        perm = np.concatenate([np.arange(off, off + TOWN),
                               np.arange(0, off), np.arange(off + TOWN, S)])
        x_rot = np.ascontiguousarray(x[b][perm])
        a8 = (8.0 * bias[:, perm]).astype(np.float32)
        hi = a8.astype(BF16NP)
        lo = (a8 - hi.astype(np.float32)).astype(BF16NP)
        m = keep[b, off:off + TOWN]
        mm = m.reshape(2, 2, 512, 8, 128)
        maskv = (1.25 * mm.astype(np.float32)).transpose(4, 0, 3, 1, 2)
        maskv = np.ascontiguousarray(maskv.reshape(128, 16384)).astype(np.float16)
        in_maps.append(dict(shared, x=x_rot, alhi=np.ascontiguousarray(hi),
                            allo=np.ascontiguousarray(lo), maskv=maskv))

    res = run_bass_kernel_spmd(nc, in_maps, core_ids=list(range(8)))
    out = np.empty((4, S, D), dtype=np.float32)
    for c in range(8):
        b, h = c // 2, c % 2
        out[b, h * TOWN:(h + 1) * TOWN] = res.results[c]["out"]
    return out


# version 16
# speedup vs baseline: 1.0301x; 1.0001x over previous
import numpy as np
import ml_dtypes
import concourse.bass as bass
import concourse.mybir as mybir
from concourse.bass_utils import run_bass_kernel_spmd

F32 = mybir.dt.float32
F16 = mybir.dt.float16
BF16 = mybir.dt.bfloat16
F8 = mybir.dt.float8e4
AX = mybir.AxisListType.X
AF = mybir.ActivationFunctionType
ALU = mybir.AluOpType
BF16NP = ml_dtypes.bfloat16

S, D, NH, HD, FF = 4096, 1024, 16, 64, 4096
TOWN = 2048
NQ = 512

_STREAMS = {
    "tensor": ("tensor", "t", 1),
    "vector": ("vector", "v", 1),
    "scalar": ("scalar", "s", 1),
    "dsync": ("sync", None, 16),
    "dgp": ("gpsimd", None, 16),
    "dscalar": ("scalar", None, 16),
}
_ENGINES = ("sync", "scalar", "vector", "tensor", "gpsimd")


class Buf:
    __slots__ = ("name", "writer", "readers")

    def __init__(self, name):
        self.name = name
        self.writer = None
        self.readers = []


class Sched:
    def __init__(self):
        self.ops = {e: [] for e in _ENGINES}
        self.cnt = {"t": 0, "v": 0, "s": 0}
        self.pend = {e: {} for e in _ENGINES}
        self.already = {e: {} for e in _ENGINES}

    SELF_DIST = 6

    def op(self, stream, fn, reads=(), writes=(), tag=""):
        eng, sem, inc = _STREAMS[stream]
        if inc == 16:
            sem = "D" + writes[0].name if writes else "Dmisc"
            self.cnt.setdefault(sem, 0)
        already = self.already[eng]
        waits = {k: v for k, v in self.pend[eng].items()
                 if already.get(k, -1) < v}
        self.pend[eng] = {}

        def add(dep):
            if dep is None:
                return
            dsem, dval = dep
            if dsem == sem:
                if inc != 1:
                    return
                if sem == "t":
                    return
                if self.cnt[sem] + 1 - dval > self.SELF_DIST:
                    return
            if already.get(dsem, -1) >= dval:
                return
            if waits.get(dsem, -1) < dval:
                waits[dsem] = dval

        for b in reads:
            add(b.writer)
        for b in writes:
            add(b.writer)
            for r in b.readers:
                add(r)
        for ws, wv in waits.items():
            if already.get(ws, -1) < wv:
                already[ws] = wv
        after = self.cnt[sem] + inc
        self.cnt[sem] = after
        self.ops[eng].append((fn, waits, sem, after, tag))
        for b in writes:
            b.writer = (sem, after)
            b.readers = []
        for b in reads:
            b.readers.append((sem, after))

    def barrier(self):
        snap = dict(self.cnt)
        for e in _ENGINES:
            p = self.pend[e]
            a = self.already[e]
            for sn, v in snap.items():
                if a.get(sn, -1) >= v:
                    continue
                if p.get(sn, -1) < v:
                    p[sn] = v

    def finalize_incs(self):
        needed = {}
        for e in _ENGINES:
            for fn, waits, sem, after, tag in self.ops[e]:
                for ws, wv in waits.items():
                    needed.setdefault(ws, set()).add(wv)
        for sn, v in self.cnt.items():
            needed.setdefault(sn, set()).add(v)
        out_ops = {}
        pending = {}
        for e in _ENGINES:
            lst = []
            for fn, waits, sem, after, tag in self.ops[e]:
                if sem not in ("t", "v", "s"):
                    lst.append((fn, waits, sem, 16, tag))
                    continue
                pend = pending.get(sem, 0) + 1
                if after in needed.get(sem, ()):
                    lst.append((fn, waits, sem, pend, tag))
                    pending[sem] = 0
                else:
                    lst.append((fn, waits, sem, 0, tag))
                    pending[sem] = pend
            out_ops[e] = lst
        self.ops = out_ops


def build_program(debug=(), stages=(1, 2, 25, 3)):
    nc = bass.Bass("TRN2", target_bir_lowering=False,
                   detect_race_conditions=False)
    dbg = set(debug)

    def mb(name, n, shape, dt):
        return [nc.alloc_sbuf_tensor(f"{name}{i}", shape, dt) for i in range(n)]

    X = nc.dram_tensor("x", [S, D], F32, kind="ExternalInput").ap()
    WQT = nc.dram_tensor("wqT", [D, D], F32, kind="ExternalInput").ap()
    WKT = nc.dram_tensor("wkT", [D, D], F32, kind="ExternalInput").ap()
    WVT = nc.dram_tensor("wvT", [D, D], F32, kind="ExternalInput").ap()
    WOT = nc.dram_tensor("woT", [D, D], F32, kind="ExternalInput").ap()
    W1T = nc.dram_tensor("w1T", [D, FF], F32, kind="ExternalInput").ap()
    W2T = nc.dram_tensor("w2T", [FF, D], F32, kind="ExternalInput").ap()
    BQ = nc.dram_tensor("bq", [128, 8], F32, kind="ExternalInput").ap()
    BK = nc.dram_tensor("bk", [128, 8], F32, kind="ExternalInput").ap()
    BO = nc.dram_tensor("bo", [128, 8], F32, kind="ExternalInput").ap()
    B1 = nc.dram_tensor("b1", [128, 32], F32, kind="ExternalInput").ap()
    B2 = nc.dram_tensor("b2", [128, 8], F32, kind="ExternalInput").ap()
    BV = nc.dram_tensor("bv", [1, D], F32, kind="ExternalInput").ap()
    ALHI = nc.dram_tensor("alhi", [NH, S], BF16, kind="ExternalInput").ap()
    ALLO = nc.dram_tensor("allo", [NH, S], BF16, kind="ExternalInput").ap()
    MASK = nc.dram_tensor("maskv", [128, 16384], F16, kind="ExternalInput").ap()
    EYE = nc.dram_tensor("eye", [128, 128], F32, kind="ExternalInput").ap()
    OUT = nc.dram_tensor("out", [TOWN, D], F32, kind="ExternalOutput").ap()
    KTS = nc.dram_tensor("kts", [D, S], BF16, kind="Internal").ap()
    QS = nc.dram_tensor("qs", [D, TOWN], BF16, kind="Internal").ap()
    VS = nc.dram_tensor("vs", [S, D], BF16, kind="Internal").ap()

    dbg_out = {}
    for name, shape, dt in (
        ("xn", [S, D], F32), ("ktd", [D, S], F32), ("vd", [S, D], F32),
        ("qtd", [D, TOWN], F32), ("xnqd", [D, TOWN], F32),
        ("oacc", [D, TOWN], F32), ("std", [D, TOWN], F32),
        ("hqd", [FF, 512], F32), ("expd", [S, NQ], F32),
        ("yd", [128, 512], F32), ("gd", [128, 512], F32),
        ("od", [128, 128], F32),
    ):
        if name in dbg:
            dbg_out[name] = nc.dram_tensor("dbg_" + name, shape, dt,
                                           kind="ExternalOutput").ap()

    sch = Sched()
    B = Buf

    eye = nc.alloc_sbuf_tensor("eye_sb", [128, 128], F32)
    ones_r = nc.alloc_sbuf_tensor("ones_r", [128, 128], F32)
    bq_sb = nc.alloc_sbuf_tensor("bq_sb", [128, 8], F32)
    bk_sb = nc.alloc_sbuf_tensor("bk_sb", [128, 8], F32)
    bo_sb = nc.alloc_sbuf_tensor("bo_sb", [128, 8], F32)
    b1_sb = nc.alloc_sbuf_tensor("b1_sb", [128, 32], F32)
    b2_sb = nc.alloc_sbuf_tensor("b2_sb", [128, 8], F32)
    b1_f8 = nc.alloc_sbuf_tensor("b1_f8", [128, 32], F8)
    b2_f8 = nc.alloc_sbuf_tensor("b2_f8", [128, 8], F8)
    b1q = nc.alloc_sbuf_tensor("b1q", [128, 32], F32)
    b2q = nc.alloc_sbuf_tensor("b2q", [128, 8], F32)
    xnq = nc.alloc_sbuf_tensor("xnq", [128, 8, TOWN], F8)
    sT = nc.alloc_sbuf_tensor("sT", [128, 8, TOWN], BF16)

    b_eye, b_ones = B("eye"), B("ones")
    b_bq, b_bk, b_bo = B("bq"), B("bk"), B("bo")
    b_b1s, b_b2s, b_b18, b_b28, b_b1q, b_b2q = (B(x) for x in
        ("b1s", "b2s", "b18", "b28", "b1q", "b2q"))
    b_bv, b_bvrep = B("bv"), B("bvrep")
    b_xnq, b_sT = B("xnq"), B("sT")
    b_wst = [B("wst0"), B("wst1")]
    b_kts, b_vs = B("kts"), B("vs")

    if "nosetup" in stages:
        pass
    else:
        sch.op("dsync", lambda e: e.dma_start(out=eye[:, :], in_=EYE), writes=[b_eye])
        sch.op("dsync", lambda e: e.dma_start(out=bq_sb[:, :], in_=BQ), writes=[b_bq])
        sch.op("dsync", lambda e: e.dma_start(out=bk_sb[:, :], in_=BK), writes=[b_bk])
        sch.op("dsync", lambda e: e.dma_start(out=bo_sb[:, :], in_=BO), writes=[b_bo])
        sch.op("dsync", lambda e: e.dma_start(out=b1_sb[:, :], in_=B1), writes=[b_b1s])
        sch.op("dsync", lambda e: e.dma_start(out=b2_sb[:, :], in_=B2), writes=[b_b2s])
        sch.op("dsync", lambda e: e.dma_start(out=bv_sb[:, :], in_=BV), writes=[b_bv])
        sch.op("vector", lambda e: e.memset(ones_r[:, :], 1.0), writes=[b_ones])
        sch.op("vector", lambda e: e.tensor_copy(b1_f8[:, :], b1_sb[:, :]),
               reads=[b_b1s], writes=[b_b18])
        sch.op("vector", lambda e: e.tensor_copy(b1q[:, :], b1_f8[:, :]),
               reads=[b_b18], writes=[b_b1q])
        sch.op("vector", lambda e: e.tensor_copy(b2_f8[:, :], b2_sb[:, :]),
               reads=[b_b2s], writes=[b_b28])
        sch.op("vector", lambda e: e.tensor_copy(b2q[:, :], b2_f8[:, :]),
               reads=[b_b28], writes=[b_b2q])

    sb_base1 = nc.sbuf_base
    ps_base1 = nc.psum_base
    bv_sb = nc.alloc_sbuf_tensor("bv_sb", [1, D], F32)
    bvrep = nc.alloc_sbuf_tensor("bvrep", [128, D], F32)
    wstage = mb("wstage", 2, [128, D], F32)
    xin = mb("xin", 2, [128, D], F32)
    sq_t = mb("sq_t", 2, [128, D], F32)
    st = mb("st", 2, [128, 16], F32)
    xn_tm = mb("xn_tm", 2, [128, D], F32)
    xnT = mb("xnT", 2, [128, 8, 512], BF16)
    wq_bf = nc.alloc_sbuf_tensor("wq_bf", [128, 8, D], BF16)
    wk_bf = nc.alloc_sbuf_tensor("wk_bf", [128, 8, D], BF16)
    wv_bf = nc.alloc_sbuf_tensor("wv_bf", [128, 8, D], BF16)
    kt_sp = mb("kt_sp", 2, [128, 8, 512], BF16)
    q_sp = mb("q_sp", 2, [128, 8, 512], BF16)
    v_sp = mb("v_sp", 2, [128, D], BF16)

    trA = nc.alloc_psum_tensor("trA", [128, 512], F32)
    trB = nc.alloc_psum_tensor("trB", [128, 512], F32)
    pjK = [nc.alloc_psum_tensor(f"pjK{i}", [128, 512], F32) for i in range(2)]
    pjQ = [nc.alloc_psum_tensor(f"pjQ{i}", [128, 512], F32) for i in range(2)]
    pjV = [nc.alloc_psum_tensor(f"pjV{i}", [128, 512], F32) for i in range(2)]

    b_xin = [B("xin0"), B("xin1")]
    b_sq = [B("sq0"), B("sq1")]
    b_st = [B("st0"), B("st1")]
    b_xntm = [B("xntm0"), B("xntm1")]
    b_xnT = [B("xnT0"), B("xnT1")]
    b_wq, b_wk, b_wv = B("wq"), B("wk"), B("wv")
    b_ktsp = [B("ktsp0"), B("ktsp1")]
    b_qsp = [B("qsp0"), B("qsp1")]
    b_vsp = [B("vsp0"), B("vsp1")]
    b_qs = B("qs")
    b_trA, b_trB = B("trA"), B("trB")
    b_pjK = [B("pjK0"), B("pjK1")]
    b_pjQ = [B("pjQ0"), B("pjQ1")]
    b_pjV = [B("pjV0"), B("pjV1")]

    for half in (() if "nobvrep" in stages else range(2)):
        sch.op("tensor", lambda e, h=half: e.matmul(
            pjV[h][:, :], ones_r[0:1, 0:128], bv_sb[0:1, h * 512:(h + 1) * 512],
            start=True, stop=True),
            reads=[b_ones, b_bv], writes=[b_pjV[half]])
        sch.op("vector", lambda e, h=half: e.tensor_copy(
            bvrep[:, h * 512:(h + 1) * 512], pjV[h][:, :]),
            reads=[b_pjV[half]], writes=[b_bvrep])

    for wdst, bw, src in ((wk_bf, b_wk, WKT), (wv_bf, b_wv, WVT), (wq_bf, b_wq, WQT)):
        for ci in range(8):
            p = ci % 2
            sch.op("dscalar", lambda e, ci=ci, p=p, src=src: e.dma_start(
                out=wstage[p][:, :], in_=src[ci * 128:(ci + 1) * 128, :]),
                writes=[b_wst[p]])
            sch.op("vector", lambda e, ci=ci, p=p, wdst=wdst: e.tensor_copy(
                wdst[:, ci, :], wstage[p][:, :]),
                reads=[b_wst[p]], writes=[bw])

    INV_D = 1.0 / D
    for sc in (range(8) if 1 in stages else ()):
        own = sc < 4
        sp = sc % 2
        def ln_steps(tb):
            t0 = sc * 512 + tb * 128
            p = tb % 2
            bx, bs, bst, bxn = b_xin[p], b_sq[p], b_st[p], b_xntm[p]
            steps = []

            def st_(stream, fn, reads=(), writes=()):
                steps.append((stream, fn, reads, writes))

            st_("dsync", lambda e, t0=t0, p=p: e.dma_start(
                out=xin[p][:, :], in_=X[t0:t0 + 128, :]), (), [bx])
            st_("vector", lambda e, p=p: e.reduce_sum(
                st[p][:, 0:1], xin[p][:, :], axis=AX), [bx], [bst])
            st_("vector", lambda e, p=p: e.tensor_mul(
                sq_t[p][:, :], xin[p][:, :], xin[p][:, :]), [bx], [bs])
            st_("vector", lambda e, p=p: e.reduce_sum(
                st[p][:, 1:2], sq_t[p][:, :], axis=AX), [bs], [bst])
            st_("vector", lambda e, p=p: e.tensor_scalar(
                st[p][:, 2:3], st[p][:, 0:1], INV_D, None, ALU.mult),
                [bst], [bst])
            st_("vector", lambda e, p=p: e.tensor_mul(
                st[p][:, 3:4], st[p][:, 2:3], st[p][:, 2:3]), [bst], [bst])
            st_("vector", lambda e, p=p: e.tensor_scalar(
                st[p][:, 4:5], st[p][:, 1:2], INV_D, 1e-5, ALU.mult, ALU.add),
                [bst], [bst])
            st_("vector", lambda e, p=p: e.tensor_sub(
                st[p][:, 5:6], st[p][:, 4:5], st[p][:, 3:4]), [bst], [bst])
            st_("scalar", lambda e, p=p: e.activation(
                st[p][:, 6:7], st[p][:, 5:6], AF.Sqrt), [bst], [bst])
            st_("vector", lambda e, p=p: e.reciprocal(
                st[p][:, 7:8], st[p][:, 6:7]), [bst], [bst])
            st_("vector", lambda e, p=p: e.tensor_mul(
                st[p][:, 8:9], st[p][:, 7:8], st[p][:, 7:8]), [bst], [bst])
            st_("vector", lambda e, p=p: e.tensor_mul(
                st[p][:, 9:10], st[p][:, 8:9], st[p][:, 5:6]), [bst], [bst])
            st_("vector", lambda e, p=p: e.tensor_scalar(
                st[p][:, 10:11], st[p][:, 9:10], -0.5, 1.5, ALU.mult, ALU.add),
                [bst], [bst])
            st_("vector", lambda e, p=p: e.tensor_mul(
                st[p][:, 11:12], st[p][:, 7:8], st[p][:, 10:11]), [bst], [bst])
            st_("vector", lambda e, p=p: e.tensor_mul(
                st[p][:, 12:13], st[p][:, 2:3], st[p][:, 11:12]), [bst], [bst])
            st_("vector", lambda e, p=p: e.tensor_scalar(
                st[p][:, 13:14], st[p][:, 12:13], -1.0, None, ALU.mult),
                [bst], [bst])
            st_("scalar", lambda e, p=p: e.activation(
                xn_tm[p][:, :], xin[p][:, :], AF.Identity,
                bias=st[p][:, 13:14], scale=st[p][:, 11:12]), [bx, bst], [bxn])
            return steps

        def emit_tr_evac(tb):
            t0 = sc * 512 + tb * 128
            p = tb % 2
            bxn = b_xntm[p]
            if "xn" in dbg:
                sch.op("dgp", lambda e, t0=t0, p=p: e.dma_start(
                    out=dbg_out["xn"][t0:t0 + 128, :], in_=xn_tm[p][:, :]),
                    reads=[bxn])
            for ci in range(8):
                tr, btr = (trA, b_trA) if ci < 4 else (trB, b_trB)
                sch.op("tensor", lambda e, p=p, ci=ci, tr=tr: e.transpose(
                    tr[:, (ci % 4) * 128:(ci % 4) * 128 + 128],
                    xn_tm[p][:, ci * 128:(ci + 1) * 128], eye[:, :]),
                    reads=[bxn, b_eye], writes=[btr])
            for gi, (tr, btr) in enumerate(((trA, b_trA), (trB, b_trB))):
                src3 = tr[:, :].rearrange("p (c j) -> p c j", j=128)
                sch.op("scalar", lambda e, sp=sp, gi=gi, tb=tb, src3=src3:
                       e.copy(xnT[sp][:, gi * 4:gi * 4 + 4,
                                      tb * 128:tb * 128 + 128], src3),
                       reads=[btr], writes=[b_xnT[sp]])
                if own:
                    sch.op("scalar", lambda e, sc=sc, gi=gi, tb=tb, src3=src3:
                           e.copy(
                               xnq[:, gi * 4:gi * 4 + 4,
                                   sc * 512 + tb * 128:sc * 512 + tb * 128 + 128],
                               src3),
                           reads=[btr], writes=[b_xnq])

        for pair in range(2):
            sa = ln_steps(pair * 2)
            sb = ln_steps(pair * 2 + 1)
            for i in range(max(len(sa), len(sb))):
                if i < len(sa):
                    stream, fn, rr, ww = sa[i]
                    sch.op(stream, fn, reads=rr, writes=ww)
                if i < len(sb):
                    stream, fn, rr, ww = sb[i]
                    sch.op(stream, fn, reads=rr, writes=ww)
            emit_tr_evac(pair * 2)
            emit_tr_evac(pair * 2 + 1)
        for co in range(8):
            pj, bpj = pjK[co % 2], b_pjK[co % 2]
            for ci in range(8):
                sch.op("tensor", lambda e, sp=sp, ci=ci, co=co, pj=pj: e.matmul(
                    pj[:, :], wk_bf[:, ci, co * 128:co * 128 + 128],
                    xnT[sp][:, ci, :], start=(ci == 0), stop=(ci == 7)),
                    reads=[b_wk, b_xnT[sp]], writes=[bpj])
            sch.op("scalar", lambda e, sp=sp, co=co, pj=pj: e.activation(
                kt_sp[sp][:, co, :], pj[:, :], AF.Identity,
                bias=bk_sb[:, co:co + 1], scale=1.0),
                reads=[bpj, b_bk], writes=[b_ktsp[sp]])
        for co in range(8):
            sch.op("dgp", lambda e, sp=sp, sc=sc, co=co: e.dma_start(
                out=KTS[co * 128:(co + 1) * 128, sc * 512:(sc + 1) * 512],
                in_=kt_sp[sp][:, co, :]),
                reads=[b_ktsp[sp]], writes=[b_kts])
        if own:
            for co in range(8):
                pj, bpj = pjQ[co % 2], b_pjQ[co % 2]
                for ci in range(8):
                    sch.op("tensor", lambda e, sp=sp, ci=ci, co=co, pj=pj: e.matmul(
                        pj[:, :], wq_bf[:, ci, co * 128:co * 128 + 128],
                        xnT[sp][:, ci, :], start=(ci == 0), stop=(ci == 7)),
                        reads=[b_wq, b_xnT[sp]], writes=[bpj])
                sch.op("scalar", lambda e, sp=sp, co=co, pj=pj, bpj=bpj: e.activation(
                    q_sp[sp][:, co, :], pj[:, :], AF.Identity,
                    bias=bq_sb[:, co:co + 1], scale=1.0),
                    reads=[bpj, b_bq], writes=[b_qsp[sp]])
            for co in range(8):
                sch.op("dgp", lambda e, sp=sp, sc=sc, co=co: e.dma_start(
                    out=QS[co * 128:(co + 1) * 128, sc * 512:(sc + 1) * 512],
                    in_=q_sp[sp][:, co, :]),
                    reads=[b_qsp[sp]], writes=[b_qs])
        for tb in range(4):
            vp = tb % 2
            for eh in range(2):
                pj, bpj = pjV[eh], b_pjV[eh]
                for ci in range(8):
                    sch.op("tensor", lambda e, sp=sp, tb=tb, eh=eh, ci=ci, pj=pj:
                           e.matmul(pj[:, :],
                                    xnT[sp][:, ci, tb * 128:tb * 128 + 128],
                                    wv_bf[:, ci, eh * 512:eh * 512 + 512],
                                    start=(ci == 0), stop=(ci == 7)),
                           reads=[b_wv, b_xnT[sp]], writes=[bpj])
                sch.op("vector", lambda e, vp=vp, eh=eh, pj=pj, bpj=bpj:
                       e.tensor_add(v_sp[vp][:, eh * 512:eh * 512 + 512],
                                    pj[:, :], bvrep[:, eh * 512:eh * 512 + 512]),
                       reads=[bpj, b_bvrep], writes=[b_vsp[vp]])
            sch.op("dgp", lambda e, sc=sc, tb=tb, vp=vp: e.dma_start(
                out=VS[sc * 512 + tb * 128:sc * 512 + tb * 128 + 128, :],
                in_=v_sp[vp][:, :]),
                reads=[b_vsp[vp]], writes=[b_vs])

    if "xnqd" in dbg:
        for co in range(8):
            for tq in range(4):
                sch.op("vector", lambda e, co=co, tq=tq: e.tensor_copy(
                    sq_t[0][:, 0:512],
                    xnq[:, co, tq * 512:(tq + 1) * 512]),
                    reads=[b_xnq], writes=[b_sq[0]])
                sch.op("dgp", lambda e, co=co, tq=tq: e.dma_start(
                    out=dbg_out["xnqd"][co * 128:(co + 1) * 128, tq * 512:(tq + 1) * 512],
                    in_=sq_t[0][:, 0:512]),
                    reads=[b_sq[0]])

    sch.barrier()
    nc.sbuf_base = sb_base1
    nc.psum_base = ps_base1

    o_acc = nc.alloc_sbuf_tensor("o_acc", [128, 8, TOWN], BF16)
    b_oacc = B("oacc")
    sb_base2i = nc.sbuf_base
    ps_base2i = nc.psum_base
    kt_buf = mb("kt_buf", 2, [128, S], BF16)
    v_buf = mb("v_buf", 2, [128, 4224], BF16)
    q_st = mb("q_st", 2, [128, TOWN], BF16)
    expS = mb("expS", 2, [128, 32, NQ], BF16)
    rcp_t = nc.alloc_sbuf_tensor("rcp_t", [128, NQ], F32)
    bc_sb = nc.alloc_sbuf_tensor("bc_sb", [128, NQ], F32)

    sc_ps = [nc.alloc_psum_tensor(f"sc{i}", [128, 1024], F32) for i in range(2)]
    pv_ps = [nc.alloc_psum_tensor(f"pv{i}", [128, NQ], F32) for i in range(2)]
    bc_ps = nc.alloc_psum_tensor("bc", [128, NQ], F32)

    b_kt = [B(f"kt{i}") for i in range(2)]
    b_vb = [B(f"vb{i}") for i in range(2)]
    b_qst = [B("qst0"), B("qst1")]
    b_expg = [[B(f"expg{e}_{g}") for g in range(16)] for e in range(2)]
    b_rcp = B("rcp")
    b_bcs = B("bcs")
    b_sc = [B(f"sc{i}") for i in range(2)]
    b_pv = [B("pv0"), B("pv1")]
    b_bc = B("bc")

    for qb_ in range(2):
        sch.op("vector", lambda e, qb_=qb_: e.memset(q_st[qb_][64:128, :], 0.0),
               writes=[b_qst[qb_]])
        sch.op("vector", lambda e, qb_=qb_: e.memset(q_st[qb_][64:66, :], 1.0),
               writes=[b_qst[qb_]])
    for vb in range(2):
        sch.op("vector", lambda e, vb=vb: e.memset(kt_buf[vb][64:128, :], 0.0),
               writes=[b_kt[vb]])
        sch.op("vector", lambda e, vb=vb: e.memset(v_buf[vb][:, 0:4160:130], 1.0),
               writes=[b_vb[vb]])
        sch.op("vector", lambda e, vb=vb: e.memset(
            v_buf[vb][:, 128:4160:130], 1.0),
            writes=[b_vb[vb]])

    def emit_head_dmas(hh):
        vb = hh % 2
        sch.op("dscalar", lambda e, hh=hh, vb=vb: e.dma_start(
            out=q_st[vb][0:64, :], in_=QS[hh * 64:(hh + 1) * 64, :]),
            reads=[b_qs], writes=[b_qst[vb]])
        sch.op("dsync", lambda e, hh=hh, vb=vb: e.dma_start(
            out=kt_buf[vb][0:64, :], in_=KTS[hh * 64:(hh + 1) * 64, :]),
            reads=[b_kts], writes=[b_kt[vb]])
        sch.op("dsync", lambda e, hh=hh, vb=vb: e.dma_start(
            out=kt_buf[vb][64:65, :], in_=ALHI[hh:hh + 1, :]), writes=[b_kt[vb]])
        sch.op("dsync", lambda e, hh=hh, vb=vb: e.dma_start(
            out=kt_buf[vb][65:66, :], in_=ALLO[hh:hh + 1, :]), writes=[b_kt[vb]])
        for kq in range(4):
            sch.op("dgp" if kq % 2 else "dsync",
                   lambda e, hh=hh, vb=vb, kq=kq: e.dma_start(
                       out=v_buf[vb][:, 64 + 1040 * kq:64 + 1040 * (kq + 1)]
                       .rearrange("p (k j) -> p k j", j=130)[:, :, 0:64],
                       in_=VS[kq * 1024:(kq + 1) * 1024, hh * 64:(hh + 1) * 64]
                       .rearrange("(kc p) j -> p kc j", p=128)),
                   reads=[b_vs], writes=[b_vb[vb]])

    def emit_pv_group(pend, kc0, n):
        hh, qs, pv, bpv, se = pend
        vb = hh % 2
        odd = hh % 2 == 1
        for kc in range(kc0, kc0 + n):
            if odd:
                lhs = v_buf[vb][:, 130 * kc:130 * kc + 128]
                o = pv[0:128, :]
            else:
                lhs = v_buf[vb][:, 130 * kc + 64:130 * kc + 129]
                o = pv[0:65, :]
            sch.op("tensor", lambda e, lhs=lhs, o=o, kc=kc, se=se: e.matmul(
                o, lhs, expS[se][:, kc, :], start=(kc == 0), stop=(kc == 31)),
                reads=[b_vb[vb], b_expg[se][kc // 2]], writes=[bpv])

    def emit_norm(pend):
        hh, qs, pv, bpv, se = pend
        odd = hh % 2 == 1
        dr = 0 if odd else 64
        orow = 64 if odd else 0
        sch.op("vector", lambda e, dr=dr, pv=pv: e.tensor_copy(
            rcp_t[dr:dr + 1, :], pv[dr:dr + 1, :]),
            reads=[bpv], writes=[b_rcp])
        sch.op("tensor", lambda e, dr=dr: e.matmul(
            bc_ps[:, :], ones_r[dr:dr + 1, 0:128], rcp_t[dr:dr + 1, :],
            start=True, stop=True),
            reads=[b_ones, b_rcp], writes=[b_bc])
        sch.op("vector", lambda e, orow=orow: e.reciprocal(
            bc_sb[orow:orow + 64, :], bc_ps[orow:orow + 64, :]),
            reads=[b_bc], writes=[b_bcs])
        sch.op("vector", lambda e, hh=hh, qs=qs, orow=orow, pv=pv: e.tensor_mul(
            o_acc[orow:orow + 64, hh // 2, qs:qs + NQ],
            pv[orow:orow + 64, :], bc_sb[orow:orow + 64, :]),
            reads=[bpv, b_bcs], writes=[b_oacc])

    pending = None
    norm_pend = None
    step2 = 0
    for hh in (range(NH) if 2 in stages else ()):
        vb = hh % 2
        if hh == 0:
            emit_head_dmas(0)
        for qtile in range(TOWN // NQ):
            qs = qtile * NQ
            pv, bpv = pv_ps[step2 % 2], b_pv[step2 % 2]
            se = step2 % 2
            step2 += 1
            for g in range(16):
                sp2, bsp = sc_ps[g % 2], b_sc[g % 2]
                for j in range(2):
                    kc = 2 * g + j
                    sch.op("tensor", lambda e, vb=vb, kc=kc, j=j, qs=qs, sp2=sp2:
                           e.matmul(sp2[:, j * 512:j * 512 + 512],
                                    kt_buf[vb][0:128, kc * 128:kc * 128 + 128],
                                    q_st[vb][0:128, qs:qs + NQ],
                                    start=True, stop=True),
                           reads=[b_kt[vb], b_qst[vb]], writes=[bsp])
                if g == 0 and norm_pend is not None:
                    emit_norm(norm_pend)
                    norm_pend = None
                if pending is not None:
                    emit_pv_group(pending, kc0=2 * g, n=2)
                sch.op("scalar", lambda e, g=g, sp2=sp2, se=se: e.activation(
                    expS[se][:, 2 * g:2 * g + 2, :], sp2[:, :], AF.Exp,
                    bias=0.0, scale=0.125),
                    reads=[bsp], writes=[b_expg[se][g]])
            norm_pend = pending
            pending = (hh, qs, pv, bpv, se)
            if qtile == 0 and hh + 1 < NH:
                emit_head_dmas(hh + 1)
    if norm_pend is not None:
        emit_norm(norm_pend)
    if pending is not None:
        emit_pv_group(pending, kc0=0, n=32)
        emit_norm(pending)

    if "oacc" in dbg:
        for co in range(8):
            for tq in range(4):
                sch.op("vector", lambda e, co=co, tq=tq: e.tensor_copy(
                    rcp_t[:, :], o_acc[:, co, tq * 512:(tq + 1) * 512]),
                    reads=[b_oacc], writes=[b_rcp])
                sch.op("dgp", lambda e, co=co, tq=tq: e.dma_start(
                    out=dbg_out["oacc"][co * 128:(co + 1) * 128,
                                        tq * 512:(tq + 1) * 512],
                    in_=rcp_t[:, :]), reads=[b_rcp])

    sch.barrier()
    nc.sbuf_base = sb_base2i
    nc.psum_base = ps_base2i
    wo_bf = nc.alloc_sbuf_tensor("wo_bf", [128, 8, D], BF16)
    wstage25 = mb("wstage25", 2, [128, D], F32)
    b_wst25 = [B("wst25a"), B("wst25b")]
    b_wo = B("wo")
    pr_ps = [nc.alloc_psum_tensor(f"pr{i}", [128, 512], F32) for i in range(2)]
    b_pr = [B("pr0"), B("pr1")]
    for ci in range(8):
        p = ci % 2
        sch.op("dscalar", lambda e, ci=ci: e.dma_start(
            out=wstage25[ci % 2][:, :], in_=WOT[ci * 128:(ci + 1) * 128, :]),
            writes=[b_wst25[p]])
        sch.op("vector", lambda e, ci=ci, p=p: e.tensor_copy(
            wo_bf[:, ci, :], wstage25[p][:, :]), reads=[b_wst25[p]], writes=[b_wo])
    for co in (range(8) if 25 in stages else ()):
        for ts in range(4):
            pj, bpj = pr_ps[ts % 2], b_pr[ts % 2]
            for ci in range(8):
                sch.op("tensor", lambda e, ci=ci, co=co, ts=ts, pj=pj: e.matmul(
                    pj[:, :], wo_bf[:, ci, co * 128:co * 128 + 128],
                    o_acc[:, ci, ts * 512:(ts + 1) * 512],
                    start=(ci == 0), stop=(ci == 7)),
                    reads=[b_wo, b_oacc], writes=[bpj])
            sch.op("vector", lambda e, co=co, ts=ts, pj=pj, bpj=bpj: e.tensor_scalar(
                sT[:, co, ts * 512:(ts + 1) * 512], pj[:, :],
                bo_sb[:, co:co + 1], None, ALU.add),
                reads=[bpj, b_bo], writes=[b_sT])

    if "std" in dbg:
        dtmp = nc.alloc_sbuf_tensor("dtmp", [128, 512], F32)
        b_dt = B("dtmp")
        for co in range(8):
            for tq in range(4):
                sch.op("vector", lambda e, co=co, tq=tq: e.tensor_copy(
                    dtmp[:, :], sT[:, co, tq * 512:(tq + 1) * 512]),
                    reads=[b_sT], writes=[b_dt])
                sch.op("dgp", lambda e, co=co, tq=tq: e.dma_start(
                    out=dbg_out["std"][co * 128:(co + 1) * 128,
                                       tq * 512:(tq + 1) * 512],
                    in_=dtmp[:, :]), reads=[b_dt])

    sch.barrier()
    nc.sbuf_base = sb_base1
    nc.psum_base = ps_base1

    hq = nc.alloc_sbuf_tensor("hq", [128, 32, 1024], F8)
    W1PRE = 2
    out_st = nc.alloc_sbuf_tensor("out_st", [128, 8, D], F32)
    xin3 = nc.alloc_sbuf_tensor("xin3", [128, 8, D], F32)
    w1st = mb("w1st", 2, [128, 8, 128], F32)
    w1q = mb("w1q", 2, [128, 8, 128], F8)
    w2st = mb("w2st", 4, [128, 8, 128], F32)
    w2q = mb("w2q", 8, [128, 8, 128], F8)
    h16 = mb("h16", 4, [128, 512], F16)
    y16 = mb("y16", 4, [128, 512], F16)
    dp_t = mb("dp_t", 4, [128, 512], F32)
    sg_t = mb("sg_t", 4, [128, 512], F32)
    mk_t = mb("mk_t", 2, [128, 1024], F16)

    f1_ps = [nc.alloc_psum_tensor(f"f1{i}", [128, 512], F32) for i in range(3)]
    f2_ps = [nc.alloc_psum_tensor(f"f2{i}", [128, 512], F32) for i in range(2)]
    tr3_ps = [nc.alloc_psum_tensor(f"tr3{i}", [128, 512], F32) for i in range(2)]

    b_hq = B("hq")
    b_outst = B("outst")
    b_x3 = B("x3")
    b_w1s = [B("w1s0"), B("w1s1")]
    b_w1q = [B("w1q0"), B("w1q1")]
    b_w2s = [B(f"w2s{i}") for i in range(4)]
    b_w2q = [B(f"w2q{i}") for i in range(8)]
    b_h16 = [B(f"h16{i}") for i in range(4)]
    b_y16 = [B(f"y16{i}") for i in range(4)]
    b_dp = [B(f"dp{i}") for i in range(4)]
    b_sg = [B(f"sg{i}") for i in range(4)]
    b_mk = [B("mk0"), B("mk1")]
    b_f1 = [B(f"f1{i}") for i in range(3)]
    b_f2 = [B("f20"), B("f21")]
    b_tr3 = [B("tr30"), B("tr31")]

    if 3 in stages:
        for fo in range(W1PRE):
            p = fo % 2
            sch.op("dsync" if fo % 2 else "dgp",
                   lambda e, fo=fo, p=p: e.dma_start(
                       out=w1st[p][:, :, :],
                       in_=W1T[:, fo * 128:(fo + 1) * 128]
                       .rearrange("(c p) j -> p c j", p=128)),
                   writes=[b_w1s[p]])
            sch.op("vector", lambda e, p=p: e.tensor_copy(
                w1q[p][:, :, :], w1st[p][:, :, :]),
                reads=[b_w1s[p]], writes=[b_w1q[p]])

    for H in (range(2) if 3 in stages else ()):
        hb = H * 1024
        for tb in (() if "nox3" in stages else range(8)):
            sch.op("dscalar", lambda e, hb=hb, tb=tb: e.dma_start(
                out=xin3[:, tb, :], in_=X[hb + tb * 128:hb + tb * 128 + 128, :]),
                writes=[b_x3])
        for fo in range(32):
            p = fo % 2
            if H == 0 and fo < W1PRE:
                pass
            else:
                if "now1dma" not in stages:
                    sch.op("dsync" if fo % 2 else "dgp",
                           lambda e, fo=fo, p=p: e.dma_start(
                               out=w1st[p][:, :, :],
                               in_=W1T[:, fo * 128:(fo + 1) * 128]
                               .rearrange("(c p) j -> p c j", p=128)),
                           writes=[b_w1s[p]])
                if "now1cast" not in stages:
                    sch.op("vector", lambda e, p=p: e.tensor_copy(
                        w1q[p][:, :, :], w1st[p][:, :, :]),
                        reads=[b_w1s[p]], writes=[b_w1q[p]])
            for ts2 in range(2):
                gidx = fo * 2 + ts2
                pj, bpj = f1_ps[gidx % 3], b_f1[gidx % 3]
                hb16, bh16 = h16[gidx % 4], b_h16[gidx % 4]
                for ci in range(8):
                    sch.op("tensor", lambda e, p=p, ci=ci, H=H, ts2=ts2, pj=pj:
                           e.matmul(pj[:, :], w1q[p][:, ci, :],
                                    xnq[:, ci, H * 1024 + ts2 * 512:
                                        H * 1024 + ts2 * 512 + 512],
                                    start=(ci == 0), stop=(ci == 7)),
                           reads=[b_w1q[p], b_xnq], writes=[bpj])
                if "noevac" not in stages:
                    sch.op("vector", lambda e, fo=fo, hb16=hb16, pj=pj, bpj=bpj:
                           e.tensor_scalar(hb16[:, :], pj[:, :],
                                           b1q[:, fo:fo + 1], 0.0,
                                           ALU.add, ALU.max),
                           reads=[bpj, b_b1q], writes=[bh16])
                    sch.op("vector", lambda e, fo=fo, ts2=ts2, hb16=hb16, bh16=bh16:
                           e.tensor_copy(
                               hq[:, fo, ts2 * 512:ts2 * 512 + 512], hb16[:, :]),
                           reads=[bh16], writes=[b_hq])
        if "hqd" in dbg and H == 0:
            for fo in range(32):
                sch.op("vector", lambda e, fo=fo: e.tensor_copy(
                    dp_t[0][:, :], hq[:, fo, 0:512]), reads=[b_hq],
                    writes=[b_dp[0]])
                sch.op("dgp", lambda e, fo=fo: e.dma_start(
                    out=dbg_out["hqd"][fo * 128:(fo + 1) * 128, :],
                    in_=dp_t[0][:, :]), reads=[b_dp[0]])
        def emit_tr_adds(p3):
            pH, pdo, pts2, slot = p3
            for qb in range(4):
                sch.op("tensor", lambda e, pts2=pts2, qb=qb, slot=slot: e.transpose(
                    tr3_ps[pts2][:, qb * 128:qb * 128 + 128],
                    sg_t[slot][:, qb * 128:qb * 128 + 128], eye[:, :]),
                    reads=[b_sg[slot], b_eye], writes=[b_tr3[pts2]])
            for qb in range(4):
                tb = pts2 * 4 + qb
                sch.op("vector", lambda e, pdo=pdo, pts2=pts2, qb=qb, tb=tb:
                       e.tensor_add(out_st[:, tb, pdo * 128:pdo * 128 + 128],
                                    tr3_ps[pts2][:, qb * 128:qb * 128 + 128],
                                    xin3[:, tb, pdo * 128:pdo * 128 + 128]),
                       reads=[b_tr3[pts2], b_x3], writes=[b_outst])

        pend3 = None
        for do in (() if "noffn2" in stages else range(8)):
            for fh in range(4):
                p = (do * 4 + fh) % 4
                q4 = (do * 4 + fh) % 8
                sch.op("dgp" if fh % 2 else "dsync",
                       lambda e, do=do, fh=fh, p=p: e.dma_start(
                           out=w2st[p][:, :, :],
                           in_=W2T[fh * 1024:(fh + 1) * 1024,
                                   do * 128:(do + 1) * 128]
                           .rearrange("(c p) j -> p c j", p=128)),
                       writes=[b_w2s[p]])
                sch.op("vector", lambda e, p=p, q4=q4: e.tensor_copy(
                    w2q[q4][:, :, :], w2st[p][:, :, :]),
                    reads=[b_w2s[p]], writes=[b_w2q[q4]])
            sch.op("dsync", lambda e, H=H, do=do: e.dma_start(
                out=mk_t[do % 2][:, :],
                in_=MASK[:, H * 8192 + do * 1024:H * 8192 + (do + 1) * 1024]),
                writes=[b_mk[do % 2]])
            for ts2 in range(2):
                if pend3 is not None:
                    emit_tr_adds(pend3)
                slot = (do * 2 + ts2) % 4
                pj, bpj = f2_ps[ts2], b_f2[ts2]
                for fo in range(32):
                    q4 = (do * 4 + fo // 8) % 8
                    sch.op("tensor", lambda e, fo=fo, ts2=ts2, q4=q4, pj=pj:
                           e.matmul(pj[:, :], w2q[q4][:, fo % 8, :],
                                    hq[:, fo, ts2 * 512:ts2 * 512 + 512],
                                    start=(fo == 0), stop=(fo == 31)),
                           reads=[b_w2q[q4], b_hq], writes=[bpj])
                sch.op("vector", lambda e, do=do, slot=slot, pj=pj, bpj=bpj:
                       e.tensor_scalar(y16[slot][:, :], pj[:, :],
                                       b2q[:, do:do + 1], None, ALU.add),
                       reads=[bpj, b_b2q], writes=[b_y16[slot]])
                if "yd" in dbg and H == 0 and do == 0 and ts2 == 0:
                    sch.op("vector", lambda e, slot=slot: e.tensor_copy(
                        sg_t[3][:, :], y16[slot][:, :]),
                        reads=[b_y16[slot]], writes=[b_sg[3]])
                    sch.op("dgp", lambda e: e.dma_start(
                        out=dbg_out["yd"], in_=sg_t[3][:, :]), reads=[b_sg[3]])
                sch.op("vector", lambda e, do=do, ts2=ts2, slot=slot: e.tensor_mul(
                    dp_t[slot][:, :], y16[slot][:, :],
                    mk_t[do % 2][:, ts2 * 512:ts2 * 512 + 512]),
                    reads=[b_y16[slot], b_mk[do % 2]], writes=[b_dp[slot]])
                sch.op("scalar", lambda e, slot=slot: e.activation(
                    dp_t[slot][:, :], dp_t[slot][:, :], AF.Silu),
                    reads=[b_dp[slot]], writes=[b_dp[slot]])
                sch.op("vector", lambda e, do=do, H=H, ts2=ts2, slot=slot: e.tensor_mul(
                    sg_t[slot][:, :], dp_t[slot][:, :],
                    sT[:, do, H * 1024 + ts2 * 512:H * 1024 + ts2 * 512 + 512]),
                    reads=[b_dp[slot], b_sT], writes=[b_sg[slot]])
                if "gd" in dbg and H == 0 and do == 0 and ts2 == 0:
                    sch.op("dgp", lambda e, slot=slot: e.dma_start(
                        out=dbg_out["gd"], in_=sg_t[slot][:, :]), reads=[b_sg[slot]])
                pend3 = (H, do, ts2, slot)
        if pend3 is not None:
            emit_tr_adds(pend3)
            pend3 = None
        if "od" in dbg and H == 0:
            sch.op("dgp", lambda e: e.dma_start(
                out=dbg_out["od"], in_=out_st[:, 0, 0:128]),
                reads=[b_outst])
        for tb in (() if "nostores" in stages else range(8)):
            sch.op("dgp", lambda e, hb=hb, tb=tb: e.dma_start(
                out=OUT[hb + tb * 128:hb + tb * 128 + 128, :],
                in_=out_st[:, tb, :]),
                reads=[b_outst])

    sch.finalize_incs()
    sems = {sn: nc.alloc_semaphore(name="sem_" + sn) for sn in sch.cnt}
    with nc.Block() as block:
        def emit(eng, name):
            for fn, waits, sem, inc, tag in sch.ops[name]:
                for ws, wv in waits.items():
                    eng.wait_ge(sems[ws], wv)
                inst = fn(eng)
                if inc:
                    inst.then_inc(sems[sem], inc)
            if name == "gpsimd":
                for sn, v in sch.cnt.items():
                    if v:
                        eng.wait_ge(sems[sn], v)

        block.sync(lambda e: emit(e, "sync"))
        block.scalar(lambda e: emit(e, "scalar"))
        block.vector(lambda e: emit(e, "vector"))
        block.tensor(lambda e: emit(e, "tensor"))
        block.gpsimd(lambda e: emit(e, "gpsimd"))
    return nc


def _threefry2x32(k1, k2, x0, x1):
    u32 = np.uint32
    rot = ((13, 15, 26, 6), (17, 29, 16, 24))

    def rl(x, d):
        return (x << u32(d)) | (x >> u32(32 - d))

    ks = [k1, k2, k1 ^ k2 ^ u32(0x1BD11BDA)]
    x = [x0 + ks[0], x1 + ks[1]]
    order = [(0, 1, 2, 1), (1, 2, 0, 2), (0, 0, 1, 3), (1, 1, 2, 4), (0, 2, 0, 5)]
    with np.errstate(over="ignore"):
        for ri, (rots, a, b, i) in enumerate(order):
            for r in rot[rots]:
                x[0] = x[0] + x[1]
                x[1] = rl(x[1], r)
                x[1] = x[0] ^ x[1]
            x[0] = x[0] + ks[a]
            x[1] = x[1] + ks[b] + u32(i)
    return x


def _jax_bernoulli_key42(p, shape):
    n = int(np.prod(shape))
    counts = np.arange(n, dtype=np.uint64)
    hi = (counts >> np.uint64(32)).astype(np.uint32)
    lo = counts.astype(np.uint32)
    with np.errstate(over="ignore"):
        b1, b2 = _threefry2x32(np.uint32(0), np.uint32(42), hi, lo)
        bits = b1 ^ b2
    fb = (bits >> np.uint32(9)) | np.uint32(0x3F800000)
    u = fb.view(np.float32) - np.float32(1.0)
    return (u < np.float32(p)).reshape(shape)


_SLOPE_BITS = [0x3d800000, 0x3d97de02, 0x3db42f4e, 0x3dd5c855, 0x3dfda51c,
               0x3e16785c, 0x3e3286f6, 0x3e53d0db, 0x3e7b4fc4, 0x3e9515fe,
               0x3eb0e286, 0x3ed1de05, 0x3ef8ffed, 0x3f13b6e5, 0x3f2f41f5,
               0x3f4fefc6]


def _host_constants():
    keep = _jax_bernoulli_key42(0.8, (4, S, D))
    slopes = np.array(_SLOPE_BITS, dtype=np.uint32).view(np.float32)
    pos = np.arange(S, dtype=np.float32)
    bias = -(slopes.astype(np.float32)[:, None] * pos[None, :])
    return keep, bias


def kernel(L0_attn_x, L0_attn_Wq, L0_attn_bq, L0_attn_Wk, L0_attn_bk,
           L0_attn_Wv, L0_attn_bv, L0_attn_Wo, L0_attn_bo,
           L0_ffn_w1, L0_ffn_b1, L0_ffn_w2, L0_ffn_b2):
    x = np.ascontiguousarray(np.asarray(L0_attn_x, dtype=np.float32))
    wq = np.asarray(L0_attn_Wq, dtype=np.float32)
    wk = np.asarray(L0_attn_Wk, dtype=np.float32)
    wv = np.asarray(L0_attn_Wv, dtype=np.float32)
    wo = np.asarray(L0_attn_Wo, dtype=np.float32)
    w1 = np.asarray(L0_ffn_w1, dtype=np.float32)
    w2 = np.asarray(L0_ffn_w2, dtype=np.float32)

    def pc(b, n):
        return np.ascontiguousarray(
            np.asarray(b, dtype=np.float32).reshape(n, 128).T)

    keep, bias = _host_constants()
    eye = np.eye(128, dtype=np.float32)
    shared = {
        "wqT": np.ascontiguousarray(wq.T), "wkT": np.ascontiguousarray(wk.T),
        "wvT": np.ascontiguousarray(wv.T), "woT": np.ascontiguousarray(wo.T),
        "w1T": np.ascontiguousarray(w1.T), "w2T": np.ascontiguousarray(w2.T),
        "bq": pc(L0_attn_bq, 8), "bk": pc(L0_attn_bk, 8), "bo": pc(L0_attn_bo, 8),
        "b1": pc(L0_ffn_b1, 32), "b2": pc(L0_ffn_b2, 8),
        "bv": np.ascontiguousarray(
            np.asarray(L0_attn_bv, dtype=np.float32).reshape(1, D)),
        "eye": eye,
    }

    nc = build_program()
    in_maps = []
    for c in range(8):
        b, h = c // 2, c % 2
        off = h * TOWN
        perm = np.concatenate([np.arange(off, off + TOWN),
                               np.arange(0, off), np.arange(off + TOWN, S)])
        x_rot = np.ascontiguousarray(x[b][perm])
        a8 = (8.0 * bias[:, perm]).astype(np.float32)
        hi = a8.astype(BF16NP)
        lo = (a8 - hi.astype(np.float32)).astype(BF16NP)
        m = keep[b, off:off + TOWN]
        mm = m.reshape(2, 2, 512, 8, 128)
        maskv = (1.25 * mm.astype(np.float32)).transpose(4, 0, 3, 1, 2)
        maskv = np.ascontiguousarray(maskv.reshape(128, 16384)).astype(np.float16)
        in_maps.append(dict(shared, x=x_rot, alhi=np.ascontiguousarray(hi),
                            allo=np.ascontiguousarray(lo), maskv=maskv))

    res = run_bass_kernel_spmd(nc, in_maps, core_ids=list(range(8)))
    out = np.empty((4, S, D), dtype=np.float32)
    for c in range(8):
        b, h = c // 2, c % 2
        out[b, h * TOWN:(h + 1) * TOWN] = res.results[c]["out"]
    return out
